# version 19
# baseline (speedup 1.0000x reference)
import contextlib
import numpy as np
import ml_dtypes
import concourse.bass as bass
import concourse.mybir as mybir
from concourse.bass_utils import run_bass_kernel_spmd

F32 = mybir.dt.float32
F32R = mybir.dt.float32r
BF16 = mybir.dt.bfloat16
I16 = mybir.dt.int16
ALU = mybir.AluOpType
AF = mybir.ActivationFunctionType
AX = mybir.AxisListType

NCORES = 8
EPS = 1e-5


def build_program(NB=4, L=2048, D=1024, H=4096, TA=256,
                  phases=("0", "A", "B1", "B2"), acap=None, dbgqd=False):
    DT = D // 128
    HT = H // 128
    R = NB * L
    NBLK = R // TA
    TPB = TA // 128
    NT = R // 128
    SEGB = L // TA
    HS = H // 512
    FH = D // 512

    nc = bass.Bass("TRN2", target_bir_lowering=False)

    x_d = nc.dram_tensor("x", [R, D], F32, kind="ExternalInput").ap()
    win_d = nc.dram_tensor("w_in", [D, D], F32, kind="ExternalInput").ap()
    wout_d = nc.dram_tensor("w_out", [D, D], F32, kind="ExternalInput").ap()
    bin_d = nc.dram_tensor("b_in", [1, D], F32, kind="ExternalInput").ap()
    bout_d = nc.dram_tensor("b_out", [1, D], F32, kind="ExternalInput").ap()
    gam_d = nc.dram_tensor("gamma", [1, D], F32, kind="ExternalInput").ap()
    laml_d = nc.dram_tensor("lamlog", [1, D], F32, kind="ExternalInput").ap()
    w1_d = nc.dram_tensor("w1", [H, D], F32, kind="ExternalInput").ap()
    w2_d = nc.dram_tensor("w2", [D, H], F32, kind="ExternalInput").ap()
    id_d = nc.dram_tensor("ident", [128, 128], F32, kind="ExternalInput").ap()
    y_d = nc.dram_tensor("y", [R, D], F32, kind="ExternalOutput").ap()
    r2_d = nc.dram_tensor("r2scr", [R, D], F32, kind="Internal").ap()
    qh_d = nc.dram_tensor("qhscr", [R, H], BF16, kind="Internal").ap()
    w1t_d = nc.dram_tensor("w1tscr", [128, DT * H], BF16, kind="Internal").ap()
    w2t_d = nc.dram_tensor("w2tscr", [128, HT * D], BF16, kind="Internal").ap()
    qd_d = nc.dram_tensor("qdscr", [R, D], BF16, kind="Internal").ap() if dbgqd else None
    scrq_d = nc.dram_tensor("scrq", [2, 128], F32, kind="Internal").ap()
    scrt_d = nc.dram_tensor("scrt", [2, 4], F32, kind="Internal").ap()

    es = contextlib.ExitStack()
    with es:
        def sb(name, shape, dt):
            return es.enter_context(nc.sbuf_tensor(name, shape, dt))

        zerov = sb("zerov", [128, 1], F32)
        halfv = sb("halfv", [128, 1], F32)
        epsv = sb("epsv", [128, 1], F32)
        ones_col = sb("ones_col", [128, 1], F32)
        ones_row = sb("ones_row", [1, 128], F32)
        thrv = [sb(f"thrv{w}", [128, 1], F32) for w in range(2)]
        nthrv = [sb(f"nthrv{w}", [128, 1], F32) for w in range(2)]
        sflv = [sb(f"sflv{w}", [128, 1], F32) for w in range(2)]
        rstd3s = sb("rstd3s", [128, NT], F32)
        m3s = sb("m3s", [128, NT], F32)
        am3s = sb("am3s", [128, NT], F32)
        sch_all = sb("sch_all", [128, NT], F32)
        a3s = sb("a3s", [128, NT], F32)
        b3s = sb("b3s", [128, NT], F32)
        g3s = sb("g3s", [128, NT], F32)
        g4s = sb("g4s", [128, NT], F32)
        p0A = es.enter_context(contextlib.ExitStack())
        def sb0A(name, shape, dt):
            return p0A.enter_context(nc.sbuf_tensor(name, shape, dt))
        ones_rr = sb0A("ones_rr", [1, 128], F32R)
        identt = sb0A("identt", [128, 128], F32)
        vecs3 = sb0A("vecs3", [128, 3 * DT], F32)
        lamx = sb0A("lamx", [128, DT], F32)
        gbv = sb0A("gbv", [128, DT], F32)
        bout_r = sb0A("bout_r", [1, D], F32R)
        winT = sb0A("winT", [128, DT * D], F32R)
        woutT = sb0A("woutT", [128, DT * D], F32R)

        sems = {}
        def S(n):
            sems[n] = [es.enter_context(nc.semaphore(n)), 0]
        for n in ["s0_row", "s0_wld0", "s0_wld1", "s0_vt", "s0_ve", "s0_lam", "s0_mis",
                  "s0_id", "s0_wtp", "s0_wte",
                  "s0_al0", "s0_al1", "s0_red", "s0_sum", "s0_ssc", "s0_bcm",
                  "s0_thr", "s0_bl0", "s0_bl1", "s0_tern", "s0_tp0", "s0_tp1", "s0_xb", "s0_sgn",
                  "s0_wst",
                  "sxA0", "sxA1", "s_st0", "s_rs0", "s_b0", "s_ln1", "s_t2",
                  "s_lt", "s_lte", "s_u",
                  "s_gu", "s_y", "s_a2", "s_attn", "s_st2", "s_rs2",
                  "s_r2v", "s_r2d0", "s_r2d1", "s_rs3", "s_st3",
                  "s_w1T", "s_r2B0", "s_r2B1", "s_pre", "s_y0d", "s_r16d",
                  "s_qd", "s_qdT0", "s_qdT1", "s_h3", "s_gel", "s_amx",
                  "s_r16h", "s_qh", "s_qhd0", "s_qhd1",
                  "s_w2T", "s_r2C0", "s_r2C1", "s_pre2", "s_qhT0", "s_qhT1",
                  "s_f4", "s_ffn", "s_st4", "s_rs4", "s_outv",
                  "s_outd0", "s_outd1", "s_dbg0", "s_dbg1"]:
            S(n)

        def sem(n):
            return sems[n][0]

        _CHAIN_OPS = {
            "tensor_scalar", "tensor_tensor", "tensor_copy", "tensor_reduce",
            "reduce_max", "reduce_sum", "bn_stats", "bn_aggr", "reciprocal",
            "tensor_tensor_scan", "scalar_tensor_tensor", "memset",
            "activation", "mul", "add", "copy", "sign", "sqrt", "square",
            "iota", "select",
        }
        _DMA_OPS = {"dma_start", "dma_start_transpose", "indirect_dma_start"}

        class _Chain:
            def __init__(self, name):
                self.h = es.enter_context(nc.semaphore(name))
                self.cnt = 0
                self.pending = None
                self.last = None

            def flush(self):
                if self.pending is not None:
                    self.pending.then_inc(self.h, 1)
                    self.cnt += 1
                    self.last = (self.h, self.cnt)
                    self.pending = None

            def pre(self, eng):
                self.flush()
                if self.last is not None:
                    eng.wait_ge(*self.last)

        class _Proxy:
            def __init__(self, chain, eng):
                self._c = chain
                self._e = eng

            def __getattr__(self, name):
                c, e = self._c, self._e
                if name in _CHAIN_OPS:
                    f = getattr(e, name)
                    def w(*a, **k):
                        c.pre(e)
                        ins = f(*a, **k)
                        c.pending = ins
                        return ins
                    return w
                if name in _DMA_OPS:
                    f = getattr(e, name)
                    def wd(*a, **k):
                        c.pre(e)
                        return f(*a, **k)
                    return wd
                return getattr(e, name)

        chains = {n: _Chain("ch_" + n) for n in ("v", "a", "g")}

        def inc(n, instr, k=1):
            instr.then_inc(sems[n][0], k)
            sems[n][1] += k
            for c in chains.values():
                if c.pending is instr:
                    c.pending = None
                    c.last = (sems[n][0], sems[n][1])

        with contextlib.ExitStack() as p0:
            def sb0(name, shape, dt):
                return p0.enter_context(nc.sbuf_tensor(name, shape, dt))
            rowbuf = sb0("rowbuf", [1, 3 * D], F32)
            wstg = [sb0(f"wstg{i}", [128, 4096], F32) for i in range(2)]
            ps_v = p0.enter_context(nc.psum_tensor("ps_v", [128, 3 * DT], F32))
            ps_t = [p0.enter_context(nc.psum_tensor(f"ps_t{i}", [128, 512], F32))
                    for i in range(2)]
            blk = p0.enter_context(nc.Block())

            def awchunk_ap(i):
                w, c = divmod(i, 2)
                src = win_d if w == 0 else wout_d
                return src[512 * c:512 * (c + 1), :].rearrange(
                    "(j p) d -> p j d", p=128)

            @blk.gpsimd
            def _(g):
                g = _Proxy(chains["g"], g)
                g.memset(zerov[:], 0.0)
                g.memset(halfv[:], 385.5)
                g.memset(epsv[:], EPS)
                g.memset(ones_col[:], 1.0)
                g.memset(ones_row[:], 1.0)
                inc("s0_row", g.dma_start(rowbuf[:, 0:D], bout_d), 16)
                g.wait_ge(sem("s0_ve"), 1)
                inc("s0_row", g.dma_start(rowbuf[:, 0:D], laml_d), 16)
                inc("s0_row", g.dma_start(rowbuf[:, D:2 * D], gam_d), 16)
                inc("s0_row", g.dma_start(rowbuf[:, 2 * D:3 * D], bin_d), 16)
                inc("s0_id", g.dma_start(identt[:], id_d), 16)
                for i in range(4):
                    if i >= 2:
                        g.wait_ge(sem("s0_wte"), 32 * (i - 1))
                    inc(f"s0_wld{i % 2}", g.dma_start(
                        wstg[i % 2][:].rearrange("p (j d) -> p j d", j=4),
                        awchunk_ap(i)), 16)

            @blk.tensor
            def _(t):
                t.wait_ge(sem("s0_row"), 64)
                mv = None
                for n in range(3 * DT):
                    mv = t.matmul(ps_v[:, n:n + 1], rowbuf[0:1, 128 * n:128 * (n + 1)],
                                  ones_col[0:1, 0:1], start=True, stop=True)
                inc("s0_vt", mv)
                t.wait_ge(sem("s0_id"), 16)
                ntp = 0
                for i in range(4):
                    t.wait_ge(sem(f"s0_wld{i % 2}"), 16 * (i // 2 + 1))
                    for j in range(4):
                        for dd in range(DT):
                            gidx = ntp
                            if gidx >= 8:
                                t.wait_ge(sem("s0_wte"), 4 * (gidx // 4 - 1))
                            mm = t.transpose(
                                ps_t[(gidx // 4) % 2][:, 128 * (gidx % 4):128 * (gidx % 4 + 1)],
                                wstg[i % 2][:, 1024 * j + 128 * dd:1024 * j + 128 * (dd + 1)],
                                identt[:])
                            if gidx % 4 == 3:
                                inc("s0_wtp", mm)
                            ntp += 1

            @blk.vector
            def _(v):
                v = _Proxy(chains["v"], v)
                v.wait_ge(sem("s0_row"), 16)
                v.tensor_copy(ones_rr[:], ones_row[:])
                ic = v.tensor_copy(bout_r[:], rowbuf[:, 0:D])
                inc("s0_ve", ic)
                v.wait_ge(sem("s0_vt"), 1)
                ic2 = v.tensor_copy(vecs3[:], ps_v[:])
                inc("s0_ve", ic2)
                wiv = winT[:].rearrange("p (i d) -> p i d", i=DT)
                wov = woutT[:].rearrange("p (i d) -> p i d", i=DT)
                ntp = 0
                for i in range(4):
                    w, c = divmod(i, 2)
                    for j in range(4):
                        etile = 4 * c + j
                        for dd in range(DT):
                            gidx = ntp
                            if gidx % 4 == 3:
                                grp = gidx // 4
                                v.wait_ge(sem("s0_wtp"), grp + 1)
                                dst = wiv if w == 0 else wov
                                iv_ = v.tensor_copy(
                                    dst[:, dd - 3:dd + 1, 128 * etile:128 * (etile + 1)],
                                    ps_t[grp % 2][:].rearrange("p (q r) -> p q r", q=4))
                                inc("s0_wte", iv_, 4)
                            ntp += 1
                v.wait_ge(sem("s0_lam"), 1)
                ig = v.tensor_tensor(gbv[:], vecs3[:, DT:2 * DT],
                                     vecs3[:, 2 * DT:3 * DT], op=ALU.mult)
                inc("s0_mis", ig)

            @blk.scalar
            def _(a):
                a = _Proxy(chains["a"], a)
                a.wait_ge(sem("s0_ve"), 2)
                ie = a.activation(lamx[:], vecs3[:, 0:DT], AF.Exp,
                                  bias=zerov[:], scale=1.0)
                inc("s0_lam", ie)

        if "A" not in phases:
            return nc
        NBLKA = NBLK if acap is None else min(NBLK, acap)
        GPB = TPB * DT // 4
        with contextlib.ExitStack() as pA:
            def sbA(name, shape, dt):
                return pA.enter_context(nc.sbuf_tensor(name, shape, dt))
            lam_bc = sbA("lam_bc", [128, DT * TA], F32)
            x_buf = [sbA(f"x_buf{i}", [128, TPB * D], F32) for i in range(2)]
            ln1f = [sbA(f"ln1f{i}", [128, TPB * D], F32) for i in range(2)]
            lnT = [sbA(f"lnT{i}", [128, DT * TA], F32R) for i in range(2)]
            gu_b = [sbA(f"gu{i}", [128, TA], F32) for i in range(2)]
            ybf = [sbA(f"ybf{i}", [128, DT * TA], F32R) for i in range(2)]
            attn_b = [sbA(f"attn_b{i}", [128, D], F32) for i in range(2)]
            t2buf = [sbA(f"t2buf{i}", [128, D], F32) for i in range(2)]
            r2b = [sbA(f"r2b{i}", [128, D], F32) for i in range(2)]
            rstd0s = [sbA(f"rstd0s{i}", [128, TPB], F32) for i in range(2)]
            bias0s = [sbA(f"bias0s{i}", [128, TPB], F32) for i in range(2)]
            rstd2c = [sbA(f"rstd2c{i}", [128, TPB], F32) for i in range(2)]
            bn6 = sbA("bn6", [128, 12], F32)
            bn2 = [sbA(f"bn2_{i}", [128, 2], F32) for i in range(2)]
            bn6a = sbA("bn6a", [128, 12], F32)
            bn2a = [sbA(f"bn2a_{i}", [128, 2], F32) for i in range(2)]
            bn6r = sbA("bn6r", [128, 12], F32)
            bn2r = [sbA(f"bn2r_{i}", [128, 2], F32) for i in range(2)]
            lnv = sbA("lnv", [128, 1], F32)
            nrstd = sbA("nrstd", [128, 1], F32)
            bias2 = sbA("bias2", [128, 1], F32)
            tmp1 = sbA("tmp1", [128, D], F32)
            qstg = [sbA(f"qstg{i}", [128, 2048], F32) for i in range(2)]
            t1b = sbA("t1b", [128, 2048], BF16)
            t2b = sbA("t2b", [128, 2048], BF16)
            wtp = [sbA(f"wtp{i}", [128, 2048], BF16) for i in range(2)]
            qracc = sbA("qracc", [128, 1], F32)
            qrtmp = sbA("qrtmp", [128, 1], F32)
            qrow = sbA("qrow", [1, 128], F32)
            qssc = sbA("qssc", [1, 1], F32)
            qth = sbA("qth", [1, 4], F32)
            ps_u = [pA.enter_context(nc.psum_tensor(f"ps_u{i}", [128, TA], F32))
                    for i in range(2)]
            ps_a = [pA.enter_context(nc.psum_tensor(f"ps_a{i}", [128, D], F32))
                    for i in range(2)]
            ps_t2 = [pA.enter_context(nc.psum_tensor(f"ps_t2{i}", [128, 512], F32))
                     for i in range(2)]
            blk = pA.enter_context(nc.Block())

            def w1src(c):
                return w1_d[256 * c:256 * (c + 1), :].rearrange(
                    "(j p) d -> p j d", p=128)

            def w2src(c):
                rb, hf = divmod(c, 2)
                return w2_d[128 * rb:128 * (rb + 1), 2048 * hf:2048 * (hf + 1)]

            def qload_src(q):
                if q < 16:
                    return w1src(q), True
                if q < 32:
                    return w2src(q - 16), False
                if q < 48:
                    return w1src(q - 32), True
                return w2src(q - 48), False

            def pieces_through(c2):
                return 2 * (c2 + 1) if c2 < 16 else 32 + (c2 - 15)

            QPIECES = ([(c2, j) for c2 in range(16) for j in range(2)]
                       + [(c2, None) for c2 in range(16, 32)])
            QN_EL = H * D

            def xap(b):
                return x_d[b * TA:(b + 1) * TA, :].rearrange("(t p) d -> p t d", p=128)

            xv = [x_buf[i][:].rearrange("p (t d) -> p t d", t=TPB) for i in range(2)]
            lTv = [lnT[i][:].rearrange("p (k t) -> p k t", k=DT) for i in range(2)]
            ybv = [ybf[i][:].rearrange("p (k t) -> p k t", k=DT) for i in range(2)]
            wiv = winT[:].rearrange("p (i d) -> p i d", i=DT)
            wov = woutT[:].rearrange("p (i d) -> p i d", i=DT)
            lbv = lam_bc[:].rearrange("p (k t) -> p k t", k=DT)

            @blk.gpsimd
            def _(g):
                g = _Proxy(chains["g"], g)
                for b in range(NBLKA):
                    if b >= 2:
                        g.wait_ge(sem("s_r2v"), TPB * (b - 1))
                    inc(f"sxA{b % 2}", g.dma_start(xv[b % 2], xap(b)), 16)
                    if b >= 1:
                        for t in range(TPB):
                            ta = TPB * (b - 1) + t
                            g.wait_ge(sem("s_r2v"), ta + 1)
                            inc(f"s_r2d{ta % 2}", g.dma_start(
                                r2_d[128 * ta:128 * (ta + 1), :], r2b[ta % 2][:]), 16)
                for t in range(TPB):
                    ta = TPB * (NBLKA - 1) + t
                    g.wait_ge(sem("s_r2v"), ta + 1)
                    inc(f"s_r2d{ta % 2}", g.dma_start(
                        r2_d[128 * ta:128 * (ta + 1), :], r2b[ta % 2][:]), 16)
                g.wait_ge(sem("s_r2d0"), 16 * (TPB * NBLKA // 2))
                g.wait_ge(sem("s_r2d1"), 16 * (TPB * NBLKA // 2))
                g.wait_ge(sem("sxA0"), 16 * ((NBLKA + 1) // 2))
                g.wait_ge(sem("sxA1"), 16 * (NBLKA // 2))

            @blk.vector
            def _(v):
                v = _Proxy(chains["v"], v)

                def q_dve_tasks():
                    def reduce_chunk(i):
                        v.wait_ge(sem(f"s0_al{i % 2}"), 16 * (i // 2 + 1))
                        v.tensor_reduce(qrtmp[:], qstg[i % 2][:], axis=AX.X,
                                        op=ALU.add, apply_absolute_value=True)
                        ia = v.tensor_tensor(qracc[:], qracc[:], qrtmp[:],
                                             op=ALU.add)
                        inc("s0_red", ia)

                    def scale_dve(w):
                        v.wait_ge(sem("s0_sum"), 32 * (w + 1))
                        if w == 1:
                            v.wait_ge(sem("s0_bcm"), 16)
                        v.tensor_reduce(qssc[:], qrow[:], axis=AX.X, op=ALU.add)
                        v.tensor_scalar(qssc[:], qssc[:], 1.0 / QN_EL, EPS,
                                        op0=ALU.mult, op1=ALU.max)
                        v.tensor_scalar(qth[:, 0:1], qssc[:], 0.5, None,
                                        op0=ALU.mult)
                        v.tensor_scalar(qth[:, 1:2], qth[:, 0:1], -1.0,
                                        None, op0=ALU.mult)
                        iq = v.tensor_copy(qth[:, 2:3], qssc[:])
                        inc("s0_ssc", iq)

                    yield lambda: v.memset(qracc[:], 0.0)
                    for i in range(16):
                        yield lambda i=i: reduce_chunk(i)

                    def switch():
                        v.wait_ge(sem("s0_sum"), 16)
                        v.memset(qracc[:], 0.0)
                    yield switch
                    yield lambda: scale_dve(0)
                    for i in range(16, 32):
                        yield lambda i=i: reduce_chunk(i)
                    for c2 in range(32):
                        if c2 == 13:
                            yield lambda: scale_dve(1)
                        def tern(c2=c2):
                            q = 32 + c2
                            w = 0 if c2 < 16 else 1
                            v.wait_ge(sem(f"s0_al{q % 2}"), 16 * (q // 2 + 1))
                            v.wait_ge(sem("s0_thr"), 48 * (w + 1))
                            if c2 >= 1:
                                v.wait_ge(sem("s0_xb"),
                                          16 * pieces_through(c2 - 1))
                            v.tensor_scalar(t1b[:], qstg[q % 2][:], thrv[w][:],
                                            None, op0=ALU.is_ge)
                            v.tensor_scalar(t2b[:], qstg[q % 2][:], nthrv[w][:],
                                            None, op0=ALU.is_le)
                            it = v.tensor_tensor(t1b[:], t1b[:], t2b[:],
                                                 op=ALU.subtract)
                            inc("s0_tern", it)
                        yield tern

                dveq = q_dve_tasks()

                def dve_qslot(n):
                    for _ in range(n):
                        t = next(dveq, None)
                        if t is None:
                            return
                        t()

                v.memset(lam_bc[:], 1.0)
                for k in range(DT):
                    v.tensor_scalar(lbv[:, k, :], lbv[:, k, :], lamx[:, k:k + 1],
                                    None, op0=ALU.mult)
                def scans_sec(b):
                    seg = b % SEGB
                    for k in range(DT):
                        gcnt = DT * b + k
                        v.wait_ge(sem("s_gu"), gcnt + 1)
                        if b >= 2:
                            v.wait_ge(sem("s_a2"), TPB * (b - 1))
                        init = 0.0 if seg == 0 else ybv[(b - 1) % 2][:, k, TA - 1:TA]
                        isc = v.tensor_tensor_scan(ybv[b % 2][:, k, :], lbv[:, k, :],
                                                   gu_b[gcnt % 2][:], init,
                                                   op0=ALU.mult, op1=ALU.add)
                        inc("s_y", isc)

                def attnp(b):
                    for t in range(TPB):
                        ta = TPB * b + t
                        v.wait_ge(sem("s_a2"), ta + 1)
                        if ta >= 2:
                            v.wait_ge(sem("s_t2"), ta - 1)
                        iat = v.tensor_tensor(attn_b[ta % 2][:],
                                              ln1f[b % 2][:, t * D:(t + 1) * D],
                                              ps_a[ta % 2][:], op=ALU.add)
                        inc("s_attn", iat)
                        v.bn_stats(bn6a[:, 0:6], attn_b[ta % 2][:, 0:512])
                        v.bn_stats(bn6a[:, 6:12], attn_b[ta % 2][:, 512:1024])
                        ia2 = v.bn_aggr(bn2a[ta % 2][:], bn6a[:])
                        inc("s_st2", ia2)

                def r2p(b):
                    for t in range(TPB):
                        ta = TPB * b + t
                        v.wait_ge(sem("s_t2"), ta + 1)
                        if ta >= 2:
                            v.wait_ge(sem(f"s_r2d{ta % 2}"), 16 * (ta // 2))
                        ir2 = v.tensor_tensor(r2b[ta % 2][:], t2buf[ta % 2][:],
                                              xv[b % 2][:, t, :], op=ALU.add)
                        inc("s_r2v", ir2)
                        v.bn_stats(bn6r[:, 0:6], r2b[ta % 2][:, 0:512])
                        v.bn_stats(bn6r[:, 6:12], r2b[ta % 2][:, 512:1024])
                        if ta >= 2:
                            v.wait_ge(sem("s_rs3"), ta - 1)
                        ia3 = v.bn_aggr(bn2r[ta % 2][:], bn6r[:])
                        inc("s_st3", ia3)
                        v.tensor_scalar(tmp1[:], r2b[ta % 2][:], bn2r[ta % 2][:, 0:1],
                                        None, op0=ALU.subtract)
                        v.tensor_reduce(am3s[:, ta:ta + 1], tmp1[:], axis=AX.X,
                                        op=ALU.max, apply_absolute_value=True)

                for b in range(NBLKA):
                    seg = b % SEGB
                    for t in range(TPB):
                        ta = TPB * b + t
                        v.wait_ge(sem(f"sxA{b % 2}"), 16 * (b // 2 + 1))
                        if ta >= 2:
                            v.wait_ge(sem("s_ln1"), ta - 1)
                        v.bn_stats(bn6[:, 0:6], x_buf[b % 2][:, t * D:t * D + 512])
                        v.bn_stats(bn6[:, 6:12], x_buf[b % 2][:, t * D + 512:t * D + 1024])
                        ia0 = v.bn_aggr(bn2[ta % 2][:], bn6[:])
                        inc("s_st0", ia0)
                    if b >= 1:
                        attnp(b - 1)
                    for gi in range(GPB):
                        grp = GPB * b + gi
                        tt = gi // (DT // 4)
                        q = gi % (DT // 4)
                        v.wait_ge(sem("s_lt"), grp + 1)
                        if b >= 2:
                            v.wait_ge(sem("s_u"), DT * (b - 1))
                        ie = v.tensor_copy(
                            lTv[b % 2][:, 4 * q:4 * q + 4, 128 * tt:128 * (tt + 1)],
                            ps_t2[grp % 2][:].rearrange("p (q r) -> p q r", q=4))
                        inc("s_lte", ie)
                    if b >= 1:
                        r2p(b - 1)
                    scans_sec(b)
                    dve_qslot(3)
                attnp(NBLKA - 1)
                r2p(NBLKA - 1)
                dve_qslot(100)

            @blk.scalar
            def _(a):
                a = _Proxy(chains["a"], a)

                def ln2p(b):
                    for t in range(TPB):
                        ta = TPB * b + t
                        a.wait_ge(sem("s_st2"), ta + 1)
                        a.activation(lnv[:], bn2a[ta % 2][:, 1:2], AF.Ln,
                                     bias=epsv[:], scale=1.0)
                        a.activation(rstd2c[b % 2][:, t:t + 1], lnv[:], AF.Exp,
                                     bias=zerov[:], scale=-0.5)
                        a.activation(nrstd[:], rstd2c[b % 2][:, t:t + 1], AF.Identity,
                                     bias=zerov[:], scale=-1.0)
                        a.activation(bias2[:], bn2a[ta % 2][:, 0:1], AF.Identity,
                                     bias=zerov[:], scale=nrstd[:])
                        if ta >= 2:
                            a.wait_ge(sem("s_r2v"), ta - 1)
                        it2 = a.activation(t2buf[ta % 2][:], attn_b[ta % 2][:],
                                           AF.Identity, bias=bias2[:],
                                           scale=rstd2c[b % 2][:, t:t + 1])
                        inc("s_t2", it2)

                def ln3p(b):
                    for t in range(TPB):
                        ta = TPB * b + t
                        a.wait_ge(sem("s_st3"), ta + 1)
                        a.activation(lnv[:], bn2r[ta % 2][:, 1:2], AF.Ln,
                                     bias=epsv[:], scale=1.0)
                        a.activation(rstd3s[:, ta:ta + 1], lnv[:], AF.Exp,
                                     bias=zerov[:], scale=-0.5)
                        im3 = a.activation(m3s[:, ta:ta + 1], bn2r[ta % 2][:, 0:1],
                                           AF.Identity, bias=zerov[:], scale=1.0)
                        inc("s_rs3", im3)

                for b in range(NBLKA):
                    for t in range(TPB):
                        ta = TPB * b + t
                        a.wait_ge(sem("s_st0"), ta + 1)
                        a.activation(lnv[:], bn2[ta % 2][:, 1:2], AF.Ln,
                                     bias=epsv[:], scale=1.0)
                        a.activation(rstd0s[b % 2][:, t:t + 1], lnv[:], AF.Exp,
                                     bias=zerov[:], scale=-0.5)
                        a.activation(nrstd[:], rstd0s[b % 2][:, t:t + 1], AF.Identity,
                                     bias=zerov[:], scale=-1.0)
                        a.activation(bias0s[b % 2][:, t:t + 1], bn2[ta % 2][:, 0:1],
                                     AF.Identity, bias=zerov[:], scale=nrstd[:])
                        if b >= 2:
                            a.wait_ge(sem("s_lt"), GPB * (b - 1))
                            a.wait_ge(sem("s_attn"), TPB * (b - 1))
                        i1 = a.activation(ln1f[b % 2][:, t * D:(t + 1) * D],
                                          xv[b % 2][:, t, :], AF.Identity,
                                          bias=bias0s[b % 2][:, t:t + 1],
                                          scale=rstd0s[b % 2][:, t:t + 1])
                        inc("s_ln1", i1)
                    if b >= 1:
                        ln2p(b - 1)
                    for k in range(DT):
                        gcnt = DT * b + k
                        a.wait_ge(sem("s_u"), gcnt + 1)
                        if gcnt >= 2:
                            a.wait_ge(sem("s_y"), gcnt - 1)
                        ig = a.activation(gu_b[gcnt % 2][:], ps_u[gcnt % 2][:],
                                          AF.Identity, bias=gbv[:, k:k + 1],
                                          scale=vecs3[:, DT + k:DT + k + 1])
                        inc("s_gu", ig)
                    if b >= 1:
                        ln3p(b - 1)
                ln2p(NBLKA - 1)
                ln3p(NBLKA - 1)

            @blk.sync
            def _(s):
                w1tv = w1t_d.rearrange("p (i h) -> p i h", i=DT)
                w2tv = w2t_d.rearrange("p (i f) -> p i f", i=HT)
                pctr = [0]

                def emit_pieces(c2):
                    npiece = 2 if c2 < 16 else 1
                    for j in (range(2) if c2 < 16 else [None]):
                        p = pctr[0]
                        pctr[0] += 1
                        s.wait_ge(sem("s0_tern"), c2 + 1)
                        if p >= 2:
                            s.wait_ge(sem(f"s0_bl{p % 2}"), 16 * (p // 2))
                        if j is None:
                            wv = wtp[p % 2][:].rearrange(
                                "p (i r) -> p i r", i=16)
                            xin = t1b[:]
                        else:
                            wv = wtp[p % 2][:, 0:1024].rearrange(
                                "p (i r) -> p i r", i=DT)
                            xin = t1b[:, 1024 * j:1024 * (j + 1)]
                        inc("s0_xb", s.dma_start_transpose(wv, xin), 16)
                        s.wait_ge(sem("s0_xb"), 16 * (p + 1))
                        if j is None:
                            c = c2 - 16
                            rb, hf = divmod(c, 2)
                            dstp = w2tv[:, 16 * hf:16 * (hf + 1),
                                        128 * rb:128 * (rb + 1)]
                        else:
                            hrow = 2 * c2 + j
                            dstp = w1tv[:, :, 128 * hrow:128 * (hrow + 1)]
                        inc(f"s0_bl{p % 2}", s.dma_start(dstp, wv), 16)

                for q in range(64):
                    if q == 18:
                        s.wait_ge(sem("s0_red"), 16)
                        inc("s0_sum", s.dma_start(scrq_d[0:1, :], qracc[:]), 16)
                        s.wait_ge(sem("s0_sum"), 16)
                        inc("s0_sum", s.dma_start(qrow[:], scrq_d[0:1, :]), 16)
                    if q == 20:
                        s.wait_ge(sem("s0_ssc"), 1)
                        inc("s0_bcm", s.dma_start(scrt_d[0:1, 0:3], qth[:, 0:3]), 16)
                        s.wait_ge(sem("s0_bcm"), 16)
                        inc("s0_thr", s.dma_start(
                            thrv[0][:],
                            scrt_d[0:1, 0:1].broadcast_to((128, 1))), 16)
                        inc("s0_thr", s.dma_start(
                            nthrv[0][:],
                            scrt_d[0:1, 1:2].broadcast_to((128, 1))), 16)
                        inc("s0_thr", s.dma_start(
                            sflv[0][:],
                            scrt_d[0:1, 2:3].broadcast_to((128, 1))), 16)
                    if q == 44:
                        s.wait_ge(sem("s0_red"), 32)
                        inc("s0_sum", s.dma_start(scrq_d[1:2, :], qracc[:]), 16)
                        s.wait_ge(sem("s0_sum"), 48)
                        inc("s0_sum", s.dma_start(qrow[:], scrq_d[1:2, :]), 16)
                    if q == 46:
                        s.wait_ge(sem("s0_ssc"), 2)
                        inc("s0_bcm", s.dma_start(scrt_d[1:2, 0:3], qth[:, 0:3]), 16)
                        s.wait_ge(sem("s0_bcm"), 32)
                        inc("s0_thr", s.dma_start(
                            thrv[1][:],
                            scrt_d[1:2, 0:1].broadcast_to((128, 1))), 16)
                        inc("s0_thr", s.dma_start(
                            nthrv[1][:],
                            scrt_d[1:2, 1:2].broadcast_to((128, 1))), 16)
                        inc("s0_thr", s.dma_start(
                            sflv[1][:],
                            scrt_d[1:2, 2:3].broadcast_to((128, 1))), 16)
                    if q >= 2:
                        if q - 2 < 32:
                            s.wait_ge(sem("s0_red"), q - 1)
                        else:
                            s.wait_ge(sem("s0_tern"), q - 33)
                    src, isw1 = qload_src(q)
                    dst = (qstg[q % 2][:].rearrange("p (j d) -> p j d", j=2)
                           if isw1 else qstg[q % 2][:])
                    inc(f"s0_al{q % 2}", s.dma_start(dst, src), 16)
                    if q >= 34:
                        emit_pieces(q - 34)
                emit_pieces(30)
                emit_pieces(31)
                s.wait_ge(sem("s0_bl0"), 16 * 24)
                s.wait_ge(sem("s0_bl1"), 16 * 24)
                s.wait_ge(sem("s0_al0"), 16 * 32)
                s.wait_ge(sem("s0_al1"), 16 * 32)

            @blk.tensor
            def _(t):
                def mm2sec(b):
                    t.wait_ge(sem("s_y"), DT * (b + 1))
                    for tt in range(TPB):
                        ta = TPB * b + tt
                        if ta >= 2:
                            t.wait_ge(sem("s_attn"), ta - 1)
                        last = None
                        for half in range(FH):
                            pslice = ps_a[ta % 2][:, 512 * half:512 * (half + 1)]
                            t.matmul(pslice, ones_rr[:],
                                     bout_r[0:1, 512 * half:512 * (half + 1)],
                                     start=True, stop=False)
                            for k in range(DT):
                                last = t.matmul(
                                    pslice,
                                    ybv[b % 2][:, k, 128 * tt:128 * (tt + 1)],
                                    wov[:, k, 512 * half:512 * (half + 1)],
                                    start=False, stop=(k == DT - 1))
                        inc("s_a2", last)

                for b in range(NBLKA):
                    if b >= 1:
                        mm2sec(b - 1)
                    for gi in range(GPB):
                        grp = GPB * b + gi
                        tt = gi // (DT // 4)
                        q = gi % (DT // 4)
                        ta = TPB * b + tt
                        t.wait_ge(sem("s_ln1"), ta + 1)
                        if grp >= 2:
                            t.wait_ge(sem("s_lte"), grp - 1)
                        mm = None
                        for r in range(4):
                            dd = 4 * q + r
                            mm = t.transpose(
                                ps_t2[grp % 2][:, 128 * r:128 * (r + 1)],
                                ln1f[b % 2][:, tt * D + 128 * dd:tt * D + 128 * (dd + 1)],
                                identt[:])
                        inc("s_lt", mm)
                    t.wait_ge(sem("s_lte"), GPB * (b + 1))
                    for k in range(DT):
                        gcnt = DT * b + k
                        if gcnt >= 2:
                            t.wait_ge(sem("s_gu"), gcnt - 1)
                        last = None
                        for j in range(DT):
                            last = t.matmul(ps_u[gcnt % 2][:],
                                            wiv[:, j, 128 * k:128 * (k + 1)],
                                            lTv[b % 2][:, j, :],
                                            start=(j == 0), stop=(j == DT - 1))
                        inc("s_u", last)
                mm2sec(NBLKA - 1)

        p0A.close()

        if "B1" not in phases:
            return nc
        with contextlib.ExitStack() as p1:
            def sb1(name, shape, dt):
                return p1.enter_context(nc.sbuf_tensor(name, shape, dt))
            w1T = sb1("w1T", [128, DT * H], BF16)
            r2B = [sb1(f"r2B{i}", [128, D], F32) for i in range(2)]
            y0d = [sb1(f"y0d{i}", [128, D], F32) for i in range(2)]
            r16d = [sb1(f"r16d{i}", [128, D], I16) for i in range(2)]
            bd = sb1("bd", [128, D], BF16)
            cd = sb1("cd", [128, D], BF16)
            qd = [sb1(f"qd{i}", [128, D], BF16) for i in range(2)]
            qdT = [sb1(f"qdT{i}", [128, DT * 128], BF16) for i in range(2)]
            h_b = sb1("h_b", [128, H], F32)
            y0h = [sb1(f"y0h{i}", [128, H], F32) for i in range(2)]
            r16h = [sb1(f"r16h{i}", [128, H], I16) for i in range(2)]
            bh = sb1("bh", [128, H], BF16)
            ch = sb1("ch", [128, H], BF16)
            qh_b = [sb1(f"qh_b{i}", [128, H], BF16) for i in range(2)]
            amh = sb1("amh", [128, 1], F32)
            irc = sb1("irc", [128, 1], F32)
            tnt = sb1("tnt", [128, NT], F32)
            ps_h = [p1.enter_context(nc.psum_tensor(f"ps_h{i}", [128, 512], F32))
                    for i in range(6)]
            blk = p1.enter_context(nc.Block())
            w1v = w1T[:].rearrange("p (i h) -> p i h", i=DT)
            qdTv = [qdT[i][:].rearrange("p (k t) -> p k t", k=DT) for i in range(2)]
            NT2 = NT

            @blk.gpsimd
            def _(g):
                g = _Proxy(chains["g"], g)
                inc("s_w1T", g.dma_start(w1T[:], w1t_d), 16)
                for ta in range(min(2, NT2)):
                    inc(f"s_r2B{ta % 2}", g.dma_start(
                        r2B[ta % 2][:], r2_d[128 * ta:128 * (ta + 1), :]), 16)
                for ta in range(NT2):
                    if ta + 2 < NT2:
                        g.wait_ge(sem("s_y0d"), ta + 1)
                        inc(f"s_r2B{ta % 2}", g.dma_start(
                            r2B[ta % 2][:], r2_d[128 * (ta + 2):128 * (ta + 3), :]),
                            16)
                    if dbgqd:
                        g.wait_ge(sem("s_qd"), ta + 1)
                        inc(f"s_dbg{ta % 2}", g.dma_start(
                            qd_d[128 * ta:128 * (ta + 1), :], qd[ta % 2][:]), 16)
                    g.wait_ge(sem("s_qh"), ta + 1)
                    inc(f"s_qhd{ta % 2}", g.dma_start(
                        qh_d[128 * ta:128 * (ta + 1), :], qh_b[ta % 2][:]), 16)
                g.wait_ge(sem("s_qhd0"), 16 * ((NT2 + 1) // 2))
                g.wait_ge(sem("s_qhd1"), 16 * (NT2 // 2))
                g.wait_ge(sem("s_r2B0"), 16 * ((NT2 + 1) // 2))
                g.wait_ge(sem("s_r2B1"), 16 * (NT2 // 2))
                if dbgqd:
                    g.wait_ge(sem("s_dbg0"), 16 * ((NT2 + 1) // 2))
                    g.wait_ge(sem("s_dbg1"), 16 * (NT2 // 2))

            @blk.vector
            def _(v):
                v = _Proxy(chains["v"], v)
                v.tensor_tensor(tnt[:], rstd3s[:], am3s[:], op=ALU.mult)
                v.tensor_scalar(tnt[:], tnt[:], EPS, None, op0=ALU.max)
                v.reciprocal(g3s[:], tnt[:])
                v.tensor_scalar(tnt[:], g3s[:], 127.0, None, op0=ALU.mult)
                v.tensor_tensor(a3s[:], rstd3s[:], tnt[:], op=ALU.mult)
                v.tensor_tensor(b3s[:], m3s[:], a3s[:], op=ALU.mult)
                v.tensor_scalar(b3s[:], b3s[:], -1.0, 385.5, op0=ALU.mult, op1=ALU.add)
                v.reciprocal(g3s[:], tnt[:])
                i8 = v.tensor_scalar(g3s[:], g3s[:], sflv[0][:], None, op0=ALU.mult)
                inc("s_pre", i8)
                def dcorr(ta):
                    v.wait_ge(sem("s_r16d"), ta + 1)
                    v.tensor_tensor(bd[:], y0d[ta % 2][:], r16d[ta % 2][:],
                                    op=ALU.is_lt)
                    v.scalar_tensor_tensor(cd[:], y0d[ta % 2][:], 1.0,
                                           r16d[ta % 2][:],
                                           op0=ALU.subtract, op1=ALU.is_ge)
                    v.tensor_tensor(bd[:], cd[:], bd[:], op=ALU.subtract)
                    if ta >= 2:
                        v.wait_ge(sem(f"s_qdT{ta % 2}"), 16 * (ta // 2))
                        if dbgqd:
                            v.wait_ge(sem(f"s_dbg{ta % 2}"), 16 * (ta // 2))
                    iq = v.scalar_tensor_tensor(qd[ta % 2][:], r16d[ta % 2][:],
                                                385.0, bd[:],
                                                op0=ALU.subtract, op1=ALU.add)
                    inc("s_qd", iq)
                dcorr(0)
                for ta in range(NT2):
                    v.wait_ge(sem("s_gel"), HS * ta + HS)
                    v.tensor_reduce(amh[:], h_b[:], axis=AX.X,
                                    op=ALU.max, apply_absolute_value=True)
                    v.tensor_scalar(amh[:], amh[:], EPS, None, op0=ALU.max)
                    v.reciprocal(irc[:], amh[:])
                    i3 = v.tensor_scalar(sch_all[:, ta:ta + 1], irc[:], 127.0, None,
                                         op0=ALU.mult)
                    inc("s_amx", i3)
                    if ta + 1 < NT2:
                        dcorr(ta + 1)
                    v.wait_ge(sem("s_r16h"), ta + 1)
                    v.tensor_tensor(bh[:], y0h[ta % 2][:], r16h[ta % 2][:],
                                    op=ALU.is_lt)
                    v.scalar_tensor_tensor(ch[:], y0h[ta % 2][:], 1.0,
                                           r16h[ta % 2][:],
                                           op0=ALU.subtract, op1=ALU.is_ge)
                    v.tensor_tensor(bh[:], ch[:], bh[:], op=ALU.subtract)
                    if ta >= 2:
                        v.wait_ge(sem(f"s_qhd{ta % 2}"), 16 * (ta // 2))
                    iqh = v.scalar_tensor_tensor(qh_b[ta % 2][:], r16h[ta % 2][:],
                                                 385.0, bh[:],
                                                 op0=ALU.subtract, op1=ALU.add)
                    inc("s_qh", iqh)

            @blk.scalar
            def _(a):
                a = _Proxy(chains["a"], a)
                a.wait_ge(sem("s_pre"), 1)
                for it in range(NT2 + 1):
                    if it < NT2:
                        ta = it
                        a.wait_ge(sem(f"s_r2B{ta % 2}"), 16 * (ta // 2 + 1))
                        if ta >= 2:
                            a.wait_ge(sem("s_qd"), ta - 1)
                        iy = a.activation(y0d[ta % 2][:], r2B[ta % 2][:], AF.Identity,
                                          bias=b3s[:, ta:ta + 1],
                                          scale=a3s[:, ta:ta + 1])
                        inc("s_y0d", iy)
                        ir = a.activation(r16d[ta % 2][:], y0d[ta % 2][:], AF.Identity,
                                          bias=zerov[:], scale=1.0)
                        inc("s_r16d", ir)
                    if it >= 1:
                        tb = it - 1
                        for hs in range(HS):
                            q = HS * tb + hs
                            a.wait_ge(sem("s_h3"), q + 1)
                            ig = a.activation(h_b[:, 512 * hs:512 * (hs + 1)],
                                              ps_h[q % 6][:], AF.Gelu,
                                              bias=zerov[:], scale=g3s[:, tb:tb + 1])
                            inc("s_gel", ig)
                        a.wait_ge(sem("s_amx"), tb + 1)
                        if tb >= 2:
                            a.wait_ge(sem("s_qh"), tb - 1)
                        a.activation(y0h[tb % 2][:], h_b[:], AF.Identity,
                                     bias=halfv[:], scale=sch_all[:, tb:tb + 1])
                        irh = a.activation(r16h[tb % 2][:], y0h[tb % 2][:],
                                           AF.Identity, bias=zerov[:], scale=1.0)
                        inc("s_r16h", irh)

            @blk.sync
            def _(s):
                for ta in range(NT2):
                    s.wait_ge(sem("s_qd"), ta + 1)
                    if ta >= 2:
                        s.wait_ge(sem("s_h3"), HS * (ta - 1))
                    inc(f"s_qdT{ta % 2}", s.dma_start_transpose(
                        qdTv[ta % 2], qd[ta % 2][:]), 16)

            @blk.tensor
            def _(t):
                t.wait_ge(sem("s_w1T"), 16)
                for ta in range(NT2):
                    t.wait_ge(sem(f"s_qdT{ta % 2}"), 16 * (ta // 2 + 1))
                    for hs in range(HS):
                        q = HS * ta + hs
                        if q >= 6:
                            t.wait_ge(sem("s_gel"), q - 5)
                        last = None
                        for j in range(DT):
                            last = t.matmul(ps_h[q % 6][:],
                                            qdTv[ta % 2][:, j, :],
                                            w1v[:, j, 512 * hs:512 * (hs + 1)],
                                            start=(j == 0), stop=(j == DT - 1))
                        inc("s_h3", last)

        if "B2" not in phases:
            return nc
        with contextlib.ExitStack() as p2:
            def sb2(name, shape, dt):
                return p2.enter_context(nc.sbuf_tensor(name, shape, dt))
            w2T = sb2("w2T", [128, HT * D], BF16)
            qhT = [sb2(f"qhT{i}", [128, HT * 128], BF16) for i in range(2)]
            r2C = [sb2(f"r2C{i}", [128, D], F32) for i in range(2)]
            ffn = [sb2(f"ffn{i}", [128, D], F32) for i in range(2)]
            outb = [sb2(f"outb{i}", [128, D], F32) for i in range(2)]
            rstd4c = sb2("rstd4c", [128, 2], F32)
            bn64 = sb2("bn64", [128, 12], F32)
            bn24 = sb2("bn24", [128, 2], F32)
            vee4 = sb2("vee4", [128, 1], F32)
            lnv4 = sb2("lnv4", [128, 1], F32)
            bias4 = sb2("bias4", [128, 1], F32)
            t4 = sb2("t4", [128, D], F32)
            ps_f = [p2.enter_context(nc.psum_tensor(f"ps_f{i}", [128, D], F32))
                    for i in range(2)]
            blk = p2.enter_context(nc.Block())
            w2v = w2T[:].rearrange("p (i f) -> p i f", i=HT)
            qhTv = [qhT[i][:].rearrange("p (k t) -> p k t", k=HT) for i in range(2)]
            NT2 = NT

            @blk.gpsimd
            def _(g):
                g = _Proxy(chains["g"], g)
                inc("s_w2T", g.dma_start(w2T[:], w2t_d), 16)
                for ta in range(NT2):
                    if ta >= 2:
                        g.wait_ge(sem("s_outv"), ta - 1)
                    inc(f"s_r2C{ta % 2}", g.dma_start(
                        r2C[ta % 2][:], r2_d[128 * ta:128 * (ta + 1), :]), 16)
                    if ta >= 1:
                        tp = ta - 1
                        g.wait_ge(sem("s_outv"), tp + 1)
                        inc(f"s_outd{tp % 2}", g.dma_start(
                            y_d[128 * tp:128 * (tp + 1), :], outb[tp % 2][:]), 16)
                tp = NT2 - 1
                g.wait_ge(sem("s_outv"), tp + 1)
                inc(f"s_outd{tp % 2}", g.dma_start(
                    y_d[128 * tp:128 * (tp + 1), :], outb[tp % 2][:]), 16)
                g.wait_ge(sem("s_outd0"), 16 * ((NT2 + 1) // 2))
                g.wait_ge(sem("s_outd1"), 16 * (NT2 // 2))

            @blk.sync
            def _(s):
                for ta in range(NT2):
                    if ta >= 2:
                        s.wait_ge(sem("s_f4"), ta - 1)
                    inc(f"s_qhT{ta % 2}", s.dma_start_transpose(
                        qhTv[ta % 2], qh_d[128 * ta:128 * (ta + 1), :]), 16)

            @blk.vector
            def _(v):
                v = _Proxy(chains["v"], v)
                v.reciprocal(g4s[:], sch_all[:])
                i1 = v.tensor_scalar(g4s[:], g4s[:], sflv[1][:], None, op0=ALU.mult)
                inc("s_pre2", i1)
                for ta in range(NT2):
                    v.wait_ge(sem("s_ffn"), ta + 1)
                    v.bn_stats(bn64[:, 0:6], ffn[ta % 2][:, 0:512])
                    v.bn_stats(bn64[:, 6:12], ffn[ta % 2][:, 512:1024])
                    v.bn_aggr(bn24[:], bn64[:])
                    iv4 = v.tensor_scalar(vee4[:], bn24[:, 1:2], EPS, None, op0=ALU.add)
                    inc("s_st4", iv4)
                    v.wait_ge(sem("s_rs4"), ta + 1)
                    v.tensor_scalar(bias4[:], bn24[:, 0:1],
                                    rstd4c[:, ta % 2:ta % 2 + 1], -1.0,
                                    op0=ALU.mult, op1=ALU.mult)
                    v.tensor_scalar(t4[:], ffn[ta % 2][:],
                                    rstd4c[:, ta % 2:ta % 2 + 1], bias4[:],
                                    op0=ALU.mult, op1=ALU.add)
                    v.wait_ge(sem(f"s_r2C{ta % 2}"), 16 * (ta // 2 + 1))
                    if ta >= 2:
                        v.wait_ge(sem(f"s_outd{ta % 2}"), 16 * (ta // 2))
                    io = v.tensor_tensor(outb[ta % 2][:], t4[:], r2C[ta % 2][:],
                                         op=ALU.add)
                    inc("s_outv", io)

            @blk.scalar
            def _(a):
                a = _Proxy(chains["a"], a)
                a.wait_ge(sem("s_pre2"), 1)
                for ta in range(NT2):
                    a.wait_ge(sem("s_f4"), ta + 1)
                    iff = a.activation(ffn[ta % 2][:], ps_f[ta % 2][:], AF.Identity,
                                       bias=zerov[:], scale=g4s[:, ta:ta + 1])
                    inc("s_ffn", iff)
                    a.wait_ge(sem("s_st4"), ta + 1)
                    a.activation(lnv4[:], vee4[:], AF.Ln, bias=zerov[:], scale=1.0)
                    ie = a.activation(rstd4c[:, ta % 2:ta % 2 + 1], lnv4[:], AF.Exp,
                                      bias=zerov[:], scale=-0.5)
                    inc("s_rs4", ie)

            @blk.tensor
            def _(t):
                t.wait_ge(sem("s_w2T"), 16)
                for ta in range(NT2):
                    t.wait_ge(sem(f"s_qhT{ta % 2}"), 16 * (ta // 2 + 1))
                    if ta >= 2:
                        t.wait_ge(sem("s_ffn"), ta - 1)
                    last = None
                    for half in range(FH):
                        pslice = ps_f[ta % 2][:, 512 * half:512 * (half + 1)]
                        for j in range(HT):
                            last = t.matmul(pslice, qhTv[ta % 2][:, j, :],
                                            w2v[:, j, 512 * half:512 * (half + 1)],
                                            start=(j == 0), stop=(j == HT - 1))
                    inc("s_f4", last)

    return nc


_NC_CACHE = {}


def kernel(L0_attn_x, L0_attn_lambda_log, L0_attn_gamma, L0_attn_W_in,
           L0_attn_b_in, L0_attn_W_out, L0_attn_b_out, L0_ffn_w1, L0_ffn_w2):
    x = np.ascontiguousarray(np.asarray(L0_attn_x, dtype=np.float32))
    Bf, Lf, Df = x.shape
    nb = Bf // NCORES
    key = (nb, Lf, Df)
    if key not in _NC_CACHE:
        _NC_CACHE[key] = build_program(NB=nb, L=Lf, D=Df,
                                       H=np.asarray(L0_ffn_w1).shape[0])
    nc = _NC_CACHE[key]

    def row(a):
        return np.ascontiguousarray(np.asarray(a, dtype=np.float32)).reshape(1, -1)

    shared = {
        "w_in": np.ascontiguousarray(np.asarray(L0_attn_W_in, dtype=np.float32)),
        "w_out": np.ascontiguousarray(np.asarray(L0_attn_W_out, dtype=np.float32)),
        "b_in": row(L0_attn_b_in),
        "b_out": row(L0_attn_b_out),
        "gamma": row(L0_attn_gamma),
        "lamlog": row(L0_attn_lambda_log),
        "w1": np.ascontiguousarray(np.asarray(L0_ffn_w1, dtype=np.float32)),
        "w2": np.ascontiguousarray(np.asarray(L0_ffn_w2, dtype=np.float32)),
        "ident": np.eye(128, dtype=np.float32),
    }
    in_maps = []
    for c in range(NCORES):
        m = dict(shared)
        m["x"] = np.ascontiguousarray(
            x[c * nb:(c + 1) * nb].reshape(nb * Lf, Df))
        in_maps.append(m)
    res = run_bass_kernel_spmd(nc, in_maps, core_ids=list(range(NCORES)))
    out = np.empty((Bf, Lf, Df), dtype=np.float32)
    for c in range(NCORES):
        out[c * nb:(c + 1) * nb] = np.asarray(
            res.results[c]["y"], dtype=np.float32).reshape(nb, Lf, Df)
    return out


# version 20
# speedup vs baseline: 1.0210x; 1.0210x over previous
import contextlib
import numpy as np
import ml_dtypes
import concourse.bass as bass
import concourse.mybir as mybir
from concourse.bass_utils import run_bass_kernel_spmd

F32 = mybir.dt.float32
F32R = mybir.dt.float32r
BF16 = mybir.dt.bfloat16
I16 = mybir.dt.int16
ALU = mybir.AluOpType
AF = mybir.ActivationFunctionType
AX = mybir.AxisListType

NCORES = 8
EPS = 1e-5


def build_program(NB=4, L=2048, D=1024, H=4096, TA=256,
                  phases=("0", "A", "B1", "B2"), acap=None, dbgqd=False):
    DT = D // 128
    HT = H // 128
    R = NB * L
    NBLK = R // TA
    TPB = TA // 128
    NT = R // 128
    SEGB = L // TA
    HS = H // 512
    FH = D // 512

    nc = bass.Bass("TRN2", target_bir_lowering=False)

    x_d = nc.dram_tensor("x", [R, D], F32, kind="ExternalInput").ap()
    win_d = nc.dram_tensor("w_in", [D, D], F32, kind="ExternalInput").ap()
    wout_d = nc.dram_tensor("w_out", [D, D], F32, kind="ExternalInput").ap()
    bin_d = nc.dram_tensor("b_in", [1, D], F32, kind="ExternalInput").ap()
    bout_d = nc.dram_tensor("b_out", [1, D], F32, kind="ExternalInput").ap()
    gam_d = nc.dram_tensor("gamma", [1, D], F32, kind="ExternalInput").ap()
    laml_d = nc.dram_tensor("lamlog", [1, D], F32, kind="ExternalInput").ap()
    w1_d = nc.dram_tensor("w1", [H, D], F32, kind="ExternalInput").ap()
    w2_d = nc.dram_tensor("w2", [D, H], F32, kind="ExternalInput").ap()
    id_d = nc.dram_tensor("ident", [128, 128], F32, kind="ExternalInput").ap()
    y_d = nc.dram_tensor("y", [R, D], F32, kind="ExternalOutput").ap()
    r2_d = nc.dram_tensor("r2scr", [R, D], F32, kind="Internal").ap()
    qh_d = nc.dram_tensor("qhscr", [R, H], BF16, kind="Internal").ap()
    w1t_d = nc.dram_tensor("w1tscr", [128, DT * H], BF16, kind="Internal").ap()
    w2t_d = nc.dram_tensor("w2tscr", [128, HT * D], BF16, kind="Internal").ap()
    qd_d = nc.dram_tensor("qdscr", [R, D], BF16, kind="Internal").ap() if dbgqd else None
    scrq_d = nc.dram_tensor("scrq", [2, 128], F32, kind="Internal").ap()
    scrt_d = nc.dram_tensor("scrt", [2, 4], F32, kind="Internal").ap()

    es = contextlib.ExitStack()
    with es:
        def sb(name, shape, dt):
            return es.enter_context(nc.sbuf_tensor(name, shape, dt))

        zerov = sb("zerov", [128, 1], F32)
        halfv = sb("halfv", [128, 1], F32)
        epsv = sb("epsv", [128, 1], F32)
        ones_col = sb("ones_col", [128, 1], F32)
        ones_row = sb("ones_row", [1, 128], F32)
        thrv = [sb(f"thrv{w}", [128, 1], F32) for w in range(2)]
        nthrv = [sb(f"nthrv{w}", [128, 1], F32) for w in range(2)]
        sflv = [sb(f"sflv{w}", [128, 1], F32) for w in range(2)]
        rstd3s = sb("rstd3s", [128, NT], F32)
        m3s = sb("m3s", [128, NT], F32)
        am3s = sb("am3s", [128, NT], F32)
        sch_all = sb("sch_all", [128, NT], F32)
        a3s = sb("a3s", [128, NT], F32)
        b3s = sb("b3s", [128, NT], F32)
        g3s = sb("g3s", [128, NT], F32)
        g4s = sb("g4s", [128, NT], F32)
        p0A = es.enter_context(contextlib.ExitStack())
        def sb0A(name, shape, dt):
            return p0A.enter_context(nc.sbuf_tensor(name, shape, dt))
        ones_rr = sb0A("ones_rr", [1, 128], F32R)
        identt = sb0A("identt", [128, 128], F32)
        vecs3 = sb0A("vecs3", [128, 3 * DT], F32)
        lamx = sb0A("lamx", [128, DT], F32)
        gbv = sb0A("gbv", [128, DT], F32)
        bout_r = sb0A("bout_r", [1, D], F32R)
        winT = sb0A("winT", [128, DT * D], F32R)
        woutT = sb0A("woutT", [128, DT * D], F32R)

        sems = {}
        def S(n):
            sems[n] = [es.enter_context(nc.semaphore(n)), 0]
        for n in ["s0_row", "s0_wld0", "s0_wld1", "s0_vt", "s0_ve", "s0_lam", "s0_mis",
                  "s0_id", "s0_wtp", "s0_wte",
                  "s0_al0", "s0_al1", "s0_red", "s0_sum", "s0_ssc", "s0_bcm",
                  "s0_thr", "s0_bl0", "s0_bl1", "s0_tern", "s0_tp0", "s0_tp1", "s0_xb", "s0_sgn",
                  "s0_wst",
                  "sxA0", "sxA1", "s_st0", "s_rs0", "s_b0", "s_ln1", "s_t2",
                  "s_lt", "s_lte", "s_u",
                  "s_gu", "s_y", "s_a2", "s_attn", "s_st2", "s_rs2",
                  "s_r2v", "s_r2d0", "s_r2d1", "s_rs3", "s_st3",
                  "s_w1T", "s_r2B0", "s_r2B1", "s_pre", "s_y0d", "s_r16d",
                  "s_qd", "s_qdT0", "s_qdT1", "s_h3", "s_gel", "s_amx",
                  "s_r16h", "s_qh", "s_qhd0", "s_qhd1",
                  "s_w2T", "s_r2C0", "s_r2C1", "s_pre2", "s_qhT0", "s_qhT1",
                  "s_f4", "s_ffn", "s_st4", "s_rs4", "s_outv",
                  "s_outd0", "s_outd1", "s_dbg0", "s_dbg1"]:
            S(n)

        def sem(n):
            return sems[n][0]

        _CHAIN_OPS = {
            "tensor_scalar", "tensor_tensor", "tensor_copy", "tensor_reduce",
            "reduce_max", "reduce_sum", "bn_stats", "bn_aggr", "reciprocal",
            "tensor_tensor_scan", "scalar_tensor_tensor", "memset",
            "activation", "mul", "add", "copy", "sign", "sqrt", "square",
            "iota", "select",
        }
        _DMA_OPS = {"dma_start", "dma_start_transpose", "indirect_dma_start"}

        class _Chain:
            def __init__(self, name):
                self.h = es.enter_context(nc.semaphore(name))
                self.cnt = 0
                self.pending = None
                self.last = None

            def flush(self):
                if self.pending is not None:
                    self.pending.then_inc(self.h, 1)
                    self.cnt += 1
                    self.last = (self.h, self.cnt)
                    self.pending = None

            def pre(self, eng):
                self.flush()
                if self.last is not None:
                    eng.wait_ge(*self.last)

        class _Proxy:
            def __init__(self, chain, eng):
                self._c = chain
                self._e = eng

            def __getattr__(self, name):
                c, e = self._c, self._e
                if name in _CHAIN_OPS:
                    f = getattr(e, name)
                    def w(*a, **k):
                        c.pre(e)
                        ins = f(*a, **k)
                        c.pending = ins
                        return ins
                    return w
                if name in _DMA_OPS:
                    f = getattr(e, name)
                    def wd(*a, **k):
                        c.pre(e)
                        return f(*a, **k)
                    return wd
                return getattr(e, name)

        chains = {n: _Chain("ch_" + n) for n in ("v", "a", "g")}

        def inc(n, instr, k=1):
            instr.then_inc(sems[n][0], k)
            sems[n][1] += k
            for c in chains.values():
                if c.pending is instr:
                    c.pending = None
                    c.last = (sems[n][0], sems[n][1])

        with contextlib.ExitStack() as p0:
            def sb0(name, shape, dt):
                return p0.enter_context(nc.sbuf_tensor(name, shape, dt))
            rowbuf = sb0("rowbuf", [1, 3 * D], F32)
            wstg = [sb0(f"wstg{i}", [128, 4096], F32) for i in range(2)]
            ps_v = p0.enter_context(nc.psum_tensor("ps_v", [128, 3 * DT], F32))
            ps_t = [p0.enter_context(nc.psum_tensor(f"ps_t{i}", [128, 512], F32))
                    for i in range(2)]
            blk = p0.enter_context(nc.Block())

            def awchunk_ap(i):
                w, c = divmod(i, 2)
                src = win_d if w == 0 else wout_d
                return src[512 * c:512 * (c + 1), :].rearrange(
                    "(j p) d -> p j d", p=128)

            @blk.gpsimd
            def _(g):
                g = _Proxy(chains["g"], g)
                g.memset(zerov[:], 0.0)
                g.memset(halfv[:], 385.5)
                g.memset(epsv[:], EPS)
                g.memset(ones_col[:], 1.0)
                g.memset(ones_row[:], 1.0)
                inc("s0_row", g.dma_start(rowbuf[:, 0:D], bout_d), 16)
                g.wait_ge(sem("s0_ve"), 1)
                inc("s0_row", g.dma_start(rowbuf[:, 0:D], laml_d), 16)
                inc("s0_row", g.dma_start(rowbuf[:, D:2 * D], gam_d), 16)
                inc("s0_row", g.dma_start(rowbuf[:, 2 * D:3 * D], bin_d), 16)
                inc("s0_id", g.dma_start(identt[:], id_d), 16)
                for i in range(4):
                    if i >= 2:
                        g.wait_ge(sem("s0_wte"), 32 * (i - 1))
                    inc(f"s0_wld{i % 2}", g.dma_start(
                        wstg[i % 2][:].rearrange("p (j d) -> p j d", j=4),
                        awchunk_ap(i)), 16)

            @blk.tensor
            def _(t):
                t.wait_ge(sem("s0_row"), 64)
                mv = None
                for n in range(3 * DT):
                    mv = t.matmul(ps_v[:, n:n + 1], rowbuf[0:1, 128 * n:128 * (n + 1)],
                                  ones_col[0:1, 0:1], start=True, stop=True)
                inc("s0_vt", mv)
                t.wait_ge(sem("s0_id"), 16)
                ntp = 0
                for i in range(4):
                    t.wait_ge(sem(f"s0_wld{i % 2}"), 16 * (i // 2 + 1))
                    for j in range(4):
                        for dd in range(DT):
                            gidx = ntp
                            if gidx >= 8:
                                t.wait_ge(sem("s0_wte"), 4 * (gidx // 4 - 1))
                            mm = t.transpose(
                                ps_t[(gidx // 4) % 2][:, 128 * (gidx % 4):128 * (gidx % 4 + 1)],
                                wstg[i % 2][:, 1024 * j + 128 * dd:1024 * j + 128 * (dd + 1)],
                                identt[:])
                            if gidx % 4 == 3:
                                inc("s0_wtp", mm)
                            ntp += 1

            @blk.vector
            def _(v):
                v = _Proxy(chains["v"], v)
                v.wait_ge(sem("s0_row"), 16)
                v.tensor_copy(ones_rr[:], ones_row[:])
                ic = v.tensor_copy(bout_r[:], rowbuf[:, 0:D])
                inc("s0_ve", ic)
                v.wait_ge(sem("s0_vt"), 1)
                ic2 = v.tensor_copy(vecs3[:], ps_v[:])
                inc("s0_ve", ic2)
                wiv = winT[:].rearrange("p (i d) -> p i d", i=DT)
                wov = woutT[:].rearrange("p (i d) -> p i d", i=DT)
                ntp = 0
                for i in range(4):
                    w, c = divmod(i, 2)
                    for j in range(4):
                        etile = 4 * c + j
                        for dd in range(DT):
                            gidx = ntp
                            if gidx % 4 == 3:
                                grp = gidx // 4
                                v.wait_ge(sem("s0_wtp"), grp + 1)
                                dst = wiv if w == 0 else wov
                                iv_ = v.tensor_copy(
                                    dst[:, dd - 3:dd + 1, 128 * etile:128 * (etile + 1)],
                                    ps_t[grp % 2][:].rearrange("p (q r) -> p q r", q=4))
                                inc("s0_wte", iv_, 4)
                            ntp += 1
                v.wait_ge(sem("s0_lam"), 1)
                ig = v.tensor_tensor(gbv[:], vecs3[:, DT:2 * DT],
                                     vecs3[:, 2 * DT:3 * DT], op=ALU.mult)
                inc("s0_mis", ig)

            @blk.scalar
            def _(a):
                a = _Proxy(chains["a"], a)
                a.wait_ge(sem("s0_ve"), 2)
                ie = a.activation(lamx[:], vecs3[:, 0:DT], AF.Exp,
                                  bias=zerov[:], scale=1.0)
                inc("s0_lam", ie)

        if "A" not in phases:
            return nc
        NBLKA = NBLK if acap is None else min(NBLK, acap)
        GPB = TPB * DT // 4
        with contextlib.ExitStack() as pA:
            def sbA(name, shape, dt):
                return pA.enter_context(nc.sbuf_tensor(name, shape, dt))
            lam_bc = sbA("lam_bc", [128, DT * TA], F32)
            x_buf = [sbA(f"x_buf{i}", [128, TPB * D], F32) for i in range(2)]
            ln1f = [sbA(f"ln1f{i}", [128, TPB * D], F32) for i in range(2)]
            lnT = [sbA(f"lnT{i}", [128, DT * TA], F32R) for i in range(2)]
            gu_b = [sbA(f"gu{i}", [128, TA], F32) for i in range(2)]
            ybf = [sbA(f"ybf{i}", [128, DT * TA], F32R) for i in range(2)]
            attn_b = [sbA(f"attn_b{i}", [128, D], F32) for i in range(2)]
            t2buf = [sbA(f"t2buf{i}", [128, D], F32) for i in range(2)]
            r2b = [sbA(f"r2b{i}", [128, D], F32) for i in range(2)]
            rstd0s = [sbA(f"rstd0s{i}", [128, TPB], F32) for i in range(2)]
            bias0s = [sbA(f"bias0s{i}", [128, TPB], F32) for i in range(2)]
            rstd2c = [sbA(f"rstd2c{i}", [128, TPB], F32) for i in range(2)]
            bn6 = sbA("bn6", [128, 12], F32)
            bn2 = [sbA(f"bn2_{i}", [128, 2], F32) for i in range(2)]
            bn6a = sbA("bn6a", [128, 12], F32)
            bn2a = [sbA(f"bn2a_{i}", [128, 2], F32) for i in range(2)]
            bn6r = sbA("bn6r", [128, 12], F32)
            bn2r = [sbA(f"bn2r_{i}", [128, 2], F32) for i in range(2)]
            lnv = sbA("lnv", [128, 1], F32)
            nrstd = sbA("nrstd", [128, 1], F32)
            bias2 = sbA("bias2", [128, 1], F32)
            tmp1 = sbA("tmp1", [128, D], F32)
            qstg = [sbA(f"qstg{i}", [128, 2048], F32) for i in range(2)]
            t1b = sbA("t1b", [128, 2048], BF16)
            t2b = sbA("t2b", [128, 2048], BF16)
            wtp = [sbA(f"wtp{i}", [128, 2048], BF16) for i in range(2)]
            qracc = sbA("qracc", [128, 1], F32)
            qrtmp = sbA("qrtmp", [128, 1], F32)
            qrow = sbA("qrow", [1, 128], F32)
            qssc = sbA("qssc", [1, 1], F32)
            qth = sbA("qth", [1, 4], F32)
            ps_u = [pA.enter_context(nc.psum_tensor(f"ps_u{i}", [128, TA], F32))
                    for i in range(2)]
            ps_a = [pA.enter_context(nc.psum_tensor(f"ps_a{i}", [128, D], F32))
                    for i in range(2)]
            ps_t2 = [pA.enter_context(nc.psum_tensor(f"ps_t2{i}", [128, 512], F32))
                     for i in range(2)]
            blk = pA.enter_context(nc.Block())

            def w1src(c):
                return w1_d[256 * c:256 * (c + 1), :].rearrange(
                    "(j p) d -> p j d", p=128)

            def w2src(c):
                rb, hf = divmod(c, 2)
                return w2_d[128 * rb:128 * (rb + 1), 2048 * hf:2048 * (hf + 1)]

            def qload_src(q):
                if q < 16:
                    return w1src(q), True
                if q < 32:
                    return w2src(q - 16), False
                if q < 48:
                    return w1src(q - 32), True
                return w2src(q - 48), False

            def pieces_through(c2):
                return 2 * (c2 + 1) if c2 < 16 else 32 + (c2 - 15)

            QPIECES = ([(c2, j) for c2 in range(16) for j in range(2)]
                       + [(c2, None) for c2 in range(16, 32)])
            QN_EL = H * D

            def xap(b):
                return x_d[b * TA:(b + 1) * TA, :].rearrange("(t p) d -> p t d", p=128)

            xv = [x_buf[i][:].rearrange("p (t d) -> p t d", t=TPB) for i in range(2)]
            lTv = [lnT[i][:].rearrange("p (k t) -> p k t", k=DT) for i in range(2)]
            ybv = [ybf[i][:].rearrange("p (k t) -> p k t", k=DT) for i in range(2)]
            wiv = winT[:].rearrange("p (i d) -> p i d", i=DT)
            wov = woutT[:].rearrange("p (i d) -> p i d", i=DT)
            lbv = lam_bc[:].rearrange("p (k t) -> p k t", k=DT)

            @blk.gpsimd
            def _(g):
                g = _Proxy(chains["g"], g)
                for b in range(NBLKA):
                    if b >= 2:
                        g.wait_ge(sem("s_r2v"), TPB * (b - 1))
                    inc(f"sxA{b % 2}", g.dma_start(xv[b % 2], xap(b)), 16)
                    if b >= 1:
                        for t in range(TPB):
                            ta = TPB * (b - 1) + t
                            g.wait_ge(sem("s_r2v"), ta + 1)
                            inc(f"s_r2d{ta % 2}", g.dma_start(
                                r2_d[128 * ta:128 * (ta + 1), :], r2b[ta % 2][:]), 16)
                for t in range(TPB):
                    ta = TPB * (NBLKA - 1) + t
                    g.wait_ge(sem("s_r2v"), ta + 1)
                    inc(f"s_r2d{ta % 2}", g.dma_start(
                        r2_d[128 * ta:128 * (ta + 1), :], r2b[ta % 2][:]), 16)
                g.wait_ge(sem("s_r2d0"), 16 * (TPB * NBLKA // 2))
                g.wait_ge(sem("s_r2d1"), 16 * (TPB * NBLKA // 2))
                g.wait_ge(sem("sxA0"), 16 * ((NBLKA + 1) // 2))
                g.wait_ge(sem("sxA1"), 16 * (NBLKA // 2))

            @blk.vector
            def _(v):
                v = _Proxy(chains["v"], v)

                def q_dve_tasks():
                    def reduce_chunk(i):
                        v.wait_ge(sem(f"s0_al{i % 2}"), 16 * (i // 2 + 1))
                        v.tensor_reduce(qrtmp[:], qstg[i % 2][:], axis=AX.X,
                                        op=ALU.add, apply_absolute_value=True)
                        ia = v.tensor_tensor(qracc[:], qracc[:], qrtmp[:],
                                             op=ALU.add)
                        inc("s0_red", ia)

                    def scale_dve(w):
                        v.wait_ge(sem("s0_sum"), 32 * (w + 1))
                        if w == 1:
                            v.wait_ge(sem("s0_bcm"), 16)
                        v.tensor_reduce(qssc[:], qrow[:], axis=AX.X, op=ALU.add)
                        v.tensor_scalar(qssc[:], qssc[:], 1.0 / QN_EL, EPS,
                                        op0=ALU.mult, op1=ALU.max)
                        v.tensor_scalar(qth[:, 0:1], qssc[:], 0.5, None,
                                        op0=ALU.mult)
                        v.tensor_scalar(qth[:, 1:2], qth[:, 0:1], -1.0,
                                        None, op0=ALU.mult)
                        iq = v.tensor_scalar(qth[:, 2:3], qssc[:], -1.0,
                                             None, op0=ALU.mult)
                        inc("s0_ssc", iq)

                    yield lambda: v.memset(qracc[:], 0.0)
                    for i in range(16):
                        yield lambda i=i: reduce_chunk(i)

                    def switch():
                        v.wait_ge(sem("s0_sum"), 16)
                        v.memset(qracc[:], 0.0)
                    yield switch
                    yield lambda: scale_dve(0)
                    for i in range(16, 32):
                        yield lambda i=i: reduce_chunk(i)
                    for c2 in range(32):
                        if c2 == 13:
                            yield lambda: scale_dve(1)
                        def tern(c2=c2):
                            q = 32 + c2
                            w = 0 if c2 < 16 else 1
                            v.wait_ge(sem(f"s0_al{q % 2}"), 16 * (q // 2 + 1))
                            v.wait_ge(sem("s0_thr"), 48 * (w + 1))
                            v.tensor_scalar(t1b[:], qstg[q % 2][:], thrv[w][:],
                                            None, op0=ALU.is_ge)
                            if c2 >= 1:
                                v.wait_ge(sem("s0_xb"),
                                          16 * pieces_through(c2 - 1))
                            it = v.scalar_tensor_tensor(
                                t2b[:], qstg[q % 2][:], nthrv[w][:], t1b[:],
                                op0=ALU.is_le, op1=ALU.subtract)
                            inc("s0_tern", it)
                        yield tern

                dveq = q_dve_tasks()

                def dve_qslot(n):
                    for _ in range(n):
                        t = next(dveq, None)
                        if t is None:
                            return
                        t()

                v.memset(lam_bc[:], 1.0)
                for k in range(DT):
                    v.tensor_scalar(lbv[:, k, :], lbv[:, k, :], lamx[:, k:k + 1],
                                    None, op0=ALU.mult)
                def scans_sec(b):
                    seg = b % SEGB
                    for k in range(DT):
                        gcnt = DT * b + k
                        v.wait_ge(sem("s_gu"), gcnt + 1)
                        if b >= 2:
                            v.wait_ge(sem("s_a2"), TPB * (b - 1))
                        init = 0.0 if seg == 0 else ybv[(b - 1) % 2][:, k, TA - 1:TA]
                        isc = v.tensor_tensor_scan(ybv[b % 2][:, k, :], lbv[:, k, :],
                                                   gu_b[gcnt % 2][:], init,
                                                   op0=ALU.mult, op1=ALU.add)
                        inc("s_y", isc)

                def attnp(b):
                    for t in range(TPB):
                        ta = TPB * b + t
                        v.wait_ge(sem("s_a2"), ta + 1)
                        if ta >= 2:
                            v.wait_ge(sem("s_t2"), ta - 1)
                        iat = v.tensor_tensor(attn_b[ta % 2][:],
                                              ln1f[b % 2][:, t * D:(t + 1) * D],
                                              ps_a[ta % 2][:], op=ALU.add)
                        inc("s_attn", iat)
                        v.bn_stats(bn6a[:, 0:6], attn_b[ta % 2][:, 0:512])
                        v.bn_stats(bn6a[:, 6:12], attn_b[ta % 2][:, 512:1024])
                        ia2 = v.bn_aggr(bn2a[ta % 2][:], bn6a[:])
                        inc("s_st2", ia2)

                def r2p(b):
                    for t in range(TPB):
                        ta = TPB * b + t
                        v.wait_ge(sem("s_t2"), ta + 1)
                        if ta >= 2:
                            v.wait_ge(sem(f"s_r2d{ta % 2}"), 16 * (ta // 2))
                        ir2 = v.tensor_tensor(r2b[ta % 2][:], t2buf[ta % 2][:],
                                              xv[b % 2][:, t, :], op=ALU.add)
                        inc("s_r2v", ir2)
                        v.bn_stats(bn6r[:, 0:6], r2b[ta % 2][:, 0:512])
                        v.bn_stats(bn6r[:, 6:12], r2b[ta % 2][:, 512:1024])
                        if ta >= 2:
                            v.wait_ge(sem("s_rs3"), ta - 1)
                        ia3 = v.bn_aggr(bn2r[ta % 2][:], bn6r[:])
                        inc("s_st3", ia3)
                        v.tensor_scalar(tmp1[:], r2b[ta % 2][:], bn2r[ta % 2][:, 0:1],
                                        None, op0=ALU.subtract)
                        v.tensor_reduce(am3s[:, ta:ta + 1], tmp1[:], axis=AX.X,
                                        op=ALU.max, apply_absolute_value=True)

                for b in range(NBLKA):
                    seg = b % SEGB
                    for t in range(TPB):
                        ta = TPB * b + t
                        v.wait_ge(sem(f"sxA{b % 2}"), 16 * (b // 2 + 1))
                        if ta >= 2:
                            v.wait_ge(sem("s_ln1"), ta - 1)
                        v.bn_stats(bn6[:, 0:6], x_buf[b % 2][:, t * D:t * D + 512])
                        v.bn_stats(bn6[:, 6:12], x_buf[b % 2][:, t * D + 512:t * D + 1024])
                        ia0 = v.bn_aggr(bn2[ta % 2][:], bn6[:])
                        inc("s_st0", ia0)
                    if b >= 1:
                        attnp(b - 1)
                    for gi in range(GPB):
                        grp = GPB * b + gi
                        tt = gi // (DT // 4)
                        q = gi % (DT // 4)
                        v.wait_ge(sem("s_lt"), grp + 1)
                        if b >= 2:
                            v.wait_ge(sem("s_u"), DT * (b - 1))
                        ie = v.tensor_copy(
                            lTv[b % 2][:, 4 * q:4 * q + 4, 128 * tt:128 * (tt + 1)],
                            ps_t2[grp % 2][:].rearrange("p (q r) -> p q r", q=4))
                        inc("s_lte", ie)
                    if b >= 1:
                        r2p(b - 1)
                    scans_sec(b)
                    dve_qslot(3)
                attnp(NBLKA - 1)
                r2p(NBLKA - 1)
                dve_qslot(100)

            @blk.scalar
            def _(a):
                a = _Proxy(chains["a"], a)

                def ln2p(b):
                    for t in range(TPB):
                        ta = TPB * b + t
                        a.wait_ge(sem("s_st2"), ta + 1)
                        a.activation(lnv[:], bn2a[ta % 2][:, 1:2], AF.Ln,
                                     bias=epsv[:], scale=1.0)
                        a.activation(rstd2c[b % 2][:, t:t + 1], lnv[:], AF.Exp,
                                     bias=zerov[:], scale=-0.5)
                        a.activation(nrstd[:], rstd2c[b % 2][:, t:t + 1], AF.Identity,
                                     bias=zerov[:], scale=-1.0)
                        a.activation(bias2[:], bn2a[ta % 2][:, 0:1], AF.Identity,
                                     bias=zerov[:], scale=nrstd[:])
                        if ta >= 2:
                            a.wait_ge(sem("s_r2v"), ta - 1)
                        it2 = a.activation(t2buf[ta % 2][:], attn_b[ta % 2][:],
                                           AF.Identity, bias=bias2[:],
                                           scale=rstd2c[b % 2][:, t:t + 1])
                        inc("s_t2", it2)

                def ln3p(b):
                    for t in range(TPB):
                        ta = TPB * b + t
                        a.wait_ge(sem("s_st3"), ta + 1)
                        a.activation(lnv[:], bn2r[ta % 2][:, 1:2], AF.Ln,
                                     bias=epsv[:], scale=1.0)
                        a.activation(rstd3s[:, ta:ta + 1], lnv[:], AF.Exp,
                                     bias=zerov[:], scale=-0.5)
                        im3 = a.activation(m3s[:, ta:ta + 1], bn2r[ta % 2][:, 0:1],
                                           AF.Identity, bias=zerov[:], scale=1.0)
                        inc("s_rs3", im3)

                for b in range(NBLKA):
                    for t in range(TPB):
                        ta = TPB * b + t
                        a.wait_ge(sem("s_st0"), ta + 1)
                        a.activation(lnv[:], bn2[ta % 2][:, 1:2], AF.Ln,
                                     bias=epsv[:], scale=1.0)
                        a.activation(rstd0s[b % 2][:, t:t + 1], lnv[:], AF.Exp,
                                     bias=zerov[:], scale=-0.5)
                        a.activation(nrstd[:], rstd0s[b % 2][:, t:t + 1], AF.Identity,
                                     bias=zerov[:], scale=-1.0)
                        a.activation(bias0s[b % 2][:, t:t + 1], bn2[ta % 2][:, 0:1],
                                     AF.Identity, bias=zerov[:], scale=nrstd[:])
                        if b >= 2:
                            a.wait_ge(sem("s_lt"), GPB * (b - 1))
                            a.wait_ge(sem("s_attn"), TPB * (b - 1))
                        i1 = a.activation(ln1f[b % 2][:, t * D:(t + 1) * D],
                                          xv[b % 2][:, t, :], AF.Identity,
                                          bias=bias0s[b % 2][:, t:t + 1],
                                          scale=rstd0s[b % 2][:, t:t + 1])
                        inc("s_ln1", i1)
                    if b >= 1:
                        ln2p(b - 1)
                    for k in range(DT):
                        gcnt = DT * b + k
                        a.wait_ge(sem("s_u"), gcnt + 1)
                        if gcnt >= 2:
                            a.wait_ge(sem("s_y"), gcnt - 1)
                        ig = a.activation(gu_b[gcnt % 2][:], ps_u[gcnt % 2][:],
                                          AF.Identity, bias=gbv[:, k:k + 1],
                                          scale=vecs3[:, DT + k:DT + k + 1])
                        inc("s_gu", ig)
                    if b >= 1:
                        ln3p(b - 1)
                ln2p(NBLKA - 1)
                ln3p(NBLKA - 1)

            @blk.sync
            def _(s):
                w1tv = w1t_d.rearrange("p (i h) -> p i h", i=DT)
                w2tv = w2t_d.rearrange("p (i f) -> p i f", i=HT)
                pctr = [0]

                def emit_pieces(c2):
                    npiece = 2 if c2 < 16 else 1
                    for j in (range(2) if c2 < 16 else [None]):
                        p = pctr[0]
                        pctr[0] += 1
                        s.wait_ge(sem("s0_tern"), c2 + 1)
                        if p >= 2:
                            s.wait_ge(sem(f"s0_bl{p % 2}"), 16 * (p // 2))
                        if j is None:
                            wv = wtp[p % 2][:].rearrange(
                                "p (i r) -> p i r", i=16)
                            xin = t2b[:]
                        else:
                            wv = wtp[p % 2][:, 0:1024].rearrange(
                                "p (i r) -> p i r", i=DT)
                            xin = t2b[:, 1024 * j:1024 * (j + 1)]
                        inc("s0_xb", s.dma_start_transpose(wv, xin), 16)
                        s.wait_ge(sem("s0_xb"), 16 * (p + 1))
                        if j is None:
                            c = c2 - 16
                            rb, hf = divmod(c, 2)
                            dstp = w2tv[:, 16 * hf:16 * (hf + 1),
                                        128 * rb:128 * (rb + 1)]
                        else:
                            hrow = 2 * c2 + j
                            dstp = w1tv[:, :, 128 * hrow:128 * (hrow + 1)]
                        inc(f"s0_bl{p % 2}", s.dma_start(dstp, wv), 16)

                for q in range(64):
                    if q == 18:
                        s.wait_ge(sem("s0_red"), 16)
                        inc("s0_sum", s.dma_start(scrq_d[0:1, :], qracc[:]), 16)
                        s.wait_ge(sem("s0_sum"), 16)
                        inc("s0_sum", s.dma_start(qrow[:], scrq_d[0:1, :]), 16)
                    if q == 20:
                        s.wait_ge(sem("s0_ssc"), 1)
                        inc("s0_bcm", s.dma_start(scrt_d[0:1, 0:3], qth[:, 0:3]), 16)
                        s.wait_ge(sem("s0_bcm"), 16)
                        inc("s0_thr", s.dma_start(
                            thrv[0][:],
                            scrt_d[0:1, 0:1].broadcast_to((128, 1))), 16)
                        inc("s0_thr", s.dma_start(
                            nthrv[0][:],
                            scrt_d[0:1, 1:2].broadcast_to((128, 1))), 16)
                        inc("s0_thr", s.dma_start(
                            sflv[0][:],
                            scrt_d[0:1, 2:3].broadcast_to((128, 1))), 16)
                    if q == 44:
                        s.wait_ge(sem("s0_red"), 32)
                        inc("s0_sum", s.dma_start(scrq_d[1:2, :], qracc[:]), 16)
                        s.wait_ge(sem("s0_sum"), 48)
                        inc("s0_sum", s.dma_start(qrow[:], scrq_d[1:2, :]), 16)
                    if q == 46:
                        s.wait_ge(sem("s0_ssc"), 2)
                        inc("s0_bcm", s.dma_start(scrt_d[1:2, 0:3], qth[:, 0:3]), 16)
                        s.wait_ge(sem("s0_bcm"), 32)
                        inc("s0_thr", s.dma_start(
                            thrv[1][:],
                            scrt_d[1:2, 0:1].broadcast_to((128, 1))), 16)
                        inc("s0_thr", s.dma_start(
                            nthrv[1][:],
                            scrt_d[1:2, 1:2].broadcast_to((128, 1))), 16)
                        inc("s0_thr", s.dma_start(
                            sflv[1][:],
                            scrt_d[1:2, 2:3].broadcast_to((128, 1))), 16)
                    if q >= 2:
                        if q - 2 < 32:
                            s.wait_ge(sem("s0_red"), q - 1)
                        else:
                            s.wait_ge(sem("s0_tern"), q - 33)
                    src, isw1 = qload_src(q)
                    dst = (qstg[q % 2][:].rearrange("p (j d) -> p j d", j=2)
                           if isw1 else qstg[q % 2][:])
                    inc(f"s0_al{q % 2}", s.dma_start(dst, src), 16)
                    if q >= 34:
                        emit_pieces(q - 34)
                emit_pieces(30)
                emit_pieces(31)
                s.wait_ge(sem("s0_bl0"), 16 * 24)
                s.wait_ge(sem("s0_bl1"), 16 * 24)
                s.wait_ge(sem("s0_al0"), 16 * 32)
                s.wait_ge(sem("s0_al1"), 16 * 32)

            @blk.tensor
            def _(t):
                def mm2sec(b):
                    t.wait_ge(sem("s_y"), DT * (b + 1))
                    for tt in range(TPB):
                        ta = TPB * b + tt
                        if ta >= 2:
                            t.wait_ge(sem("s_attn"), ta - 1)
                        last = None
                        for half in range(FH):
                            pslice = ps_a[ta % 2][:, 512 * half:512 * (half + 1)]
                            t.matmul(pslice, ones_rr[:],
                                     bout_r[0:1, 512 * half:512 * (half + 1)],
                                     start=True, stop=False)
                            for k in range(DT):
                                last = t.matmul(
                                    pslice,
                                    ybv[b % 2][:, k, 128 * tt:128 * (tt + 1)],
                                    wov[:, k, 512 * half:512 * (half + 1)],
                                    start=False, stop=(k == DT - 1))
                        inc("s_a2", last)

                for b in range(NBLKA):
                    if b >= 1:
                        mm2sec(b - 1)
                    for gi in range(GPB):
                        grp = GPB * b + gi
                        tt = gi // (DT // 4)
                        q = gi % (DT // 4)
                        ta = TPB * b + tt
                        t.wait_ge(sem("s_ln1"), ta + 1)
                        if grp >= 2:
                            t.wait_ge(sem("s_lte"), grp - 1)
                        mm = None
                        for r in range(4):
                            dd = 4 * q + r
                            mm = t.transpose(
                                ps_t2[grp % 2][:, 128 * r:128 * (r + 1)],
                                ln1f[b % 2][:, tt * D + 128 * dd:tt * D + 128 * (dd + 1)],
                                identt[:])
                        inc("s_lt", mm)
                    t.wait_ge(sem("s_lte"), GPB * (b + 1))
                    for k in range(DT):
                        gcnt = DT * b + k
                        if gcnt >= 2:
                            t.wait_ge(sem("s_gu"), gcnt - 1)
                        last = None
                        for j in range(DT):
                            last = t.matmul(ps_u[gcnt % 2][:],
                                            wiv[:, j, 128 * k:128 * (k + 1)],
                                            lTv[b % 2][:, j, :],
                                            start=(j == 0), stop=(j == DT - 1))
                        inc("s_u", last)
                mm2sec(NBLKA - 1)

        p0A.close()

        if "B1" not in phases:
            return nc
        with contextlib.ExitStack() as p1:
            def sb1(name, shape, dt):
                return p1.enter_context(nc.sbuf_tensor(name, shape, dt))
            w1T = sb1("w1T", [128, DT * H], BF16)
            r2B = [sb1(f"r2B{i}", [128, D], F32) for i in range(2)]
            y0d = [sb1(f"y0d{i}", [128, D], F32) for i in range(2)]
            r16d = [sb1(f"r16d{i}", [128, D], I16) for i in range(2)]
            bd = sb1("bd", [128, D], BF16)
            cd = sb1("cd", [128, D], BF16)
            qd = [sb1(f"qd{i}", [128, D], BF16) for i in range(2)]
            qdT = [sb1(f"qdT{i}", [128, DT * 128], BF16) for i in range(2)]
            h_b = sb1("h_b", [128, H], F32)
            y0h = [sb1(f"y0h{i}", [128, H], F32) for i in range(2)]
            r16h = [sb1(f"r16h{i}", [128, H], I16) for i in range(2)]
            bh = sb1("bh", [128, H], BF16)
            ch = sb1("ch", [128, H], BF16)
            qh_b = [sb1(f"qh_b{i}", [128, H], BF16) for i in range(2)]
            amh = sb1("amh", [128, 1], F32)
            irc = sb1("irc", [128, 1], F32)
            tnt = sb1("tnt", [128, NT], F32)
            ps_h = [p1.enter_context(nc.psum_tensor(f"ps_h{i}", [128, 512], F32))
                    for i in range(6)]
            blk = p1.enter_context(nc.Block())
            w1v = w1T[:].rearrange("p (i h) -> p i h", i=DT)
            qdTv = [qdT[i][:].rearrange("p (k t) -> p k t", k=DT) for i in range(2)]
            NT2 = NT

            @blk.gpsimd
            def _(g):
                g = _Proxy(chains["g"], g)
                inc("s_w1T", g.dma_start(w1T[:], w1t_d), 16)
                for ta in range(min(2, NT2)):
                    inc(f"s_r2B{ta % 2}", g.dma_start(
                        r2B[ta % 2][:], r2_d[128 * ta:128 * (ta + 1), :]), 16)
                for ta in range(NT2):
                    if ta + 2 < NT2:
                        g.wait_ge(sem("s_y0d"), ta + 1)
                        inc(f"s_r2B{ta % 2}", g.dma_start(
                            r2B[ta % 2][:], r2_d[128 * (ta + 2):128 * (ta + 3), :]),
                            16)
                    if dbgqd:
                        g.wait_ge(sem("s_qd"), ta + 1)
                        inc(f"s_dbg{ta % 2}", g.dma_start(
                            qd_d[128 * ta:128 * (ta + 1), :], qd[ta % 2][:]), 16)
                    g.wait_ge(sem("s_qh"), ta + 1)
                    inc(f"s_qhd{ta % 2}", g.dma_start(
                        qh_d[128 * ta:128 * (ta + 1), :], qh_b[ta % 2][:]), 16)
                g.wait_ge(sem("s_qhd0"), 16 * ((NT2 + 1) // 2))
                g.wait_ge(sem("s_qhd1"), 16 * (NT2 // 2))
                g.wait_ge(sem("s_r2B0"), 16 * ((NT2 + 1) // 2))
                g.wait_ge(sem("s_r2B1"), 16 * (NT2 // 2))
                if dbgqd:
                    g.wait_ge(sem("s_dbg0"), 16 * ((NT2 + 1) // 2))
                    g.wait_ge(sem("s_dbg1"), 16 * (NT2 // 2))

            @blk.vector
            def _(v):
                v = _Proxy(chains["v"], v)
                v.tensor_tensor(tnt[:], rstd3s[:], am3s[:], op=ALU.mult)
                v.tensor_scalar(tnt[:], tnt[:], EPS, None, op0=ALU.max)
                v.reciprocal(g3s[:], tnt[:])
                v.tensor_scalar(tnt[:], g3s[:], 127.0, None, op0=ALU.mult)
                v.tensor_tensor(a3s[:], rstd3s[:], tnt[:], op=ALU.mult)
                v.tensor_tensor(b3s[:], m3s[:], a3s[:], op=ALU.mult)
                v.tensor_scalar(b3s[:], b3s[:], -1.0, 385.5, op0=ALU.mult, op1=ALU.add)
                v.reciprocal(g3s[:], tnt[:])
                i8 = v.tensor_scalar(g3s[:], g3s[:], sflv[0][:], None, op0=ALU.mult)
                inc("s_pre", i8)
                def dcorr(ta):
                    v.wait_ge(sem("s_r16d"), ta + 1)
                    v.tensor_tensor(bd[:], y0d[ta % 2][:], r16d[ta % 2][:],
                                    op=ALU.is_lt)
                    v.scalar_tensor_tensor(cd[:], y0d[ta % 2][:], 1.0,
                                           r16d[ta % 2][:],
                                           op0=ALU.subtract, op1=ALU.is_ge)
                    v.tensor_tensor(bd[:], cd[:], bd[:], op=ALU.subtract)
                    if ta >= 2:
                        v.wait_ge(sem(f"s_qdT{ta % 2}"), 16 * (ta // 2))
                        if dbgqd:
                            v.wait_ge(sem(f"s_dbg{ta % 2}"), 16 * (ta // 2))
                    iq = v.scalar_tensor_tensor(qd[ta % 2][:], r16d[ta % 2][:],
                                                385.0, bd[:],
                                                op0=ALU.subtract, op1=ALU.add)
                    inc("s_qd", iq)
                dcorr(0)
                for ta in range(NT2):
                    v.wait_ge(sem("s_gel"), HS * ta + HS)
                    v.tensor_reduce(amh[:], h_b[:], axis=AX.X,
                                    op=ALU.max, apply_absolute_value=True)
                    v.tensor_scalar(amh[:], amh[:], EPS, None, op0=ALU.max)
                    v.reciprocal(irc[:], amh[:])
                    i3 = v.tensor_scalar(sch_all[:, ta:ta + 1], irc[:], 127.0, None,
                                         op0=ALU.mult)
                    inc("s_amx", i3)
                    if ta + 1 < NT2:
                        dcorr(ta + 1)
                    v.wait_ge(sem("s_r16h"), ta + 1)
                    v.tensor_tensor(bh[:], y0h[ta % 2][:], r16h[ta % 2][:],
                                    op=ALU.is_lt)
                    v.scalar_tensor_tensor(ch[:], y0h[ta % 2][:], 1.0,
                                           r16h[ta % 2][:],
                                           op0=ALU.subtract, op1=ALU.is_ge)
                    v.tensor_tensor(bh[:], ch[:], bh[:], op=ALU.subtract)
                    if ta >= 2:
                        v.wait_ge(sem(f"s_qhd{ta % 2}"), 16 * (ta // 2))
                    iqh = v.scalar_tensor_tensor(qh_b[ta % 2][:], r16h[ta % 2][:],
                                                 385.0, bh[:],
                                                 op0=ALU.subtract, op1=ALU.add)
                    inc("s_qh", iqh)

            @blk.scalar
            def _(a):
                a = _Proxy(chains["a"], a)
                a.wait_ge(sem("s_pre"), 1)
                for it in range(NT2 + 1):
                    if it < NT2:
                        ta = it
                        a.wait_ge(sem(f"s_r2B{ta % 2}"), 16 * (ta // 2 + 1))
                        if ta >= 2:
                            a.wait_ge(sem("s_qd"), ta - 1)
                        iy = a.activation(y0d[ta % 2][:], r2B[ta % 2][:], AF.Identity,
                                          bias=b3s[:, ta:ta + 1],
                                          scale=a3s[:, ta:ta + 1])
                        inc("s_y0d", iy)
                        ir = a.activation(r16d[ta % 2][:], y0d[ta % 2][:], AF.Identity,
                                          bias=zerov[:], scale=1.0)
                        inc("s_r16d", ir)
                    if it >= 1:
                        tb = it - 1
                        for hs in range(HS):
                            q = HS * tb + hs
                            a.wait_ge(sem("s_h3"), q + 1)
                            ig = a.activation(h_b[:, 512 * hs:512 * (hs + 1)],
                                              ps_h[q % 6][:], AF.Gelu,
                                              bias=zerov[:], scale=g3s[:, tb:tb + 1])
                            inc("s_gel", ig)
                        a.wait_ge(sem("s_amx"), tb + 1)
                        if tb >= 2:
                            a.wait_ge(sem("s_qh"), tb - 1)
                        a.activation(y0h[tb % 2][:], h_b[:], AF.Identity,
                                     bias=halfv[:], scale=sch_all[:, tb:tb + 1])
                        irh = a.activation(r16h[tb % 2][:], y0h[tb % 2][:],
                                           AF.Identity, bias=zerov[:], scale=1.0)
                        inc("s_r16h", irh)

            @blk.sync
            def _(s):
                for ta in range(NT2):
                    s.wait_ge(sem("s_qd"), ta + 1)
                    if ta >= 2:
                        s.wait_ge(sem("s_h3"), HS * (ta - 1))
                    inc(f"s_qdT{ta % 2}", s.dma_start_transpose(
                        qdTv[ta % 2], qd[ta % 2][:]), 16)

            @blk.tensor
            def _(t):
                t.wait_ge(sem("s_w1T"), 16)
                for ta in range(NT2):
                    t.wait_ge(sem(f"s_qdT{ta % 2}"), 16 * (ta // 2 + 1))
                    for hs in range(HS):
                        q = HS * ta + hs
                        if q >= 6:
                            t.wait_ge(sem("s_gel"), q - 5)
                        last = None
                        for j in range(DT):
                            last = t.matmul(ps_h[q % 6][:],
                                            qdTv[ta % 2][:, j, :],
                                            w1v[:, j, 512 * hs:512 * (hs + 1)],
                                            start=(j == 0), stop=(j == DT - 1))
                        inc("s_h3", last)

        if "B2" not in phases:
            return nc
        with contextlib.ExitStack() as p2:
            def sb2(name, shape, dt):
                return p2.enter_context(nc.sbuf_tensor(name, shape, dt))
            w2T = sb2("w2T", [128, HT * D], BF16)
            qhT = [sb2(f"qhT{i}", [128, HT * 128], BF16) for i in range(2)]
            r2C = [sb2(f"r2C{i}", [128, D], F32) for i in range(2)]
            ffn = [sb2(f"ffn{i}", [128, D], F32) for i in range(2)]
            outb = [sb2(f"outb{i}", [128, D], F32) for i in range(2)]
            rstd4c = sb2("rstd4c", [128, 2], F32)
            bn64 = sb2("bn64", [128, 12], F32)
            bn24 = sb2("bn24", [128, 2], F32)
            vee4 = sb2("vee4", [128, 1], F32)
            lnv4 = sb2("lnv4", [128, 1], F32)
            bias4 = sb2("bias4", [128, 1], F32)
            t4 = sb2("t4", [128, D], F32)
            ps_f = [p2.enter_context(nc.psum_tensor(f"ps_f{i}", [128, D], F32))
                    for i in range(2)]
            blk = p2.enter_context(nc.Block())
            w2v = w2T[:].rearrange("p (i f) -> p i f", i=HT)
            qhTv = [qhT[i][:].rearrange("p (k t) -> p k t", k=HT) for i in range(2)]
            NT2 = NT

            @blk.gpsimd
            def _(g):
                g = _Proxy(chains["g"], g)
                inc("s_w2T", g.dma_start(w2T[:], w2t_d), 16)
                for ta in range(NT2):
                    if ta >= 2:
                        g.wait_ge(sem("s_outv"), ta - 1)
                    inc(f"s_r2C{ta % 2}", g.dma_start(
                        r2C[ta % 2][:], r2_d[128 * ta:128 * (ta + 1), :]), 16)
                    if ta >= 1:
                        tp = ta - 1
                        g.wait_ge(sem("s_outv"), tp + 1)
                        inc(f"s_outd{tp % 2}", g.dma_start(
                            y_d[128 * tp:128 * (tp + 1), :], outb[tp % 2][:]), 16)
                tp = NT2 - 1
                g.wait_ge(sem("s_outv"), tp + 1)
                inc(f"s_outd{tp % 2}", g.dma_start(
                    y_d[128 * tp:128 * (tp + 1), :], outb[tp % 2][:]), 16)
                g.wait_ge(sem("s_outd0"), 16 * ((NT2 + 1) // 2))
                g.wait_ge(sem("s_outd1"), 16 * (NT2 // 2))

            @blk.sync
            def _(s):
                for ta in range(NT2):
                    if ta >= 2:
                        s.wait_ge(sem("s_f4"), ta - 1)
                    inc(f"s_qhT{ta % 2}", s.dma_start_transpose(
                        qhTv[ta % 2], qh_d[128 * ta:128 * (ta + 1), :]), 16)

            @blk.vector
            def _(v):
                v = _Proxy(chains["v"], v)
                v.reciprocal(g4s[:], sch_all[:])
                i1 = v.tensor_scalar(g4s[:], g4s[:], sflv[1][:], None, op0=ALU.mult)
                inc("s_pre2", i1)
                for ta in range(NT2):
                    v.wait_ge(sem("s_ffn"), ta + 1)
                    v.bn_stats(bn64[:, 0:6], ffn[ta % 2][:, 0:512])
                    v.bn_stats(bn64[:, 6:12], ffn[ta % 2][:, 512:1024])
                    v.bn_aggr(bn24[:], bn64[:])
                    iv4 = v.tensor_scalar(vee4[:], bn24[:, 1:2], EPS, None, op0=ALU.add)
                    inc("s_st4", iv4)
                    v.wait_ge(sem("s_rs4"), ta + 1)
                    v.tensor_scalar(bias4[:], bn24[:, 0:1],
                                    rstd4c[:, ta % 2:ta % 2 + 1], -1.0,
                                    op0=ALU.mult, op1=ALU.mult)
                    v.tensor_scalar(t4[:], ffn[ta % 2][:],
                                    rstd4c[:, ta % 2:ta % 2 + 1], bias4[:],
                                    op0=ALU.mult, op1=ALU.add)
                    v.wait_ge(sem(f"s_r2C{ta % 2}"), 16 * (ta // 2 + 1))
                    if ta >= 2:
                        v.wait_ge(sem(f"s_outd{ta % 2}"), 16 * (ta // 2))
                    io = v.tensor_tensor(outb[ta % 2][:], t4[:], r2C[ta % 2][:],
                                         op=ALU.add)
                    inc("s_outv", io)

            @blk.scalar
            def _(a):
                a = _Proxy(chains["a"], a)
                a.wait_ge(sem("s_pre2"), 1)
                for ta in range(NT2):
                    a.wait_ge(sem("s_f4"), ta + 1)
                    iff = a.activation(ffn[ta % 2][:], ps_f[ta % 2][:], AF.Identity,
                                       bias=zerov[:], scale=g4s[:, ta:ta + 1])
                    inc("s_ffn", iff)
                    a.wait_ge(sem("s_st4"), ta + 1)
                    a.activation(lnv4[:], vee4[:], AF.Ln, bias=zerov[:], scale=1.0)
                    ie = a.activation(rstd4c[:, ta % 2:ta % 2 + 1], lnv4[:], AF.Exp,
                                      bias=zerov[:], scale=-0.5)
                    inc("s_rs4", ie)

            @blk.tensor
            def _(t):
                t.wait_ge(sem("s_w2T"), 16)
                for ta in range(NT2):
                    t.wait_ge(sem(f"s_qhT{ta % 2}"), 16 * (ta // 2 + 1))
                    if ta >= 2:
                        t.wait_ge(sem("s_ffn"), ta - 1)
                    last = None
                    for half in range(FH):
                        pslice = ps_f[ta % 2][:, 512 * half:512 * (half + 1)]
                        for j in range(HT):
                            last = t.matmul(pslice, qhTv[ta % 2][:, j, :],
                                            w2v[:, j, 512 * half:512 * (half + 1)],
                                            start=(j == 0), stop=(j == HT - 1))
                    inc("s_f4", last)

    return nc


_NC_CACHE = {}


def kernel(L0_attn_x, L0_attn_lambda_log, L0_attn_gamma, L0_attn_W_in,
           L0_attn_b_in, L0_attn_W_out, L0_attn_b_out, L0_ffn_w1, L0_ffn_w2):
    x = np.ascontiguousarray(np.asarray(L0_attn_x, dtype=np.float32))
    Bf, Lf, Df = x.shape
    nb = Bf // NCORES
    key = (nb, Lf, Df)
    if key not in _NC_CACHE:
        _NC_CACHE[key] = build_program(NB=nb, L=Lf, D=Df,
                                       H=np.asarray(L0_ffn_w1).shape[0])
    nc = _NC_CACHE[key]

    def row(a):
        return np.ascontiguousarray(np.asarray(a, dtype=np.float32)).reshape(1, -1)

    shared = {
        "w_in": np.ascontiguousarray(np.asarray(L0_attn_W_in, dtype=np.float32)),
        "w_out": np.ascontiguousarray(np.asarray(L0_attn_W_out, dtype=np.float32)),
        "b_in": row(L0_attn_b_in),
        "b_out": row(L0_attn_b_out),
        "gamma": row(L0_attn_gamma),
        "lamlog": row(L0_attn_lambda_log),
        "w1": np.ascontiguousarray(np.asarray(L0_ffn_w1, dtype=np.float32)),
        "w2": np.ascontiguousarray(np.asarray(L0_ffn_w2, dtype=np.float32)),
        "ident": np.eye(128, dtype=np.float32),
    }
    in_maps = []
    for c in range(NCORES):
        m = dict(shared)
        m["x"] = np.ascontiguousarray(
            x[c * nb:(c + 1) * nb].reshape(nb * Lf, Df))
        in_maps.append(m)
    res = run_bass_kernel_spmd(nc, in_maps, core_ids=list(range(NCORES)))
    out = np.empty((Bf, Lf, Df), dtype=np.float32)
    for c in range(NCORES):
        out[c * nb:(c + 1) * nb] = np.asarray(
            res.results[c]["y"], dtype=np.float32).reshape(nb, Lf, Df)
    return out


# version 21
# speedup vs baseline: 1.0441x; 1.0226x over previous
import contextlib
import numpy as np
import ml_dtypes
import concourse.bass as bass
import concourse.mybir as mybir
from concourse.bass_utils import run_bass_kernel_spmd

F32 = mybir.dt.float32
F32R = mybir.dt.float32r
BF16 = mybir.dt.bfloat16
I16 = mybir.dt.int16
ALU = mybir.AluOpType
AF = mybir.ActivationFunctionType
AX = mybir.AxisListType

NCORES = 8
EPS = 1e-5


def build_program(NB=4, L=2048, D=1024, H=4096, TA=256,
                  phases=("0", "A", "B1", "B2"), acap=None, dbgqd=False):
    DT = D // 128
    HT = H // 128
    R = NB * L
    NBLK = R // TA
    TPB = TA // 128
    NT = R // 128
    SEGB = L // TA
    HS = H // 512
    FH = D // 512

    nc = bass.Bass("TRN2", target_bir_lowering=False)

    x_d = nc.dram_tensor("x", [R, D], F32, kind="ExternalInput").ap()
    win_d = nc.dram_tensor("w_in", [D, D], F32, kind="ExternalInput").ap()
    wout_d = nc.dram_tensor("w_out", [D, D], F32, kind="ExternalInput").ap()
    bin_d = nc.dram_tensor("b_in", [1, D], F32, kind="ExternalInput").ap()
    bout_d = nc.dram_tensor("b_out", [1, D], F32, kind="ExternalInput").ap()
    gam_d = nc.dram_tensor("gamma", [1, D], F32, kind="ExternalInput").ap()
    laml_d = nc.dram_tensor("lamlog", [1, D], F32, kind="ExternalInput").ap()
    w1_d = nc.dram_tensor("w1", [H, D], F32, kind="ExternalInput").ap()
    w2_d = nc.dram_tensor("w2", [D, H], F32, kind="ExternalInput").ap()
    id_d = nc.dram_tensor("ident", [128, 128], F32, kind="ExternalInput").ap()
    y_d = nc.dram_tensor("y", [R, D], F32, kind="ExternalOutput").ap()
    r2_d = nc.dram_tensor("r2scr", [R, D], F32, kind="Internal").ap()
    qh_d = nc.dram_tensor("qhscr", [R, H], BF16, kind="Internal").ap()
    w1t_d = nc.dram_tensor("w1tscr", [128, DT * H], BF16, kind="Internal").ap()
    w2t_d = nc.dram_tensor("w2tscr", [128, HT * D], BF16, kind="Internal").ap()
    qd_d = nc.dram_tensor("qdscr", [R, D], BF16, kind="Internal").ap() if dbgqd else None
    scrq_d = nc.dram_tensor("scrq", [2, 128], F32, kind="Internal").ap()
    scrt_d = nc.dram_tensor("scrt", [2, 4], F32, kind="Internal").ap()

    es = contextlib.ExitStack()
    with es:
        def sb(name, shape, dt):
            return es.enter_context(nc.sbuf_tensor(name, shape, dt))

        zerov = sb("zerov", [128, 1], F32)
        halfv = sb("halfv", [128, 1], F32)
        epsv = sb("epsv", [128, 1], F32)
        ones_col = sb("ones_col", [128, 1], F32)
        ones_row = sb("ones_row", [1, 128], F32)
        thrv = [sb(f"thrv{w}", [128, 1], F32) for w in range(2)]
        nthrv = [sb(f"nthrv{w}", [128, 1], F32) for w in range(2)]
        sflv = [sb(f"sflv{w}", [128, 1], F32) for w in range(2)]
        rstd3s = sb("rstd3s", [128, NT], F32)
        m3s = sb("m3s", [128, NT], F32)
        am3s = sb("am3s", [128, NT], F32)
        sch_all = sb("sch_all", [128, NT], F32)
        a3s = sb("a3s", [128, NT], F32)
        b3s = sb("b3s", [128, NT], F32)
        g3s = sb("g3s", [128, NT], F32)
        g4s = sb("g4s", [128, NT], F32)
        p0A = es.enter_context(contextlib.ExitStack())
        def sb0A(name, shape, dt):
            return p0A.enter_context(nc.sbuf_tensor(name, shape, dt))
        ones_rr = sb0A("ones_rr", [1, 128], F32R)
        identt = sb0A("identt", [128, 128], F32)
        vecs3 = sb0A("vecs3", [128, 3 * DT], F32)
        lamx = sb0A("lamx", [128, DT], F32)
        gbv = sb0A("gbv", [128, DT], F32)
        bout_r = sb0A("bout_r", [1, D], F32R)
        winT = sb0A("winT", [128, DT * D], F32R)
        woutT = sb0A("woutT", [128, DT * D], F32R)

        sems = {}
        def S(n):
            sems[n] = [es.enter_context(nc.semaphore(n)), 0]
        for n in ["s0_row", "s0_wld0", "s0_wld1", "s0_vt", "s0_ve", "s0_lam", "s0_mis",
                  "s0_id", "s0_wtp", "s0_wte",
                  "s0_al0", "s0_al1", "s0_red", "s0_sum", "s0_ssc", "s0_bcm",
                  "s0_thr", "s0_bl0", "s0_bl1", "s0_tern", "s0_tp0", "s0_tp1", "s0_xb", "s0_sgn",
                  "s0_wst",
                  "sxA0", "sxA1", "s_st0", "s_rs0", "s_b0", "s_ln1", "s_t2",
                  "s_lt", "s_lte", "s_u",
                  "s_gu", "s_y", "s_a2", "s_attn", "s_st2", "s_rs2",
                  "s_r2v", "s_r2d0", "s_r2d1", "s_rs3", "s_st3",
                  "s_w1T", "s_r2B0", "s_r2B1", "s_pre", "s_y0d", "s_r16d",
                  "s_qd", "s_qdT0", "s_qdT1", "s_h3", "s_gel", "s_amx",
                  "s_r16h", "s_qh", "s_qhd0", "s_qhd1",
                  "s_w2T", "s_r2C0", "s_r2C1", "s_pre2", "s_qhT0", "s_qhT1",
                  "s_f4", "s_ffn", "s_st4", "s_rs4", "s_outv",
                  "s_outd0", "s_outd1", "s_dbg0", "s_dbg1"]:
            S(n)

        def sem(n):
            return sems[n][0]

        _CHAIN_OPS = {
            "tensor_scalar", "tensor_tensor", "tensor_copy", "tensor_reduce",
            "reduce_max", "reduce_sum", "bn_stats", "bn_aggr", "reciprocal",
            "tensor_tensor_scan", "scalar_tensor_tensor", "memset",
            "activation", "mul", "add", "copy", "sign", "sqrt", "square",
            "iota", "select",
        }
        _DMA_OPS = {"dma_start", "dma_start_transpose", "indirect_dma_start"}

        class _Chain:
            def __init__(self, name):
                self.h = es.enter_context(nc.semaphore(name))
                self.cnt = 0
                self.pending = None
                self.last = None

            def flush(self):
                if self.pending is not None:
                    self.pending.then_inc(self.h, 1)
                    self.cnt += 1
                    self.last = (self.h, self.cnt)
                    self.pending = None

            def pre(self, eng):
                self.flush()
                if self.last is not None:
                    eng.wait_ge(*self.last)

        class _Proxy:
            def __init__(self, chain, eng):
                self._c = chain
                self._e = eng

            def __getattr__(self, name):
                c, e = self._c, self._e
                if name in _CHAIN_OPS:
                    f = getattr(e, name)
                    def w(*a, **k):
                        c.pre(e)
                        ins = f(*a, **k)
                        c.pending = ins
                        return ins
                    return w
                if name in _DMA_OPS:
                    f = getattr(e, name)
                    def wd(*a, **k):
                        c.pre(e)
                        return f(*a, **k)
                    return wd
                return getattr(e, name)

        chains = {n: _Chain("ch_" + n) for n in ("v", "a", "g")}

        def inc(n, instr, k=1):
            instr.then_inc(sems[n][0], k)
            sems[n][1] += k
            for c in chains.values():
                if c.pending is instr:
                    c.pending = None
                    c.last = (sems[n][0], sems[n][1])

        with contextlib.ExitStack() as p0:
            def sb0(name, shape, dt):
                return p0.enter_context(nc.sbuf_tensor(name, shape, dt))
            rowbuf = sb0("rowbuf", [1, 3 * D], F32)
            wstg = [sb0(f"wstg{i}", [128, 4096], F32) for i in range(2)]
            ps_v = p0.enter_context(nc.psum_tensor("ps_v", [128, 3 * DT], F32))
            ps_t = [p0.enter_context(nc.psum_tensor(f"ps_t{i}", [128, 512], F32))
                    for i in range(2)]
            blk = p0.enter_context(nc.Block())

            def awchunk_ap(i):
                w, c = divmod(i, 2)
                src = win_d if w == 0 else wout_d
                return src[512 * c:512 * (c + 1), :].rearrange(
                    "(j p) d -> p j d", p=128)

            @blk.gpsimd
            def _(g):
                g = _Proxy(chains["g"], g)
                g.memset(zerov[:], 0.0)
                g.memset(halfv[:], 385.5)
                g.memset(epsv[:], EPS)
                g.memset(ones_col[:], 1.0)
                g.memset(ones_row[:], 1.0)
                inc("s0_row", g.dma_start(rowbuf[:, 0:D], bout_d), 16)
                g.wait_ge(sem("s0_ve"), 1)
                inc("s0_row", g.dma_start(rowbuf[:, 0:D], laml_d), 16)
                inc("s0_row", g.dma_start(rowbuf[:, D:2 * D], gam_d), 16)
                inc("s0_row", g.dma_start(rowbuf[:, 2 * D:3 * D], bin_d), 16)
                inc("s0_id", g.dma_start(identt[:], id_d), 16)
                for i in range(4):
                    if i >= 2:
                        g.wait_ge(sem("s0_wte"), 32 * (i - 1))
                    inc(f"s0_wld{i % 2}", g.dma_start(
                        wstg[i % 2][:].rearrange("p (j d) -> p j d", j=4),
                        awchunk_ap(i)), 16)

            @blk.tensor
            def _(t):
                t.wait_ge(sem("s0_row"), 64)
                mv = None
                for n in range(3 * DT):
                    mv = t.matmul(ps_v[:, n:n + 1], rowbuf[0:1, 128 * n:128 * (n + 1)],
                                  ones_col[0:1, 0:1], start=True, stop=True)
                inc("s0_vt", mv)
                t.wait_ge(sem("s0_id"), 16)
                ntp = 0
                for i in range(4):
                    t.wait_ge(sem(f"s0_wld{i % 2}"), 16 * (i // 2 + 1))
                    for j in range(4):
                        for dd in range(DT):
                            gidx = ntp
                            if gidx >= 8:
                                t.wait_ge(sem("s0_wte"), 4 * (gidx // 4 - 1))
                            mm = t.transpose(
                                ps_t[(gidx // 4) % 2][:, 128 * (gidx % 4):128 * (gidx % 4 + 1)],
                                wstg[i % 2][:, 1024 * j + 128 * dd:1024 * j + 128 * (dd + 1)],
                                identt[:])
                            if gidx % 4 == 3:
                                inc("s0_wtp", mm)
                            ntp += 1

            @blk.vector
            def _(v):
                v = _Proxy(chains["v"], v)
                v.wait_ge(sem("s0_row"), 16)
                v.tensor_copy(ones_rr[:], ones_row[:])
                ic = v.tensor_copy(bout_r[:], rowbuf[:, 0:D])
                inc("s0_ve", ic)
                v.wait_ge(sem("s0_vt"), 1)
                ic2 = v.tensor_copy(vecs3[:], ps_v[:])
                inc("s0_ve", ic2)
                wiv = winT[:].rearrange("p (i d) -> p i d", i=DT)
                wov = woutT[:].rearrange("p (i d) -> p i d", i=DT)
                ntp = 0
                for i in range(4):
                    w, c = divmod(i, 2)
                    for j in range(4):
                        etile = 4 * c + j
                        for dd in range(DT):
                            gidx = ntp
                            if gidx % 4 == 3:
                                grp = gidx // 4
                                v.wait_ge(sem("s0_wtp"), grp + 1)
                                dst = wiv if w == 0 else wov
                                iv_ = v.tensor_copy(
                                    dst[:, dd - 3:dd + 1, 128 * etile:128 * (etile + 1)],
                                    ps_t[grp % 2][:].rearrange("p (q r) -> p q r", q=4))
                                inc("s0_wte", iv_, 4)
                            ntp += 1
                v.wait_ge(sem("s0_lam"), 1)
                ig = v.tensor_tensor(gbv[:], vecs3[:, DT:2 * DT],
                                     vecs3[:, 2 * DT:3 * DT], op=ALU.mult)
                inc("s0_mis", ig)

            @blk.scalar
            def _(a):
                a = _Proxy(chains["a"], a)
                a.wait_ge(sem("s0_ve"), 2)
                ie = a.activation(lamx[:], vecs3[:, 0:DT], AF.Exp,
                                  bias=zerov[:], scale=1.0)
                inc("s0_lam", ie)

        if "A" not in phases:
            return nc
        NBLKA = NBLK if acap is None else min(NBLK, acap)
        GPB = TPB * DT // 4
        with contextlib.ExitStack() as pA:
            def sbA(name, shape, dt):
                return pA.enter_context(nc.sbuf_tensor(name, shape, dt))
            lam_bc = sbA("lam_bc", [128, DT * TA], F32)
            x_buf = [sbA(f"x_buf{i}", [128, TPB * D], F32) for i in range(2)]
            ln1f = [sbA(f"ln1f{i}", [128, TPB * D], F32) for i in range(2)]
            lnT = [sbA(f"lnT{i}", [128, DT * TA], F32R) for i in range(2)]
            gu_b = [sbA(f"gu{i}", [128, TA], F32) for i in range(2)]
            ybf = [sbA(f"ybf{i}", [128, DT * TA], F32R) for i in range(2)]
            attn_b = [sbA(f"attn_b{i}", [128, D], F32) for i in range(2)]
            t2buf = [sbA(f"t2buf{i}", [128, D], F32) for i in range(2)]
            r2b = [sbA(f"r2b{i}", [128, D], F32) for i in range(2)]
            rstd0s = [sbA(f"rstd0s{i}", [128, TPB], F32) for i in range(2)]
            bias0s = [sbA(f"bias0s{i}", [128, TPB], F32) for i in range(2)]
            rstd2c = [sbA(f"rstd2c{i}", [128, TPB], F32) for i in range(2)]
            bn6 = sbA("bn6", [128, 12], F32)
            bn2 = [sbA(f"bn2_{i}", [128, 2], F32) for i in range(2)]
            bn6a = sbA("bn6a", [128, 12], F32)
            bn2a = [sbA(f"bn2a_{i}", [128, 2], F32) for i in range(2)]
            bn6r = sbA("bn6r", [128, 12], F32)
            bn2r = [sbA(f"bn2r_{i}", [128, 2], F32) for i in range(2)]
            lnv = sbA("lnv", [128, 1], F32)
            nrstd = sbA("nrstd", [128, 1], F32)
            bias2 = sbA("bias2", [128, 1], F32)
            qstg = [sbA(f"qstg{i}", [128, 2048], F32) for i in range(2)]
            t1b = sbA("t1b", [128, 2048], BF16)
            t2b = [sbA(f"t2b{i}", [128, 2048], BF16) for i in range(2)]
            wtp = [sbA(f"wtp{i}", [128, 2048], BF16) for i in range(2)]
            qracc = sbA("qracc", [128, 1], F32)
            qrtmp = sbA("qrtmp", [128, 1], F32)
            qrow = sbA("qrow", [1, 128], F32)
            qssc = sbA("qssc", [1, 1], F32)
            qth = sbA("qth", [1, 4], F32)
            ps_u = [pA.enter_context(nc.psum_tensor(f"ps_u{i}", [128, TA], F32))
                    for i in range(2)]
            ps_a = [pA.enter_context(nc.psum_tensor(f"ps_a{i}", [128, D], F32))
                    for i in range(2)]
            ps_t2 = [pA.enter_context(nc.psum_tensor(f"ps_t2{i}", [128, 512], F32))
                     for i in range(2)]
            blk = pA.enter_context(nc.Block())

            def w1src(c):
                return w1_d[256 * c:256 * (c + 1), :].rearrange(
                    "(j p) d -> p j d", p=128)

            def w2src(c):
                rb, hf = divmod(c, 2)
                return w2_d[128 * rb:128 * (rb + 1), 2048 * hf:2048 * (hf + 1)]

            def qload_src(q):
                if q < 16:
                    return w1src(q), True
                if q < 32:
                    return w2src(q - 16), False
                if q < 48:
                    return w1src(q - 32), True
                return w2src(q - 48), False

            def pieces_through(c2):
                return 2 * (c2 + 1) if c2 < 16 else 32 + (c2 - 15)

            QPIECES = ([(c2, j) for c2 in range(16) for j in range(2)]
                       + [(c2, None) for c2 in range(16, 32)])
            QN_EL = H * D

            def xap(b):
                return x_d[b * TA:(b + 1) * TA, :].rearrange("(t p) d -> p t d", p=128)

            xv = [x_buf[i][:].rearrange("p (t d) -> p t d", t=TPB) for i in range(2)]
            lTv = [lnT[i][:].rearrange("p (k t) -> p k t", k=DT) for i in range(2)]
            ybv = [ybf[i][:].rearrange("p (k t) -> p k t", k=DT) for i in range(2)]
            wiv = winT[:].rearrange("p (i d) -> p i d", i=DT)
            wov = woutT[:].rearrange("p (i d) -> p i d", i=DT)
            lbv = lam_bc[:].rearrange("p (k t) -> p k t", k=DT)

            @blk.gpsimd
            def _(g):
                g = _Proxy(chains["g"], g)
                for b in range(NBLKA):
                    if b >= 2:
                        g.wait_ge(sem("s_r2v"), TPB * (b - 1))
                    inc(f"sxA{b % 2}", g.dma_start(xv[b % 2], xap(b)), 16)
                    if b >= 1:
                        for t in range(TPB):
                            ta = TPB * (b - 1) + t
                            g.wait_ge(sem("s_r2v"), ta + 1)
                            inc(f"s_r2d{ta % 2}", g.dma_start(
                                r2_d[128 * ta:128 * (ta + 1), :], r2b[ta % 2][:]), 16)
                for t in range(TPB):
                    ta = TPB * (NBLKA - 1) + t
                    g.wait_ge(sem("s_r2v"), ta + 1)
                    inc(f"s_r2d{ta % 2}", g.dma_start(
                        r2_d[128 * ta:128 * (ta + 1), :], r2b[ta % 2][:]), 16)
                g.wait_ge(sem("s_r2d0"), 16 * (TPB * NBLKA // 2))
                g.wait_ge(sem("s_r2d1"), 16 * (TPB * NBLKA // 2))
                g.wait_ge(sem("sxA0"), 16 * ((NBLKA + 1) // 2))
                g.wait_ge(sem("sxA1"), 16 * (NBLKA // 2))

            @blk.vector
            def _(v):
                v = _Proxy(chains["v"], v)

                def q_dve_tasks():
                    def reduce_chunk(i):
                        v.wait_ge(sem(f"s0_al{i % 2}"), 16 * (i // 2 + 1))
                        v.tensor_reduce(qrtmp[:], qstg[i % 2][:], axis=AX.X,
                                        op=ALU.add, apply_absolute_value=True)
                        ia = v.tensor_tensor(qracc[:], qracc[:], qrtmp[:],
                                             op=ALU.add)
                        inc("s0_red", ia)

                    def scale_dve(w):
                        v.wait_ge(sem("s0_sum"), 32 * (w + 1))
                        if w == 1:
                            v.wait_ge(sem("s0_bcm"), 16)
                        v.tensor_reduce(qssc[:], qrow[:], axis=AX.X, op=ALU.add)
                        v.tensor_scalar(qssc[:], qssc[:], 1.0 / QN_EL, EPS,
                                        op0=ALU.mult, op1=ALU.max)
                        v.tensor_scalar(qth[:, 0:1], qssc[:], 0.5, None,
                                        op0=ALU.mult)
                        v.tensor_scalar(qth[:, 1:2], qth[:, 0:1], -1.0,
                                        None, op0=ALU.mult)
                        iq = v.tensor_scalar(qth[:, 2:3], qssc[:], -1.0,
                                             None, op0=ALU.mult)
                        inc("s0_ssc", iq)

                    yield lambda: v.memset(qracc[:], 0.0)
                    for i in range(16):
                        yield lambda i=i: reduce_chunk(i)

                    def switch():
                        v.wait_ge(sem("s0_sum"), 16)
                        v.memset(qracc[:], 0.0)
                    yield switch
                    yield lambda: scale_dve(0)
                    for i in range(16, 32):
                        yield lambda i=i: reduce_chunk(i)
                    for c2 in range(32):
                        if c2 == 13:
                            yield lambda: scale_dve(1)
                        def tern(c2=c2):
                            q = 32 + c2
                            w = 0 if c2 < 16 else 1
                            v.wait_ge(sem(f"s0_al{q % 2}"), 16 * (q // 2 + 1))
                            v.wait_ge(sem("s0_thr"), 48 * (w + 1))
                            v.tensor_scalar(t1b[:], qstg[q % 2][:], thrv[w][:],
                                            None, op0=ALU.is_ge)
                            if c2 >= 2:
                                v.wait_ge(sem("s0_xb"),
                                          16 * pieces_through(c2 - 2))
                            it = v.scalar_tensor_tensor(
                                t2b[c2 % 2][:], qstg[q % 2][:], nthrv[w][:],
                                t1b[:], op0=ALU.is_le, op1=ALU.subtract)
                            inc("s0_tern", it)
                        yield tern

                dveq = q_dve_tasks()

                def dve_qslot(n):
                    for _ in range(n):
                        t = next(dveq, None)
                        if t is None:
                            return
                        t()

                v.memset(lam_bc[:], 1.0)
                for k in range(DT):
                    v.tensor_scalar(lbv[:, k, :], lbv[:, k, :], lamx[:, k:k + 1],
                                    None, op0=ALU.mult)
                def scans_sec(b):
                    seg = b % SEGB
                    for k in range(DT):
                        gcnt = DT * b + k
                        v.wait_ge(sem("s_gu"), gcnt + 1)
                        if b >= 2:
                            v.wait_ge(sem("s_a2"), TPB * (b - 1))
                        init = 0.0 if seg == 0 else ybv[(b - 1) % 2][:, k, TA - 1:TA]
                        isc = v.tensor_tensor_scan(ybv[b % 2][:, k, :], lbv[:, k, :],
                                                   gu_b[gcnt % 2][:], init,
                                                   op0=ALU.mult, op1=ALU.add)
                        inc("s_y", isc)

                def attnp(b):
                    for t in range(TPB):
                        ta = TPB * b + t
                        v.wait_ge(sem("s_a2"), ta + 1)
                        if ta >= 2:
                            v.wait_ge(sem("s_t2"), ta - 1)
                        iat = v.tensor_tensor(attn_b[ta % 2][:],
                                              ln1f[b % 2][:, t * D:(t + 1) * D],
                                              ps_a[ta % 2][:], op=ALU.add)
                        inc("s_attn", iat)
                        v.bn_stats(bn6a[:, 0:6], attn_b[ta % 2][:, 0:512])
                        v.bn_stats(bn6a[:, 6:12], attn_b[ta % 2][:, 512:1024])
                        ia2 = v.bn_aggr(bn2a[ta % 2][:], bn6a[:])
                        inc("s_st2", ia2)

                def r2p(b):
                    for t in range(TPB):
                        ta = TPB * b + t
                        v.wait_ge(sem("s_t2"), ta + 1)
                        if ta >= 2:
                            v.wait_ge(sem(f"s_r2d{ta % 2}"), 16 * (ta // 2))
                        ir2 = v.tensor_tensor(r2b[ta % 2][:], t2buf[ta % 2][:],
                                              xv[b % 2][:, t, :], op=ALU.add)
                        inc("s_r2v", ir2)
                        v.bn_stats(bn6r[:, 0:6], r2b[ta % 2][:, 0:512])
                        v.bn_stats(bn6r[:, 6:12], r2b[ta % 2][:, 512:1024])
                        if ta >= 2:
                            v.wait_ge(sem("s_rs3"), ta - 1)
                        ia3 = v.bn_aggr(bn2r[ta % 2][:], bn6r[:])
                        inc("s_st3", ia3)
                        v.tensor_scalar(t2buf[ta % 2][:], r2b[ta % 2][:],
                                        bn2r[ta % 2][:, 0:1],
                                        None, op0=ALU.subtract)
                        iam = v.tensor_reduce(am3s[:, ta:ta + 1], t2buf[ta % 2][:],
                                              axis=AX.X,
                                              op=ALU.max, apply_absolute_value=True)
                        inc("s_rs2", iam)

                for b in range(NBLKA):
                    seg = b % SEGB
                    for t in range(TPB):
                        ta = TPB * b + t
                        v.wait_ge(sem(f"sxA{b % 2}"), 16 * (b // 2 + 1))
                        if ta >= 2:
                            v.wait_ge(sem("s_ln1"), ta - 1)
                        v.bn_stats(bn6[:, 0:6], x_buf[b % 2][:, t * D:t * D + 512])
                        v.bn_stats(bn6[:, 6:12], x_buf[b % 2][:, t * D + 512:t * D + 1024])
                        ia0 = v.bn_aggr(bn2[ta % 2][:], bn6[:])
                        inc("s_st0", ia0)
                    if b >= 1:
                        attnp(b - 1)
                    for gi in range(GPB):
                        grp = GPB * b + gi
                        tt = gi // (DT // 4)
                        q = gi % (DT // 4)
                        v.wait_ge(sem("s_lt"), grp + 1)
                        if b >= 2:
                            v.wait_ge(sem("s_u"), DT * (b - 1))
                        ie = v.tensor_copy(
                            lTv[b % 2][:, 4 * q:4 * q + 4, 128 * tt:128 * (tt + 1)],
                            ps_t2[grp % 2][:].rearrange("p (q r) -> p q r", q=4))
                        inc("s_lte", ie)
                    if b >= 1:
                        r2p(b - 1)
                    scans_sec(b)
                    dve_qslot(3)
                attnp(NBLKA - 1)
                r2p(NBLKA - 1)
                dve_qslot(100)

            @blk.scalar
            def _(a):
                a = _Proxy(chains["a"], a)

                def ln2p(b):
                    for t in range(TPB):
                        ta = TPB * b + t
                        a.wait_ge(sem("s_st2"), ta + 1)
                        a.activation(lnv[:], bn2a[ta % 2][:, 1:2], AF.Ln,
                                     bias=epsv[:], scale=1.0)
                        a.activation(rstd2c[b % 2][:, t:t + 1], lnv[:], AF.Exp,
                                     bias=zerov[:], scale=-0.5)
                        a.activation(nrstd[:], rstd2c[b % 2][:, t:t + 1], AF.Identity,
                                     bias=zerov[:], scale=-1.0)
                        a.activation(bias2[:], bn2a[ta % 2][:, 0:1], AF.Identity,
                                     bias=zerov[:], scale=nrstd[:])
                        if ta >= 2:
                            a.wait_ge(sem("s_rs2"), ta - 1)
                        it2 = a.activation(t2buf[ta % 2][:], attn_b[ta % 2][:],
                                           AF.Identity, bias=bias2[:],
                                           scale=rstd2c[b % 2][:, t:t + 1])
                        inc("s_t2", it2)

                def ln3p(b):
                    for t in range(TPB):
                        ta = TPB * b + t
                        a.wait_ge(sem("s_st3"), ta + 1)
                        a.activation(lnv[:], bn2r[ta % 2][:, 1:2], AF.Ln,
                                     bias=epsv[:], scale=1.0)
                        a.activation(rstd3s[:, ta:ta + 1], lnv[:], AF.Exp,
                                     bias=zerov[:], scale=-0.5)
                        im3 = a.activation(m3s[:, ta:ta + 1], bn2r[ta % 2][:, 0:1],
                                           AF.Identity, bias=zerov[:], scale=1.0)
                        inc("s_rs3", im3)

                for b in range(NBLKA):
                    for t in range(TPB):
                        ta = TPB * b + t
                        a.wait_ge(sem("s_st0"), ta + 1)
                        a.activation(lnv[:], bn2[ta % 2][:, 1:2], AF.Ln,
                                     bias=epsv[:], scale=1.0)
                        a.activation(rstd0s[b % 2][:, t:t + 1], lnv[:], AF.Exp,
                                     bias=zerov[:], scale=-0.5)
                        a.activation(nrstd[:], rstd0s[b % 2][:, t:t + 1], AF.Identity,
                                     bias=zerov[:], scale=-1.0)
                        a.activation(bias0s[b % 2][:, t:t + 1], bn2[ta % 2][:, 0:1],
                                     AF.Identity, bias=zerov[:], scale=nrstd[:])
                        if b >= 2:
                            a.wait_ge(sem("s_lt"), GPB * (b - 1))
                            a.wait_ge(sem("s_attn"), TPB * (b - 1))
                        i1 = a.activation(ln1f[b % 2][:, t * D:(t + 1) * D],
                                          xv[b % 2][:, t, :], AF.Identity,
                                          bias=bias0s[b % 2][:, t:t + 1],
                                          scale=rstd0s[b % 2][:, t:t + 1])
                        inc("s_ln1", i1)
                    if b >= 1:
                        ln2p(b - 1)
                    for k in range(DT):
                        gcnt = DT * b + k
                        a.wait_ge(sem("s_u"), gcnt + 1)
                        if gcnt >= 2:
                            a.wait_ge(sem("s_y"), gcnt - 1)
                        ig = a.activation(gu_b[gcnt % 2][:], ps_u[gcnt % 2][:],
                                          AF.Identity, bias=gbv[:, k:k + 1],
                                          scale=vecs3[:, DT + k:DT + k + 1])
                        inc("s_gu", ig)
                    if b >= 1:
                        ln3p(b - 1)
                ln2p(NBLKA - 1)
                ln3p(NBLKA - 1)

            @blk.sync
            def _(s):
                w1tv = w1t_d.rearrange("p (i h) -> p i h", i=DT)
                w2tv = w2t_d.rearrange("p (i f) -> p i f", i=HT)
                pctr = [0]

                def emit_pieces(c2):
                    npiece = 2 if c2 < 16 else 1
                    for j in (range(2) if c2 < 16 else [None]):
                        p = pctr[0]
                        pctr[0] += 1
                        s.wait_ge(sem("s0_tern"), c2 + 1)
                        if p >= 2:
                            s.wait_ge(sem(f"s0_bl{p % 2}"), 16 * (p // 2))
                        if j is None:
                            wv = wtp[p % 2][:].rearrange(
                                "p (i r) -> p i r", i=16)
                            xin = t2b[c2 % 2][:]
                        else:
                            wv = wtp[p % 2][:, 0:1024].rearrange(
                                "p (i r) -> p i r", i=DT)
                            xin = t2b[c2 % 2][:, 1024 * j:1024 * (j + 1)]
                        inc("s0_xb", s.dma_start_transpose(wv, xin), 16)
                        s.wait_ge(sem("s0_xb"), 16 * (p + 1))
                        if j is None:
                            c = c2 - 16
                            rb, hf = divmod(c, 2)
                            dstp = w2tv[:, 16 * hf:16 * (hf + 1),
                                        128 * rb:128 * (rb + 1)]
                        else:
                            hrow = 2 * c2 + j
                            dstp = w1tv[:, :, 128 * hrow:128 * (hrow + 1)]
                        inc(f"s0_bl{p % 2}", s.dma_start(dstp, wv), 16)

                for q in range(64):
                    if q == 18:
                        s.wait_ge(sem("s0_red"), 16)
                        inc("s0_sum", s.dma_start(scrq_d[0:1, :], qracc[:]), 16)
                        s.wait_ge(sem("s0_sum"), 16)
                        inc("s0_sum", s.dma_start(qrow[:], scrq_d[0:1, :]), 16)
                    if q == 20:
                        s.wait_ge(sem("s0_ssc"), 1)
                        inc("s0_bcm", s.dma_start(scrt_d[0:1, 0:3], qth[:, 0:3]), 16)
                        s.wait_ge(sem("s0_bcm"), 16)
                        inc("s0_thr", s.dma_start(
                            thrv[0][:],
                            scrt_d[0:1, 0:1].broadcast_to((128, 1))), 16)
                        inc("s0_thr", s.dma_start(
                            nthrv[0][:],
                            scrt_d[0:1, 1:2].broadcast_to((128, 1))), 16)
                        inc("s0_thr", s.dma_start(
                            sflv[0][:],
                            scrt_d[0:1, 2:3].broadcast_to((128, 1))), 16)
                    if q == 44:
                        s.wait_ge(sem("s0_red"), 32)
                        inc("s0_sum", s.dma_start(scrq_d[1:2, :], qracc[:]), 16)
                        s.wait_ge(sem("s0_sum"), 48)
                        inc("s0_sum", s.dma_start(qrow[:], scrq_d[1:2, :]), 16)
                    if q == 46:
                        s.wait_ge(sem("s0_ssc"), 2)
                        inc("s0_bcm", s.dma_start(scrt_d[1:2, 0:3], qth[:, 0:3]), 16)
                        s.wait_ge(sem("s0_bcm"), 32)
                        inc("s0_thr", s.dma_start(
                            thrv[1][:],
                            scrt_d[1:2, 0:1].broadcast_to((128, 1))), 16)
                        inc("s0_thr", s.dma_start(
                            nthrv[1][:],
                            scrt_d[1:2, 1:2].broadcast_to((128, 1))), 16)
                        inc("s0_thr", s.dma_start(
                            sflv[1][:],
                            scrt_d[1:2, 2:3].broadcast_to((128, 1))), 16)
                    if q >= 2:
                        if q - 2 < 32:
                            s.wait_ge(sem("s0_red"), q - 1)
                        else:
                            s.wait_ge(sem("s0_tern"), q - 33)
                    src, isw1 = qload_src(q)
                    dst = (qstg[q % 2][:].rearrange("p (j d) -> p j d", j=2)
                           if isw1 else qstg[q % 2][:])
                    inc(f"s0_al{q % 2}", s.dma_start(dst, src), 16)
                    if q >= 34:
                        emit_pieces(q - 34)
                emit_pieces(30)
                emit_pieces(31)
                s.wait_ge(sem("s0_bl0"), 16 * 24)
                s.wait_ge(sem("s0_bl1"), 16 * 24)
                s.wait_ge(sem("s0_al0"), 16 * 32)
                s.wait_ge(sem("s0_al1"), 16 * 32)

            @blk.tensor
            def _(t):
                def mm2sec(b):
                    t.wait_ge(sem("s_y"), DT * (b + 1))
                    for tt in range(TPB):
                        ta = TPB * b + tt
                        if ta >= 2:
                            t.wait_ge(sem("s_attn"), ta - 1)
                        last = None
                        for half in range(FH):
                            pslice = ps_a[ta % 2][:, 512 * half:512 * (half + 1)]
                            t.matmul(pslice, ones_rr[:],
                                     bout_r[0:1, 512 * half:512 * (half + 1)],
                                     start=True, stop=False)
                            for k in range(DT):
                                last = t.matmul(
                                    pslice,
                                    ybv[b % 2][:, k, 128 * tt:128 * (tt + 1)],
                                    wov[:, k, 512 * half:512 * (half + 1)],
                                    start=False, stop=(k == DT - 1))
                        inc("s_a2", last)

                for b in range(NBLKA):
                    if b >= 1:
                        mm2sec(b - 1)
                    for gi in range(GPB):
                        grp = GPB * b + gi
                        tt = gi // (DT // 4)
                        q = gi % (DT // 4)
                        ta = TPB * b + tt
                        t.wait_ge(sem("s_ln1"), ta + 1)
                        if grp >= 2:
                            t.wait_ge(sem("s_lte"), grp - 1)
                        mm = None
                        for r in range(4):
                            dd = 4 * q + r
                            mm = t.transpose(
                                ps_t2[grp % 2][:, 128 * r:128 * (r + 1)],
                                ln1f[b % 2][:, tt * D + 128 * dd:tt * D + 128 * (dd + 1)],
                                identt[:])
                        inc("s_lt", mm)
                    t.wait_ge(sem("s_lte"), GPB * (b + 1))
                    for k in range(DT):
                        gcnt = DT * b + k
                        if gcnt >= 2:
                            t.wait_ge(sem("s_gu"), gcnt - 1)
                        last = None
                        for j in range(DT):
                            last = t.matmul(ps_u[gcnt % 2][:],
                                            wiv[:, j, 128 * k:128 * (k + 1)],
                                            lTv[b % 2][:, j, :],
                                            start=(j == 0), stop=(j == DT - 1))
                        inc("s_u", last)
                mm2sec(NBLKA - 1)

        p0A.close()

        if "B1" not in phases:
            return nc
        with contextlib.ExitStack() as p1:
            def sb1(name, shape, dt):
                return p1.enter_context(nc.sbuf_tensor(name, shape, dt))
            w1T = sb1("w1T", [128, DT * H], BF16)
            r2B = [sb1(f"r2B{i}", [128, D], F32) for i in range(2)]
            y0d = [sb1(f"y0d{i}", [128, D], F32) for i in range(2)]
            r16d = [sb1(f"r16d{i}", [128, D], I16) for i in range(2)]
            bd = sb1("bd", [128, D], BF16)
            cd = sb1("cd", [128, D], BF16)
            qd = [sb1(f"qd{i}", [128, D], BF16) for i in range(2)]
            qdT = [sb1(f"qdT{i}", [128, DT * 128], BF16) for i in range(2)]
            h_b = sb1("h_b", [128, H], F32)
            y0h = [sb1(f"y0h{i}", [128, H], F32) for i in range(2)]
            r16h = [sb1(f"r16h{i}", [128, H], I16) for i in range(2)]
            bh = sb1("bh", [128, H], BF16)
            ch = sb1("ch", [128, H], BF16)
            qh_b = [sb1(f"qh_b{i}", [128, H], BF16) for i in range(2)]
            amh = sb1("amh", [128, 1], F32)
            irc = sb1("irc", [128, 1], F32)
            tnt = sb1("tnt", [128, NT], F32)
            ps_h = [p1.enter_context(nc.psum_tensor(f"ps_h{i}", [128, 512], F32))
                    for i in range(6)]
            blk = p1.enter_context(nc.Block())
            w1v = w1T[:].rearrange("p (i h) -> p i h", i=DT)
            qdTv = [qdT[i][:].rearrange("p (k t) -> p k t", k=DT) for i in range(2)]
            NT2 = NT

            @blk.gpsimd
            def _(g):
                g = _Proxy(chains["g"], g)
                inc("s_w1T", g.dma_start(w1T[:], w1t_d), 16)
                for ta in range(min(2, NT2)):
                    inc(f"s_r2B{ta % 2}", g.dma_start(
                        r2B[ta % 2][:], r2_d[128 * ta:128 * (ta + 1), :]), 16)
                for ta in range(NT2):
                    if ta + 2 < NT2:
                        g.wait_ge(sem("s_y0d"), ta + 1)
                        inc(f"s_r2B{ta % 2}", g.dma_start(
                            r2B[ta % 2][:], r2_d[128 * (ta + 2):128 * (ta + 3), :]),
                            16)
                    if dbgqd:
                        g.wait_ge(sem("s_qd"), ta + 1)
                        inc(f"s_dbg{ta % 2}", g.dma_start(
                            qd_d[128 * ta:128 * (ta + 1), :], qd[ta % 2][:]), 16)
                    g.wait_ge(sem("s_qh"), ta + 1)
                    inc(f"s_qhd{ta % 2}", g.dma_start(
                        qh_d[128 * ta:128 * (ta + 1), :], qh_b[ta % 2][:]), 16)
                g.wait_ge(sem("s_qhd0"), 16 * ((NT2 + 1) // 2))
                g.wait_ge(sem("s_qhd1"), 16 * (NT2 // 2))
                g.wait_ge(sem("s_r2B0"), 16 * ((NT2 + 1) // 2))
                g.wait_ge(sem("s_r2B1"), 16 * (NT2 // 2))
                if dbgqd:
                    g.wait_ge(sem("s_dbg0"), 16 * ((NT2 + 1) // 2))
                    g.wait_ge(sem("s_dbg1"), 16 * (NT2 // 2))

            @blk.vector
            def _(v):
                v = _Proxy(chains["v"], v)
                v.tensor_tensor(tnt[:], rstd3s[:], am3s[:], op=ALU.mult)
                v.tensor_scalar(tnt[:], tnt[:], EPS, None, op0=ALU.max)
                v.reciprocal(g3s[:], tnt[:])
                v.tensor_scalar(tnt[:], g3s[:], 127.0, None, op0=ALU.mult)
                v.tensor_tensor(a3s[:], rstd3s[:], tnt[:], op=ALU.mult)
                v.tensor_tensor(b3s[:], m3s[:], a3s[:], op=ALU.mult)
                v.tensor_scalar(b3s[:], b3s[:], -1.0, 385.5, op0=ALU.mult, op1=ALU.add)
                v.reciprocal(g3s[:], tnt[:])
                i8 = v.tensor_scalar(g3s[:], g3s[:], sflv[0][:], None, op0=ALU.mult)
                inc("s_pre", i8)
                def dcorr(ta):
                    v.wait_ge(sem("s_r16d"), ta + 1)
                    v.tensor_tensor(bd[:], y0d[ta % 2][:], r16d[ta % 2][:],
                                    op=ALU.is_lt)
                    v.scalar_tensor_tensor(cd[:], y0d[ta % 2][:], 1.0,
                                           r16d[ta % 2][:],
                                           op0=ALU.subtract, op1=ALU.is_ge)
                    v.tensor_tensor(bd[:], cd[:], bd[:], op=ALU.subtract)
                    if ta >= 2:
                        v.wait_ge(sem(f"s_qdT{ta % 2}"), 16 * (ta // 2))
                        if dbgqd:
                            v.wait_ge(sem(f"s_dbg{ta % 2}"), 16 * (ta // 2))
                    iq = v.scalar_tensor_tensor(qd[ta % 2][:], r16d[ta % 2][:],
                                                385.0, bd[:],
                                                op0=ALU.subtract, op1=ALU.add)
                    inc("s_qd", iq)
                dcorr(0)
                for ta in range(NT2):
                    v.wait_ge(sem("s_gel"), HS * ta + HS)
                    v.tensor_reduce(amh[:], h_b[:], axis=AX.X,
                                    op=ALU.max, apply_absolute_value=True)
                    v.tensor_scalar(amh[:], amh[:], EPS, None, op0=ALU.max)
                    v.reciprocal(irc[:], amh[:])
                    i3 = v.tensor_scalar(sch_all[:, ta:ta + 1], irc[:], 127.0, None,
                                         op0=ALU.mult)
                    inc("s_amx", i3)
                    if ta + 1 < NT2:
                        dcorr(ta + 1)
                    v.wait_ge(sem("s_r16h"), ta + 1)
                    v.tensor_tensor(bh[:], y0h[ta % 2][:], r16h[ta % 2][:],
                                    op=ALU.is_lt)
                    v.scalar_tensor_tensor(ch[:], y0h[ta % 2][:], 1.0,
                                           r16h[ta % 2][:],
                                           op0=ALU.subtract, op1=ALU.is_ge)
                    v.tensor_tensor(bh[:], ch[:], bh[:], op=ALU.subtract)
                    if ta >= 2:
                        v.wait_ge(sem(f"s_qhd{ta % 2}"), 16 * (ta // 2))
                    iqh = v.scalar_tensor_tensor(qh_b[ta % 2][:], r16h[ta % 2][:],
                                                 385.0, bh[:],
                                                 op0=ALU.subtract, op1=ALU.add)
                    inc("s_qh", iqh)

            @blk.scalar
            def _(a):
                a = _Proxy(chains["a"], a)
                a.wait_ge(sem("s_pre"), 1)
                for it in range(NT2 + 1):
                    if it < NT2:
                        ta = it
                        a.wait_ge(sem(f"s_r2B{ta % 2}"), 16 * (ta // 2 + 1))
                        if ta >= 2:
                            a.wait_ge(sem("s_qd"), ta - 1)
                        iy = a.activation(y0d[ta % 2][:], r2B[ta % 2][:], AF.Identity,
                                          bias=b3s[:, ta:ta + 1],
                                          scale=a3s[:, ta:ta + 1])
                        inc("s_y0d", iy)
                        ir = a.activation(r16d[ta % 2][:], y0d[ta % 2][:], AF.Identity,
                                          bias=zerov[:], scale=1.0)
                        inc("s_r16d", ir)
                    if it >= 1:
                        tb = it - 1
                        for hs in range(HS):
                            q = HS * tb + hs
                            a.wait_ge(sem("s_h3"), q + 1)
                            ig = a.activation(h_b[:, 512 * hs:512 * (hs + 1)],
                                              ps_h[q % 6][:], AF.Gelu,
                                              bias=zerov[:], scale=g3s[:, tb:tb + 1])
                            inc("s_gel", ig)
                        a.wait_ge(sem("s_amx"), tb + 1)
                        if tb >= 2:
                            a.wait_ge(sem("s_qh"), tb - 1)
                        a.activation(y0h[tb % 2][:], h_b[:], AF.Identity,
                                     bias=halfv[:], scale=sch_all[:, tb:tb + 1])
                        irh = a.activation(r16h[tb % 2][:], y0h[tb % 2][:],
                                           AF.Identity, bias=zerov[:], scale=1.0)
                        inc("s_r16h", irh)

            @blk.sync
            def _(s):
                for ta in range(NT2):
                    s.wait_ge(sem("s_qd"), ta + 1)
                    if ta >= 2:
                        s.wait_ge(sem("s_h3"), HS * (ta - 1))
                    inc(f"s_qdT{ta % 2}", s.dma_start_transpose(
                        qdTv[ta % 2], qd[ta % 2][:]), 16)

            @blk.tensor
            def _(t):
                t.wait_ge(sem("s_w1T"), 16)
                for ta in range(NT2):
                    t.wait_ge(sem(f"s_qdT{ta % 2}"), 16 * (ta // 2 + 1))
                    for hs in range(HS):
                        q = HS * ta + hs
                        if q >= 6:
                            t.wait_ge(sem("s_gel"), q - 5)
                        last = None
                        for j in range(DT):
                            last = t.matmul(ps_h[q % 6][:],
                                            qdTv[ta % 2][:, j, :],
                                            w1v[:, j, 512 * hs:512 * (hs + 1)],
                                            start=(j == 0), stop=(j == DT - 1))
                        inc("s_h3", last)

        if "B2" not in phases:
            return nc
        with contextlib.ExitStack() as p2:
            def sb2(name, shape, dt):
                return p2.enter_context(nc.sbuf_tensor(name, shape, dt))
            w2T = sb2("w2T", [128, HT * D], BF16)
            qhT = [sb2(f"qhT{i}", [128, HT * 128], BF16) for i in range(2)]
            r2C = [sb2(f"r2C{i}", [128, D], F32) for i in range(2)]
            ffn = [sb2(f"ffn{i}", [128, D], F32) for i in range(2)]
            outb = [sb2(f"outb{i}", [128, D], F32) for i in range(2)]
            rstd4c = sb2("rstd4c", [128, 2], F32)
            bn64 = sb2("bn64", [128, 12], F32)
            bn24 = sb2("bn24", [128, 2], F32)
            vee4 = sb2("vee4", [128, 1], F32)
            lnv4 = sb2("lnv4", [128, 1], F32)
            bias4 = sb2("bias4", [128, 1], F32)
            t4 = sb2("t4", [128, D], F32)
            ps_f = [p2.enter_context(nc.psum_tensor(f"ps_f{i}", [128, D], F32))
                    for i in range(2)]
            blk = p2.enter_context(nc.Block())
            w2v = w2T[:].rearrange("p (i f) -> p i f", i=HT)
            qhTv = [qhT[i][:].rearrange("p (k t) -> p k t", k=HT) for i in range(2)]
            NT2 = NT

            @blk.gpsimd
            def _(g):
                g = _Proxy(chains["g"], g)
                inc("s_w2T", g.dma_start(w2T[:], w2t_d), 16)
                for ta in range(NT2):
                    if ta >= 2:
                        g.wait_ge(sem("s_outv"), ta - 1)
                    inc(f"s_r2C{ta % 2}", g.dma_start(
                        r2C[ta % 2][:], r2_d[128 * ta:128 * (ta + 1), :]), 16)
                    if ta >= 1:
                        tp = ta - 1
                        g.wait_ge(sem("s_outv"), tp + 1)
                        inc(f"s_outd{tp % 2}", g.dma_start(
                            y_d[128 * tp:128 * (tp + 1), :], outb[tp % 2][:]), 16)
                tp = NT2 - 1
                g.wait_ge(sem("s_outv"), tp + 1)
                inc(f"s_outd{tp % 2}", g.dma_start(
                    y_d[128 * tp:128 * (tp + 1), :], outb[tp % 2][:]), 16)
                g.wait_ge(sem("s_outd0"), 16 * ((NT2 + 1) // 2))
                g.wait_ge(sem("s_outd1"), 16 * (NT2 // 2))

            @blk.sync
            def _(s):
                for ta in range(NT2):
                    if ta >= 2:
                        s.wait_ge(sem("s_f4"), ta - 1)
                    inc(f"s_qhT{ta % 2}", s.dma_start_transpose(
                        qhTv[ta % 2], qh_d[128 * ta:128 * (ta + 1), :]), 16)

            @blk.vector
            def _(v):
                v = _Proxy(chains["v"], v)
                v.reciprocal(g4s[:], sch_all[:])
                i1 = v.tensor_scalar(g4s[:], g4s[:], sflv[1][:], None, op0=ALU.mult)
                inc("s_pre2", i1)
                for ta in range(NT2):
                    v.wait_ge(sem("s_ffn"), ta + 1)
                    v.bn_stats(bn64[:, 0:6], ffn[ta % 2][:, 0:512])
                    v.bn_stats(bn64[:, 6:12], ffn[ta % 2][:, 512:1024])
                    v.bn_aggr(bn24[:], bn64[:])
                    iv4 = v.tensor_scalar(vee4[:], bn24[:, 1:2], EPS, None, op0=ALU.add)
                    inc("s_st4", iv4)
                    v.wait_ge(sem("s_rs4"), ta + 1)
                    v.tensor_scalar(bias4[:], bn24[:, 0:1],
                                    rstd4c[:, ta % 2:ta % 2 + 1], -1.0,
                                    op0=ALU.mult, op1=ALU.mult)
                    v.tensor_scalar(t4[:], ffn[ta % 2][:],
                                    rstd4c[:, ta % 2:ta % 2 + 1], bias4[:],
                                    op0=ALU.mult, op1=ALU.add)
                    v.wait_ge(sem(f"s_r2C{ta % 2}"), 16 * (ta // 2 + 1))
                    if ta >= 2:
                        v.wait_ge(sem(f"s_outd{ta % 2}"), 16 * (ta // 2))
                    io = v.tensor_tensor(outb[ta % 2][:], t4[:], r2C[ta % 2][:],
                                         op=ALU.add)
                    inc("s_outv", io)

            @blk.scalar
            def _(a):
                a = _Proxy(chains["a"], a)
                a.wait_ge(sem("s_pre2"), 1)
                for ta in range(NT2):
                    a.wait_ge(sem("s_f4"), ta + 1)
                    iff = a.activation(ffn[ta % 2][:], ps_f[ta % 2][:], AF.Identity,
                                       bias=zerov[:], scale=g4s[:, ta:ta + 1])
                    inc("s_ffn", iff)
                    a.wait_ge(sem("s_st4"), ta + 1)
                    a.activation(lnv4[:], vee4[:], AF.Ln, bias=zerov[:], scale=1.0)
                    ie = a.activation(rstd4c[:, ta % 2:ta % 2 + 1], lnv4[:], AF.Exp,
                                      bias=zerov[:], scale=-0.5)
                    inc("s_rs4", ie)

            @blk.tensor
            def _(t):
                t.wait_ge(sem("s_w2T"), 16)
                for ta in range(NT2):
                    t.wait_ge(sem(f"s_qhT{ta % 2}"), 16 * (ta // 2 + 1))
                    if ta >= 2:
                        t.wait_ge(sem("s_ffn"), ta - 1)
                    last = None
                    for half in range(FH):
                        pslice = ps_f[ta % 2][:, 512 * half:512 * (half + 1)]
                        for j in range(HT):
                            last = t.matmul(pslice, qhTv[ta % 2][:, j, :],
                                            w2v[:, j, 512 * half:512 * (half + 1)],
                                            start=(j == 0), stop=(j == HT - 1))
                    inc("s_f4", last)

    return nc


_NC_CACHE = {}


def kernel(L0_attn_x, L0_attn_lambda_log, L0_attn_gamma, L0_attn_W_in,
           L0_attn_b_in, L0_attn_W_out, L0_attn_b_out, L0_ffn_w1, L0_ffn_w2):
    x = np.ascontiguousarray(np.asarray(L0_attn_x, dtype=np.float32))
    Bf, Lf, Df = x.shape
    nb = Bf // NCORES
    key = (nb, Lf, Df)
    if key not in _NC_CACHE:
        _NC_CACHE[key] = build_program(NB=nb, L=Lf, D=Df,
                                       H=np.asarray(L0_ffn_w1).shape[0])
    nc = _NC_CACHE[key]

    def row(a):
        return np.ascontiguousarray(np.asarray(a, dtype=np.float32)).reshape(1, -1)

    shared = {
        "w_in": np.ascontiguousarray(np.asarray(L0_attn_W_in, dtype=np.float32)),
        "w_out": np.ascontiguousarray(np.asarray(L0_attn_W_out, dtype=np.float32)),
        "b_in": row(L0_attn_b_in),
        "b_out": row(L0_attn_b_out),
        "gamma": row(L0_attn_gamma),
        "lamlog": row(L0_attn_lambda_log),
        "w1": np.ascontiguousarray(np.asarray(L0_ffn_w1, dtype=np.float32)),
        "w2": np.ascontiguousarray(np.asarray(L0_ffn_w2, dtype=np.float32)),
        "ident": np.eye(128, dtype=np.float32),
    }
    in_maps = []
    for c in range(NCORES):
        m = dict(shared)
        m["x"] = np.ascontiguousarray(
            x[c * nb:(c + 1) * nb].reshape(nb * Lf, Df))
        in_maps.append(m)
    res = run_bass_kernel_spmd(nc, in_maps, core_ids=list(range(NCORES)))
    out = np.empty((Bf, Lf, Df), dtype=np.float32)
    for c in range(NCORES):
        out[c * nb:(c + 1) * nb] = np.asarray(
            res.results[c]["y"], dtype=np.float32).reshape(nb, Lf, Df)
    return out


# version 22
# speedup vs baseline: 1.0493x; 1.0050x over previous
import contextlib
import numpy as np
import ml_dtypes
import concourse.bass as bass
import concourse.mybir as mybir
from concourse.bass_utils import run_bass_kernel_spmd

F32 = mybir.dt.float32
F32R = mybir.dt.float32r
BF16 = mybir.dt.bfloat16
I16 = mybir.dt.int16
ALU = mybir.AluOpType
AF = mybir.ActivationFunctionType
AX = mybir.AxisListType

NCORES = 8
EPS = 1e-5


def build_program(NB=4, L=2048, D=1024, H=4096, TA=256,
                  phases=("0", "A", "B1", "B2"), acap=None, dbgqd=False):
    DT = D // 128
    HT = H // 128
    R = NB * L
    NBLK = R // TA
    TPB = TA // 128
    NT = R // 128
    SEGB = L // TA
    HS = H // 512
    FH = D // 512

    nc = bass.Bass("TRN2", target_bir_lowering=False)

    x_d = nc.dram_tensor("x", [R, D], F32, kind="ExternalInput").ap()
    win_d = nc.dram_tensor("w_in", [D, D], F32, kind="ExternalInput").ap()
    wout_d = nc.dram_tensor("w_out", [D, D], F32, kind="ExternalInput").ap()
    bin_d = nc.dram_tensor("b_in", [1, D], F32, kind="ExternalInput").ap()
    bout_d = nc.dram_tensor("b_out", [1, D], F32, kind="ExternalInput").ap()
    gam_d = nc.dram_tensor("gamma", [1, D], F32, kind="ExternalInput").ap()
    laml_d = nc.dram_tensor("lamlog", [1, D], F32, kind="ExternalInput").ap()
    w1_d = nc.dram_tensor("w1", [H, D], F32, kind="ExternalInput").ap()
    w2_d = nc.dram_tensor("w2", [D, H], F32, kind="ExternalInput").ap()
    id_d = nc.dram_tensor("ident", [128, 128], F32, kind="ExternalInput").ap()
    y_d = nc.dram_tensor("y", [R, D], F32, kind="ExternalOutput").ap()
    r2_d = nc.dram_tensor("r2scr", [R, D], F32, kind="Internal").ap()
    qh_d = nc.dram_tensor("qhscr", [R, H], BF16, kind="Internal").ap()
    w1t_d = nc.dram_tensor("w1tscr", [128, DT * H], BF16, kind="Internal").ap()
    w2t_d = nc.dram_tensor("w2tscr", [128, HT * D], BF16, kind="Internal").ap()
    qd_d = nc.dram_tensor("qdscr", [R, D], BF16, kind="Internal").ap() if dbgqd else None
    scrq_d = nc.dram_tensor("scrq", [2, 128], F32, kind="Internal").ap()
    scrt_d = nc.dram_tensor("scrt", [2, 4], F32, kind="Internal").ap()

    es = contextlib.ExitStack()
    with es:
        def sb(name, shape, dt):
            return es.enter_context(nc.sbuf_tensor(name, shape, dt))

        zerov = sb("zerov", [128, 1], F32)
        halfv = sb("halfv", [128, 1], F32)
        epsv = sb("epsv", [128, 1], F32)
        ones_col = sb("ones_col", [128, 1], F32)
        ones_row = sb("ones_row", [1, 128], F32)
        thrv = [sb(f"thrv{w}", [128, 1], F32) for w in range(2)]
        nthrv = [sb(f"nthrv{w}", [128, 1], F32) for w in range(2)]
        sflv = [sb(f"sflv{w}", [128, 1], F32) for w in range(2)]
        rstd3s = sb("rstd3s", [128, NT], F32)
        m3s = sb("m3s", [128, NT], F32)
        am3s = sb("am3s", [128, NT], F32)
        sch_all = sb("sch_all", [128, NT], F32)
        a3s = sb("a3s", [128, NT], F32)
        b3s = sb("b3s", [128, NT], F32)
        g3s = sb("g3s", [128, NT], F32)
        g4s = sb("g4s", [128, NT], F32)
        p0A = es.enter_context(contextlib.ExitStack())
        def sb0A(name, shape, dt):
            return p0A.enter_context(nc.sbuf_tensor(name, shape, dt))
        ones_rr = sb0A("ones_rr", [1, 128], F32R)
        identt = sb0A("identt", [128, 128], F32)
        vecs3 = sb0A("vecs3", [128, 3 * DT], F32)
        lamx = sb0A("lamx", [128, DT], F32)
        gbv = sb0A("gbv", [128, DT], F32)
        bout_r = sb0A("bout_r", [1, D], F32R)
        winT = sb0A("winT", [128, DT * D], F32R)
        woutT = sb0A("woutT", [128, DT * D], F32R)

        sems = {}
        def S(n):
            sems[n] = [es.enter_context(nc.semaphore(n)), 0]
        for n in ["s0_row", "s0_wld0", "s0_wld1", "s0_vt", "s0_ve", "s0_lam", "s0_mis",
                  "s0_id", "s0_wtp", "s0_wte",
                  "s0_al0", "s0_al1", "s0_red", "s0_sum", "s0_ssc", "s0_bcm",
                  "s0_thr", "s0_bl0", "s0_bl1", "s0_tern", "s0_tp0", "s0_tp1", "s0_xb", "s0_sgn",
                  "s0_wst",
                  "sxA0", "sxA1", "s_st0", "s_rs0", "s_b0", "s_ln1", "s_t2",
                  "s_lt", "s_lte", "s_u",
                  "s_gu", "s_y", "s_a2", "s_attn", "s_st2", "s_rs2",
                  "s_r2v", "s_r2d0", "s_r2d1", "s_rs3", "s_st3",
                  "s_w1T", "s_r2B0", "s_r2B1", "s_pre", "s_y0d", "s_r16d",
                  "s_qd", "s_qdT0", "s_qdT1", "s_h3", "s_gel", "s_amx",
                  "s_r16h", "s_qh", "s_qhd0", "s_qhd1",
                  "s_w2T", "s_r2C0", "s_r2C1", "s_pre2", "s_qhT0", "s_qhT1",
                  "s_f4", "s_ffn", "s_st4", "s_rs4", "s_outv",
                  "s_outd0", "s_outd1", "s_dbg0", "s_dbg1"]:
            S(n)

        def sem(n):
            return sems[n][0]

        _CHAIN_OPS = {
            "tensor_scalar", "tensor_tensor", "tensor_copy", "tensor_reduce",
            "reduce_max", "reduce_sum", "bn_stats", "bn_aggr", "reciprocal",
            "tensor_tensor_scan", "scalar_tensor_tensor", "memset",
            "activation", "mul", "add", "copy", "sign", "sqrt", "square",
            "iota", "select",
        }
        _DMA_OPS = {"dma_start", "dma_start_transpose", "indirect_dma_start"}

        class _Chain:
            def __init__(self, name):
                self.h = es.enter_context(nc.semaphore(name))
                self.cnt = 0
                self.pending = None
                self.last = None

            def flush(self):
                if self.pending is not None:
                    self.pending.then_inc(self.h, 1)
                    self.cnt += 1
                    self.last = (self.h, self.cnt)
                    self.pending = None

            def pre(self, eng):
                self.flush()
                if self.last is not None:
                    eng.wait_ge(*self.last)

        class _Proxy:
            def __init__(self, chain, eng):
                self._c = chain
                self._e = eng

            def __getattr__(self, name):
                c, e = self._c, self._e
                if name in _CHAIN_OPS:
                    f = getattr(e, name)
                    def w(*a, **k):
                        c.pre(e)
                        ins = f(*a, **k)
                        c.pending = ins
                        return ins
                    return w
                if name in _DMA_OPS:
                    f = getattr(e, name)
                    def wd(*a, **k):
                        c.pre(e)
                        return f(*a, **k)
                    return wd
                return getattr(e, name)

        chains = {n: _Chain("ch_" + n) for n in ("v", "a", "g")}

        def inc(n, instr, k=1):
            instr.then_inc(sems[n][0], k)
            sems[n][1] += k
            for c in chains.values():
                if c.pending is instr:
                    c.pending = None
                    c.last = (sems[n][0], sems[n][1])

        with contextlib.ExitStack() as p0:
            def sb0(name, shape, dt):
                return p0.enter_context(nc.sbuf_tensor(name, shape, dt))
            rowbuf = sb0("rowbuf", [1, 3 * D], F32)
            wstg = [sb0(f"wstg{i}", [128, 4096], F32) for i in range(2)]
            ps_v = p0.enter_context(nc.psum_tensor("ps_v", [128, 3 * DT], F32))
            ps_t = [p0.enter_context(nc.psum_tensor(f"ps_t{i}", [128, 512], F32))
                    for i in range(2)]
            blk = p0.enter_context(nc.Block())

            def awchunk_ap(i):
                w, c = divmod(i, 2)
                src = win_d if w == 0 else wout_d
                return src[512 * c:512 * (c + 1), :].rearrange(
                    "(j p) d -> p j d", p=128)

            @blk.gpsimd
            def _(g):
                g = _Proxy(chains["g"], g)
                g.memset(zerov[:], 0.0)
                g.memset(halfv[:], 385.5)
                g.memset(epsv[:], EPS)
                g.memset(ones_col[:], 1.0)
                g.memset(ones_row[:], 1.0)
                inc("s0_row", g.dma_start(rowbuf[:, 0:D], bout_d), 16)
                g.wait_ge(sem("s0_ve"), 1)
                inc("s0_row", g.dma_start(rowbuf[:, 0:D], laml_d), 16)
                inc("s0_row", g.dma_start(rowbuf[:, D:2 * D], gam_d), 16)
                inc("s0_row", g.dma_start(rowbuf[:, 2 * D:3 * D], bin_d), 16)
                inc("s0_id", g.dma_start(identt[:], id_d), 16)
                for i in range(4):
                    if i >= 2:
                        g.wait_ge(sem("s0_wte"), 32 * (i - 1))
                    inc(f"s0_wld{i % 2}", g.dma_start(
                        wstg[i % 2][:].rearrange("p (j d) -> p j d", j=4),
                        awchunk_ap(i)), 16)

            @blk.tensor
            def _(t):
                t.wait_ge(sem("s0_row"), 64)
                mv = None
                for n in range(3 * DT):
                    mv = t.matmul(ps_v[:, n:n + 1], rowbuf[0:1, 128 * n:128 * (n + 1)],
                                  ones_col[0:1, 0:1], start=True, stop=True)
                inc("s0_vt", mv)
                t.wait_ge(sem("s0_id"), 16)
                ntp = 0
                for i in range(4):
                    t.wait_ge(sem(f"s0_wld{i % 2}"), 16 * (i // 2 + 1))
                    for j in range(4):
                        for dd in range(DT):
                            gidx = ntp
                            if gidx >= 8:
                                t.wait_ge(sem("s0_wte"), 4 * (gidx // 4 - 1))
                            mm = t.transpose(
                                ps_t[(gidx // 4) % 2][:, 128 * (gidx % 4):128 * (gidx % 4 + 1)],
                                wstg[i % 2][:, 1024 * j + 128 * dd:1024 * j + 128 * (dd + 1)],
                                identt[:])
                            if gidx % 4 == 3:
                                inc("s0_wtp", mm)
                            ntp += 1

            @blk.vector
            def _(v):
                v = _Proxy(chains["v"], v)
                v.wait_ge(sem("s0_row"), 16)
                v.tensor_copy(ones_rr[:], ones_row[:])
                ic = v.tensor_copy(bout_r[:], rowbuf[:, 0:D])
                inc("s0_ve", ic)
                v.wait_ge(sem("s0_vt"), 1)
                ic2 = v.tensor_copy(vecs3[:], ps_v[:])
                inc("s0_ve", ic2)
                wiv = winT[:].rearrange("p (i d) -> p i d", i=DT)
                wov = woutT[:].rearrange("p (i d) -> p i d", i=DT)
                ntp = 0
                for i in range(4):
                    w, c = divmod(i, 2)
                    for j in range(4):
                        etile = 4 * c + j
                        for dd in range(DT):
                            gidx = ntp
                            if gidx % 4 == 3:
                                grp = gidx // 4
                                v.wait_ge(sem("s0_wtp"), grp + 1)
                                dst = wiv if w == 0 else wov
                                iv_ = v.tensor_copy(
                                    dst[:, dd - 3:dd + 1, 128 * etile:128 * (etile + 1)],
                                    ps_t[grp % 2][:].rearrange("p (q r) -> p q r", q=4))
                                inc("s0_wte", iv_, 4)
                            ntp += 1
                v.wait_ge(sem("s0_lam"), 1)
                ig = v.tensor_tensor(gbv[:], vecs3[:, DT:2 * DT],
                                     vecs3[:, 2 * DT:3 * DT], op=ALU.mult)
                inc("s0_mis", ig)

            @blk.scalar
            def _(a):
                a = _Proxy(chains["a"], a)
                a.wait_ge(sem("s0_ve"), 2)
                ie = a.activation(lamx[:], vecs3[:, 0:DT], AF.Exp,
                                  bias=zerov[:], scale=1.0)
                inc("s0_lam", ie)

        if "A" not in phases:
            return nc
        NBLKA = NBLK if acap is None else min(NBLK, acap)
        GPB = TPB * DT // 4
        with contextlib.ExitStack() as pA:
            def sbA(name, shape, dt):
                return pA.enter_context(nc.sbuf_tensor(name, shape, dt))
            lam_bc = sbA("lam_bc", [128, DT * TA], F32)
            x_buf = [sbA(f"x_buf{i}", [128, TPB * D], F32) for i in range(2)]
            ln1f = [sbA(f"ln1f{i}", [128, TPB * D], F32) for i in range(2)]
            lnT = [sbA(f"lnT{i}", [128, DT * TA], F32R) for i in range(2)]
            gu_b = [sbA(f"gu{i}", [128, TA], F32) for i in range(2)]
            ybf = [sbA(f"ybf{i}", [128, DT * TA], F32R) for i in range(2)]
            attn_b = [sbA(f"attn_b{i}", [128, D], F32) for i in range(2)]
            t2buf = [sbA(f"t2buf{i}", [128, D], F32) for i in range(2)]
            r2b = [sbA(f"r2b{i}", [128, D], F32) for i in range(2)]
            rstd0s = [sbA(f"rstd0s{i}", [128, TPB], F32) for i in range(2)]
            bias0s = [sbA(f"bias0s{i}", [128, TPB], F32) for i in range(2)]
            rstd2c = [sbA(f"rstd2c{i}", [128, TPB], F32) for i in range(2)]
            bn6 = sbA("bn6", [128, 12], F32)
            bn2 = [sbA(f"bn2_{i}", [128, 2], F32) for i in range(2)]
            bn6a = sbA("bn6a", [128, 12], F32)
            bn2a = [sbA(f"bn2a_{i}", [128, 2], F32) for i in range(2)]
            bn6r = sbA("bn6r", [128, 12], F32)
            bn2r = [sbA(f"bn2r_{i}", [128, 2], F32) for i in range(2)]
            lnv = sbA("lnv", [128, 1], F32)
            nrstd = sbA("nrstd", [128, 1], F32)
            bias2 = sbA("bias2", [128, 1], F32)
            qstg = [sbA(f"qstg{i}", [128, 2048], F32) for i in range(2)]
            t1b = sbA("t1b", [128, 2048], BF16)
            t2b = [sbA(f"t2b{i}", [128, 2048], BF16) for i in range(2)]
            wtp = [sbA(f"wtp{i}", [128, 2048], BF16) for i in range(2)]
            qracc = sbA("qracc", [128, 1], F32)
            qrtmp = sbA("qrtmp", [128, 1], F32)
            qrow = sbA("qrow", [1, 128], F32)
            qssc = sbA("qssc", [1, 1], F32)
            qth = sbA("qth", [1, 4], F32)
            ps_u = [pA.enter_context(nc.psum_tensor(f"ps_u{i}", [128, TA], F32))
                    for i in range(2)]
            ps_a = [pA.enter_context(nc.psum_tensor(f"ps_a{i}", [128, D], F32))
                    for i in range(2)]
            ps_t2 = [pA.enter_context(nc.psum_tensor(f"ps_t2{i}", [128, 512], F32))
                     for i in range(2)]
            blk = pA.enter_context(nc.Block())

            def w1src(c):
                return w1_d[256 * c:256 * (c + 1), :].rearrange(
                    "(j p) d -> p j d", p=128)

            def w2src(c):
                rb, hf = divmod(c, 2)
                return w2_d[128 * rb:128 * (rb + 1), 2048 * hf:2048 * (hf + 1)]

            def qload_src(q):
                if q < 16:
                    return w1src(q), True
                if q < 32:
                    return w2src(q - 16), False
                if q < 48:
                    return w1src(q - 32), True
                return w2src(q - 48), False

            def pieces_through(c2):
                return 2 * (c2 + 1) if c2 < 16 else 32 + (c2 - 15)

            QPIECES = ([(c2, j) for c2 in range(16) for j in range(2)]
                       + [(c2, None) for c2 in range(16, 32)])
            QN_EL = H * D

            def xap(b):
                return x_d[b * TA:(b + 1) * TA, :].rearrange("(t p) d -> p t d", p=128)

            xv = [x_buf[i][:].rearrange("p (t d) -> p t d", t=TPB) for i in range(2)]
            lTv = [lnT[i][:].rearrange("p (k t) -> p k t", k=DT) for i in range(2)]
            ybv = [ybf[i][:].rearrange("p (k t) -> p k t", k=DT) for i in range(2)]
            wiv = winT[:].rearrange("p (i d) -> p i d", i=DT)
            wov = woutT[:].rearrange("p (i d) -> p i d", i=DT)
            lbv = lam_bc[:].rearrange("p (k t) -> p k t", k=DT)

            @blk.gpsimd
            def _(g):
                g = _Proxy(chains["g"], g)
                for b in range(NBLKA):
                    if b >= 2:
                        g.wait_ge(sem("s_r2v"), TPB * (b - 1))
                    inc(f"sxA{b % 2}", g.dma_start(xv[b % 2], xap(b)), 16)
                    if b >= 1:
                        for t in range(TPB):
                            ta = TPB * (b - 1) + t
                            g.wait_ge(sem("s_r2v"), ta + 1)
                            inc(f"s_r2d{ta % 2}", g.dma_start(
                                r2_d[128 * ta:128 * (ta + 1), :], r2b[ta % 2][:]), 16)
                for t in range(TPB):
                    ta = TPB * (NBLKA - 1) + t
                    g.wait_ge(sem("s_r2v"), ta + 1)
                    inc(f"s_r2d{ta % 2}", g.dma_start(
                        r2_d[128 * ta:128 * (ta + 1), :], r2b[ta % 2][:]), 16)
                g.wait_ge(sem("s_r2d0"), 16 * (TPB * NBLKA // 2))
                g.wait_ge(sem("s_r2d1"), 16 * (TPB * NBLKA // 2))
                g.wait_ge(sem("sxA0"), 16 * ((NBLKA + 1) // 2))
                g.wait_ge(sem("sxA1"), 16 * (NBLKA // 2))

            @blk.vector
            def _(v):
                v = _Proxy(chains["v"], v)

                def q_dve_tasks():
                    def reduce_chunk(i):
                        v.wait_ge(sem(f"s0_al{i % 2}"), 16 * (i // 2 + 1))
                        v.tensor_reduce(qrtmp[:], qstg[i % 2][:], axis=AX.X,
                                        op=ALU.add, apply_absolute_value=True)
                        ia = v.tensor_tensor(qracc[:], qracc[:], qrtmp[:],
                                             op=ALU.add)
                        inc("s0_red", ia)

                    def scale_dve(w):
                        v.wait_ge(sem("s0_sum"), 32 * (w + 1))
                        if w == 1:
                            v.wait_ge(sem("s0_bcm"), 16)
                        v.tensor_reduce(qssc[:], qrow[:], axis=AX.X, op=ALU.add)
                        v.tensor_scalar(qssc[:], qssc[:], 1.0 / QN_EL, EPS,
                                        op0=ALU.mult, op1=ALU.max)
                        v.tensor_scalar(qth[:, 0:1], qssc[:], 0.5, None,
                                        op0=ALU.mult)
                        v.tensor_scalar(qth[:, 1:2], qth[:, 0:1], -1.0,
                                        None, op0=ALU.mult)
                        iq = v.tensor_scalar(qth[:, 2:3], qssc[:], -1.0,
                                             None, op0=ALU.mult)
                        inc("s0_ssc", iq)

                    yield lambda: v.memset(qracc[:], 0.0)
                    for i in range(16):
                        yield lambda i=i: reduce_chunk(i)

                    def switch():
                        v.wait_ge(sem("s0_sum"), 16)
                        v.memset(qracc[:], 0.0)
                    yield switch
                    yield lambda: scale_dve(0)
                    for i in range(16, 32):
                        yield lambda i=i: reduce_chunk(i)
                    for c2 in range(32):
                        if c2 == 13:
                            yield lambda: scale_dve(1)
                        def tern(c2=c2):
                            q = 32 + c2
                            w = 0 if c2 < 16 else 1
                            v.wait_ge(sem(f"s0_al{q % 2}"), 16 * (q // 2 + 1))
                            v.wait_ge(sem("s0_thr"), 48 * (w + 1))
                            v.tensor_scalar(t1b[:], qstg[q % 2][:], thrv[w][:],
                                            None, op0=ALU.is_ge)
                            if c2 >= 2:
                                v.wait_ge(sem("s0_xb"),
                                          16 * pieces_through(c2 - 2))
                            it = v.scalar_tensor_tensor(
                                t2b[c2 % 2][:], qstg[q % 2][:], nthrv[w][:],
                                t1b[:], op0=ALU.is_le, op1=ALU.subtract)
                            inc("s0_tern", it)
                        yield tern

                dveq = q_dve_tasks()

                def dve_qslot(n):
                    for _ in range(n):
                        t = next(dveq, None)
                        if t is None:
                            return
                        t()

                v.memset(lam_bc[:], 1.0)
                for k in range(DT):
                    v.tensor_scalar(lbv[:, k, :], lbv[:, k, :], lamx[:, k:k + 1],
                                    None, op0=ALU.mult)
                def scans_sec(b):
                    seg = b % SEGB
                    for k in range(DT):
                        gcnt = DT * b + k
                        v.wait_ge(sem("s_gu"), gcnt + 1)
                        if b >= 2:
                            v.wait_ge(sem("s_a2"), TPB * (b - 1))
                        init = 0.0 if seg == 0 else ybv[(b - 1) % 2][:, k, TA - 1:TA]
                        isc = v.tensor_tensor_scan(ybv[b % 2][:, k, :], lbv[:, k, :],
                                                   gu_b[gcnt % 2][:], init,
                                                   op0=ALU.mult, op1=ALU.add)
                        inc("s_y", isc)

                def attnp(b):
                    for t in range(TPB):
                        ta = TPB * b + t
                        v.wait_ge(sem("s_a2"), ta + 1)
                        if ta >= 2:
                            v.wait_ge(sem("s_t2"), ta - 1)
                        iat = v.tensor_tensor(attn_b[ta % 2][:],
                                              ln1f[b % 2][:, t * D:(t + 1) * D],
                                              ps_a[ta % 2][:], op=ALU.add)
                        inc("s_attn", iat)
                        v.bn_stats(bn6a[:, 0:6], attn_b[ta % 2][:, 0:512])
                        v.bn_stats(bn6a[:, 6:12], attn_b[ta % 2][:, 512:1024])
                        ia2 = v.bn_aggr(bn2a[ta % 2][:], bn6a[:])
                        inc("s_st2", ia2)

                def r2p(b):
                    for t in range(TPB):
                        ta = TPB * b + t
                        v.wait_ge(sem("s_t2"), ta + 1)
                        if ta >= 2:
                            v.wait_ge(sem(f"s_r2d{ta % 2}"), 16 * (ta // 2))
                        ir2 = v.tensor_tensor(r2b[ta % 2][:], t2buf[ta % 2][:],
                                              xv[b % 2][:, t, :], op=ALU.add)
                        inc("s_r2v", ir2)
                        v.bn_stats(bn6r[:, 0:6], r2b[ta % 2][:, 0:512])
                        v.bn_stats(bn6r[:, 6:12], r2b[ta % 2][:, 512:1024])
                        if ta >= 2:
                            v.wait_ge(sem("s_rs3"), ta - 1)
                        ia3 = v.bn_aggr(bn2r[ta % 2][:], bn6r[:])
                        inc("s_st3", ia3)
                        v.tensor_scalar(t2buf[ta % 2][:], r2b[ta % 2][:],
                                        bn2r[ta % 2][:, 0:1],
                                        None, op0=ALU.subtract)
                        iam = v.tensor_reduce(am3s[:, ta:ta + 1], t2buf[ta % 2][:],
                                              axis=AX.X,
                                              op=ALU.max, apply_absolute_value=True)
                        inc("s_rs2", iam)

                for b in range(NBLKA):
                    seg = b % SEGB
                    for t in range(TPB):
                        ta = TPB * b + t
                        v.wait_ge(sem(f"sxA{b % 2}"), 16 * (b // 2 + 1))
                        if ta >= 2:
                            v.wait_ge(sem("s_ln1"), ta - 1)
                        v.bn_stats(bn6[:, 0:6], x_buf[b % 2][:, t * D:t * D + 512])
                        v.bn_stats(bn6[:, 6:12], x_buf[b % 2][:, t * D + 512:t * D + 1024])
                        ia0 = v.bn_aggr(bn2[ta % 2][:], bn6[:])
                        inc("s_st0", ia0)
                    if b >= 1:
                        attnp(b - 1)
                    for gi in range(GPB):
                        grp = GPB * b + gi
                        tt = gi // (DT // 4)
                        q = gi % (DT // 4)
                        v.wait_ge(sem("s_lt"), grp + 1)
                        if b >= 2:
                            v.wait_ge(sem("s_u"), DT * (b - 1))
                        ie = v.tensor_copy(
                            lTv[b % 2][:, 4 * q:4 * q + 4, 128 * tt:128 * (tt + 1)],
                            ps_t2[grp % 2][:].rearrange("p (q r) -> p q r", q=4))
                        inc("s_lte", ie)
                    if b >= 1:
                        r2p(b - 1)
                    scans_sec(b)
                    dve_qslot(3)
                attnp(NBLKA - 1)
                r2p(NBLKA - 1)
                dve_qslot(100)

            @blk.scalar
            def _(a):
                a = _Proxy(chains["a"], a)

                def ln2p(b):
                    for t in range(TPB):
                        ta = TPB * b + t
                        a.wait_ge(sem("s_st2"), ta + 1)
                        a.activation(lnv[:], bn2a[ta % 2][:, 1:2], AF.Ln,
                                     bias=epsv[:], scale=1.0)
                        a.activation(rstd2c[b % 2][:, t:t + 1], lnv[:], AF.Exp,
                                     bias=zerov[:], scale=-0.5)
                        a.activation(nrstd[:], rstd2c[b % 2][:, t:t + 1], AF.Identity,
                                     bias=zerov[:], scale=-1.0)
                        a.activation(bias2[:], bn2a[ta % 2][:, 0:1], AF.Identity,
                                     bias=zerov[:], scale=nrstd[:])
                        if ta >= 2:
                            a.wait_ge(sem("s_rs2"), ta - 1)
                        it2 = a.activation(t2buf[ta % 2][:], attn_b[ta % 2][:],
                                           AF.Identity, bias=bias2[:],
                                           scale=rstd2c[b % 2][:, t:t + 1])
                        inc("s_t2", it2)

                def ln3p(b):
                    for t in range(TPB):
                        ta = TPB * b + t
                        a.wait_ge(sem("s_st3"), ta + 1)
                        a.activation(lnv[:], bn2r[ta % 2][:, 1:2], AF.Ln,
                                     bias=epsv[:], scale=1.0)
                        a.activation(rstd3s[:, ta:ta + 1], lnv[:], AF.Exp,
                                     bias=zerov[:], scale=-0.5)
                        im3 = a.activation(m3s[:, ta:ta + 1], bn2r[ta % 2][:, 0:1],
                                           AF.Identity, bias=zerov[:], scale=1.0)
                        inc("s_rs3", im3)

                for b in range(NBLKA):
                    for t in range(TPB):
                        ta = TPB * b + t
                        a.wait_ge(sem("s_st0"), ta + 1)
                        a.activation(lnv[:], bn2[ta % 2][:, 1:2], AF.Ln,
                                     bias=epsv[:], scale=1.0)
                        a.activation(rstd0s[b % 2][:, t:t + 1], lnv[:], AF.Exp,
                                     bias=zerov[:], scale=-0.5)
                        a.activation(nrstd[:], rstd0s[b % 2][:, t:t + 1], AF.Identity,
                                     bias=zerov[:], scale=-1.0)
                        a.activation(bias0s[b % 2][:, t:t + 1], bn2[ta % 2][:, 0:1],
                                     AF.Identity, bias=zerov[:], scale=nrstd[:])
                        if b >= 2:
                            a.wait_ge(sem("s_lt"), GPB * (b - 1))
                            a.wait_ge(sem("s_attn"), TPB * (b - 1))
                        i1 = a.activation(ln1f[b % 2][:, t * D:(t + 1) * D],
                                          xv[b % 2][:, t, :], AF.Identity,
                                          bias=bias0s[b % 2][:, t:t + 1],
                                          scale=rstd0s[b % 2][:, t:t + 1])
                        inc("s_ln1", i1)
                    if b >= 1:
                        ln2p(b - 1)
                    for k in range(DT):
                        gcnt = DT * b + k
                        a.wait_ge(sem("s_u"), gcnt + 1)
                        if gcnt >= 2:
                            a.wait_ge(sem("s_y"), gcnt - 1)
                        ig = a.activation(gu_b[gcnt % 2][:], ps_u[gcnt % 2][:],
                                          AF.Identity, bias=gbv[:, k:k + 1],
                                          scale=vecs3[:, DT + k:DT + k + 1])
                        inc("s_gu", ig)
                    if b >= 1:
                        ln3p(b - 1)
                ln2p(NBLKA - 1)
                ln3p(NBLKA - 1)

            @blk.sync
            def _(s):
                w1tv = w1t_d.rearrange("p (i h) -> p i h", i=DT)
                w2tv = w2t_d.rearrange("p (i f) -> p i f", i=HT)
                pctr = [0]

                def emit_pieces(c2):
                    npiece = 2 if c2 < 16 else 1
                    for j in (range(2) if c2 < 16 else [None]):
                        p = pctr[0]
                        pctr[0] += 1
                        s.wait_ge(sem("s0_tern"), c2 + 1)
                        if p >= 2:
                            s.wait_ge(sem(f"s0_bl{p % 2}"), 16 * (p // 2))
                        if j is None:
                            wv = wtp[p % 2][:].rearrange(
                                "p (i r) -> p i r", i=16)
                            xin = t2b[c2 % 2][:]
                        else:
                            wv = wtp[p % 2][:, 0:1024].rearrange(
                                "p (i r) -> p i r", i=DT)
                            xin = t2b[c2 % 2][:, 1024 * j:1024 * (j + 1)]
                        inc("s0_xb", s.dma_start_transpose(wv, xin), 16)
                        s.wait_ge(sem("s0_xb"), 16 * (p + 1))
                        if j is None:
                            c = c2 - 16
                            rb, hf = divmod(c, 2)
                            dstp = w2tv[:, 16 * hf:16 * (hf + 1),
                                        128 * rb:128 * (rb + 1)]
                        else:
                            hrow = 2 * c2 + j
                            dstp = w1tv[:, :, 128 * hrow:128 * (hrow + 1)]
                        inc(f"s0_bl{p % 2}", s.dma_start(dstp, wv), 16)

                for q in range(64):
                    if q == 18:
                        s.wait_ge(sem("s0_red"), 16)
                        inc("s0_sum", s.dma_start(scrq_d[0:1, :], qracc[:]), 16)
                        s.wait_ge(sem("s0_sum"), 16)
                        inc("s0_sum", s.dma_start(qrow[:], scrq_d[0:1, :]), 16)
                    if q == 20:
                        s.wait_ge(sem("s0_ssc"), 1)
                        inc("s0_bcm", s.dma_start(scrt_d[0:1, 0:3], qth[:, 0:3]), 16)
                        s.wait_ge(sem("s0_bcm"), 16)
                        inc("s0_thr", s.dma_start(
                            thrv[0][:],
                            scrt_d[0:1, 0:1].broadcast_to((128, 1))), 16)
                        inc("s0_thr", s.dma_start(
                            nthrv[0][:],
                            scrt_d[0:1, 1:2].broadcast_to((128, 1))), 16)
                        inc("s0_thr", s.dma_start(
                            sflv[0][:],
                            scrt_d[0:1, 2:3].broadcast_to((128, 1))), 16)
                    if q == 44:
                        s.wait_ge(sem("s0_red"), 32)
                        inc("s0_sum", s.dma_start(scrq_d[1:2, :], qracc[:]), 16)
                        s.wait_ge(sem("s0_sum"), 48)
                        inc("s0_sum", s.dma_start(qrow[:], scrq_d[1:2, :]), 16)
                    if q == 46:
                        s.wait_ge(sem("s0_ssc"), 2)
                        inc("s0_bcm", s.dma_start(scrt_d[1:2, 0:3], qth[:, 0:3]), 16)
                        s.wait_ge(sem("s0_bcm"), 32)
                        inc("s0_thr", s.dma_start(
                            thrv[1][:],
                            scrt_d[1:2, 0:1].broadcast_to((128, 1))), 16)
                        inc("s0_thr", s.dma_start(
                            nthrv[1][:],
                            scrt_d[1:2, 1:2].broadcast_to((128, 1))), 16)
                        inc("s0_thr", s.dma_start(
                            sflv[1][:],
                            scrt_d[1:2, 2:3].broadcast_to((128, 1))), 16)
                    if q >= 2:
                        if q - 2 < 32:
                            s.wait_ge(sem("s0_red"), q - 1)
                        else:
                            s.wait_ge(sem("s0_tern"), q - 33)
                    src, isw1 = qload_src(q)
                    dst = (qstg[q % 2][:].rearrange("p (j d) -> p j d", j=2)
                           if isw1 else qstg[q % 2][:])
                    inc(f"s0_al{q % 2}", s.dma_start(dst, src), 16)
                    if q >= 34:
                        emit_pieces(q - 34)
                emit_pieces(30)
                emit_pieces(31)
                s.wait_ge(sem("s0_bl0"), 16 * 24)
                s.wait_ge(sem("s0_bl1"), 16 * 24)
                s.wait_ge(sem("s0_al0"), 16 * 32)
                s.wait_ge(sem("s0_al1"), 16 * 32)

            @blk.tensor
            def _(t):
                def mm2sec(b):
                    t.wait_ge(sem("s_y"), DT * (b + 1))
                    for tt in range(TPB):
                        ta = TPB * b + tt
                        if ta >= 2:
                            t.wait_ge(sem("s_attn"), ta - 1)
                        last = None
                        for half in range(FH):
                            pslice = ps_a[ta % 2][:, 512 * half:512 * (half + 1)]
                            t.matmul(pslice, ones_rr[:],
                                     bout_r[0:1, 512 * half:512 * (half + 1)],
                                     start=True, stop=False)
                            for k in range(DT):
                                last = t.matmul(
                                    pslice,
                                    ybv[b % 2][:, k, 128 * tt:128 * (tt + 1)],
                                    wov[:, k, 512 * half:512 * (half + 1)],
                                    start=False, stop=(k == DT - 1))
                        inc("s_a2", last)

                for b in range(NBLKA):
                    if b >= 1:
                        mm2sec(b - 1)
                    for gi in range(GPB):
                        grp = GPB * b + gi
                        tt = gi // (DT // 4)
                        q = gi % (DT // 4)
                        ta = TPB * b + tt
                        t.wait_ge(sem("s_ln1"), ta + 1)
                        if grp >= 2:
                            t.wait_ge(sem("s_lte"), grp - 1)
                        mm = None
                        for r in range(4):
                            dd = 4 * q + r
                            mm = t.transpose(
                                ps_t2[grp % 2][:, 128 * r:128 * (r + 1)],
                                ln1f[b % 2][:, tt * D + 128 * dd:tt * D + 128 * (dd + 1)],
                                identt[:])
                        inc("s_lt", mm)
                    t.wait_ge(sem("s_lte"), GPB * (b + 1))
                    for k in range(DT):
                        gcnt = DT * b + k
                        if gcnt >= 2:
                            t.wait_ge(sem("s_gu"), gcnt - 1)
                        last = None
                        for j in range(DT):
                            last = t.matmul(ps_u[gcnt % 2][:],
                                            wiv[:, j, 128 * k:128 * (k + 1)],
                                            lTv[b % 2][:, j, :],
                                            start=(j == 0), stop=(j == DT - 1))
                        inc("s_u", last)
                mm2sec(NBLKA - 1)

        p0A.close()

        if "B1" not in phases:
            return nc
        with contextlib.ExitStack() as p1:
            def sb1(name, shape, dt):
                return p1.enter_context(nc.sbuf_tensor(name, shape, dt))
            w1T = sb1("w1T", [128, DT * H], BF16)
            r2B = [sb1(f"r2B{i}", [128, D], F32) for i in range(2)]
            y0d = [sb1(f"y0d{i}", [128, D], F32) for i in range(2)]
            r16d = [sb1(f"r16d{i}", [128, D], I16) for i in range(2)]
            bd = sb1("bd", [128, D], BF16)
            cd = sb1("cd", [128, D], BF16)
            qd = [sb1(f"qd{i}", [128, D], BF16) for i in range(2)]
            qdT = [sb1(f"qdT{i}", [128, DT * 128], BF16) for i in range(2)]
            h_b = sb1("h_b", [128, H], F32)
            y0h = [sb1(f"y0h{i}", [128, H], F32) for i in range(2)]
            r16h = [sb1(f"r16h{i}", [128, H], I16) for i in range(2)]
            bh = sb1("bh", [128, H], BF16)
            ch = sb1("ch", [128, H], BF16)
            qh_b = [sb1(f"qh_b{i}", [128, H], BF16) for i in range(2)]
            amh = sb1("amh", [128, 1], F32)
            irc = sb1("irc", [128, 1], F32)
            tnt = sb1("tnt", [128, NT], F32)
            ps_h = [p1.enter_context(nc.psum_tensor(f"ps_h{i}", [128, 512], F32))
                    for i in range(8)]
            blk = p1.enter_context(nc.Block())
            w1v = w1T[:].rearrange("p (i h) -> p i h", i=DT)
            qdTv = [qdT[i][:].rearrange("p (k t) -> p k t", k=DT) for i in range(2)]
            NT2 = NT

            @blk.gpsimd
            def _(g):
                g = _Proxy(chains["g"], g)
                inc("s_w1T", g.dma_start(w1T[:], w1t_d), 16)
                for ta in range(min(2, NT2)):
                    inc(f"s_r2B{ta % 2}", g.dma_start(
                        r2B[ta % 2][:], r2_d[128 * ta:128 * (ta + 1), :]), 16)
                for ta in range(NT2):
                    if ta + 2 < NT2:
                        g.wait_ge(sem("s_y0d"), ta + 1)
                        inc(f"s_r2B{ta % 2}", g.dma_start(
                            r2B[ta % 2][:], r2_d[128 * (ta + 2):128 * (ta + 3), :]),
                            16)
                    if dbgqd:
                        g.wait_ge(sem("s_qd"), ta + 1)
                        inc(f"s_dbg{ta % 2}", g.dma_start(
                            qd_d[128 * ta:128 * (ta + 1), :], qd[ta % 2][:]), 16)
                    g.wait_ge(sem("s_qh"), ta + 1)
                    inc(f"s_qhd{ta % 2}", g.dma_start(
                        qh_d[128 * ta:128 * (ta + 1), :], qh_b[ta % 2][:]), 16)
                g.wait_ge(sem("s_qhd0"), 16 * ((NT2 + 1) // 2))
                g.wait_ge(sem("s_qhd1"), 16 * (NT2 // 2))
                g.wait_ge(sem("s_r2B0"), 16 * ((NT2 + 1) // 2))
                g.wait_ge(sem("s_r2B1"), 16 * (NT2 // 2))
                if dbgqd:
                    g.wait_ge(sem("s_dbg0"), 16 * ((NT2 + 1) // 2))
                    g.wait_ge(sem("s_dbg1"), 16 * (NT2 // 2))

            @blk.vector
            def _(v):
                v = _Proxy(chains["v"], v)
                v.tensor_tensor(tnt[:], rstd3s[:], am3s[:], op=ALU.mult)
                v.tensor_scalar(tnt[:], tnt[:], EPS, None, op0=ALU.max)
                v.reciprocal(g3s[:], tnt[:])
                v.tensor_scalar(tnt[:], g3s[:], 127.0, None, op0=ALU.mult)
                v.tensor_tensor(a3s[:], rstd3s[:], tnt[:], op=ALU.mult)
                v.tensor_tensor(b3s[:], m3s[:], a3s[:], op=ALU.mult)
                v.tensor_scalar(b3s[:], b3s[:], -1.0, 385.5, op0=ALU.mult, op1=ALU.add)
                v.reciprocal(g3s[:], tnt[:])
                i8 = v.tensor_scalar(g3s[:], g3s[:], sflv[0][:], None, op0=ALU.mult)
                inc("s_pre", i8)
                def dcorr(ta):
                    v.wait_ge(sem("s_r16d"), ta + 1)
                    v.tensor_tensor(bd[:], y0d[ta % 2][:], r16d[ta % 2][:],
                                    op=ALU.is_lt)
                    v.scalar_tensor_tensor(cd[:], y0d[ta % 2][:], 1.0,
                                           r16d[ta % 2][:],
                                           op0=ALU.subtract, op1=ALU.is_ge)
                    v.tensor_tensor(bd[:], cd[:], bd[:], op=ALU.subtract)
                    if ta >= 2:
                        v.wait_ge(sem(f"s_qdT{ta % 2}"), 16 * (ta // 2))
                        if dbgqd:
                            v.wait_ge(sem(f"s_dbg{ta % 2}"), 16 * (ta // 2))
                    iq = v.scalar_tensor_tensor(qd[ta % 2][:], r16d[ta % 2][:],
                                                385.0, bd[:],
                                                op0=ALU.subtract, op1=ALU.add)
                    inc("s_qd", iq)
                dcorr(0)
                for ta in range(NT2):
                    v.wait_ge(sem("s_gel"), HS * ta + HS)
                    v.tensor_reduce(amh[:], h_b[:], axis=AX.X,
                                    op=ALU.max, apply_absolute_value=True)
                    v.tensor_scalar(amh[:], amh[:], EPS, None, op0=ALU.max)
                    v.reciprocal(irc[:], amh[:])
                    i3 = v.tensor_scalar(sch_all[:, ta:ta + 1], irc[:], 127.0, None,
                                         op0=ALU.mult)
                    inc("s_amx", i3)
                    if ta + 1 < NT2:
                        dcorr(ta + 1)
                    v.wait_ge(sem("s_r16h"), ta + 1)
                    v.tensor_tensor(bh[:], y0h[ta % 2][:], r16h[ta % 2][:],
                                    op=ALU.is_lt)
                    v.scalar_tensor_tensor(ch[:], y0h[ta % 2][:], 1.0,
                                           r16h[ta % 2][:],
                                           op0=ALU.subtract, op1=ALU.is_ge)
                    v.tensor_tensor(bh[:], ch[:], bh[:], op=ALU.subtract)
                    if ta >= 2:
                        v.wait_ge(sem(f"s_qhd{ta % 2}"), 16 * (ta // 2))
                    iqh = v.scalar_tensor_tensor(qh_b[ta % 2][:], r16h[ta % 2][:],
                                                 385.0, bh[:],
                                                 op0=ALU.subtract, op1=ALU.add)
                    inc("s_qh", iqh)

            @blk.scalar
            def _(a):
                a = _Proxy(chains["a"], a)
                a.wait_ge(sem("s_pre"), 1)
                for it in range(NT2 + 1):
                    if it < NT2:
                        ta = it
                        a.wait_ge(sem(f"s_r2B{ta % 2}"), 16 * (ta // 2 + 1))
                        if ta >= 2:
                            a.wait_ge(sem("s_qd"), ta - 1)
                        iy = a.activation(y0d[ta % 2][:], r2B[ta % 2][:], AF.Identity,
                                          bias=b3s[:, ta:ta + 1],
                                          scale=a3s[:, ta:ta + 1])
                        inc("s_y0d", iy)
                        ir = a.activation(r16d[ta % 2][:], y0d[ta % 2][:], AF.Identity,
                                          bias=zerov[:], scale=1.0)
                        inc("s_r16d", ir)
                    if it >= 1:
                        tb = it - 1
                        for hs in range(HS):
                            q = HS * tb + hs
                            a.wait_ge(sem("s_h3"), q + 1)
                            ig = a.activation(h_b[:, 512 * hs:512 * (hs + 1)],
                                              ps_h[q % 8][:], AF.Gelu,
                                              bias=zerov[:], scale=g3s[:, tb:tb + 1])
                            inc("s_gel", ig)
                        a.wait_ge(sem("s_amx"), tb + 1)
                        if tb >= 2:
                            a.wait_ge(sem("s_qh"), tb - 1)
                        a.activation(y0h[tb % 2][:], h_b[:], AF.Identity,
                                     bias=halfv[:], scale=sch_all[:, tb:tb + 1])
                        irh = a.activation(r16h[tb % 2][:], y0h[tb % 2][:],
                                           AF.Identity, bias=zerov[:], scale=1.0)
                        inc("s_r16h", irh)

            @blk.sync
            def _(s):
                for ta in range(NT2):
                    s.wait_ge(sem("s_qd"), ta + 1)
                    if ta >= 2:
                        s.wait_ge(sem("s_h3"), HS * (ta - 1))
                    inc(f"s_qdT{ta % 2}", s.dma_start_transpose(
                        qdTv[ta % 2], qd[ta % 2][:]), 16)

            @blk.tensor
            def _(t):
                t.wait_ge(sem("s_w1T"), 16)
                for ta in range(NT2):
                    t.wait_ge(sem(f"s_qdT{ta % 2}"), 16 * (ta // 2 + 1))
                    for hs in range(HS):
                        q = HS * ta + hs
                        if q >= 8:
                            t.wait_ge(sem("s_gel"), q - 7)
                        last = None
                        for j in range(DT):
                            last = t.matmul(ps_h[q % 8][:],
                                            qdTv[ta % 2][:, j, :],
                                            w1v[:, j, 512 * hs:512 * (hs + 1)],
                                            start=(j == 0), stop=(j == DT - 1))
                        inc("s_h3", last)

        if "B2" not in phases:
            return nc
        with contextlib.ExitStack() as p2:
            def sb2(name, shape, dt):
                return p2.enter_context(nc.sbuf_tensor(name, shape, dt))
            w2T = sb2("w2T", [128, HT * D], BF16)
            qhT = [sb2(f"qhT{i}", [128, HT * 128], BF16) for i in range(2)]
            r2C = [sb2(f"r2C{i}", [128, D], F32) for i in range(2)]
            ffn = [sb2(f"ffn{i}", [128, D], F32) for i in range(2)]
            outb = [sb2(f"outb{i}", [128, D], F32) for i in range(2)]
            rstd4c = sb2("rstd4c", [128, 2], F32)
            bn64 = sb2("bn64", [128, 12], F32)
            bn24 = sb2("bn24", [128, 2], F32)
            vee4 = sb2("vee4", [128, 1], F32)
            lnv4 = sb2("lnv4", [128, 1], F32)
            bias4 = sb2("bias4", [128, 1], F32)
            t4 = sb2("t4", [128, D], F32)
            ps_f = [p2.enter_context(nc.psum_tensor(f"ps_f{i}", [128, D], F32))
                    for i in range(2)]
            blk = p2.enter_context(nc.Block())
            w2v = w2T[:].rearrange("p (i f) -> p i f", i=HT)
            qhTv = [qhT[i][:].rearrange("p (k t) -> p k t", k=HT) for i in range(2)]
            NT2 = NT

            @blk.gpsimd
            def _(g):
                g = _Proxy(chains["g"], g)
                inc("s_w2T", g.dma_start(w2T[:], w2t_d), 16)
                for ta in range(NT2):
                    if ta >= 2:
                        g.wait_ge(sem("s_outv"), ta - 1)
                    inc(f"s_r2C{ta % 2}", g.dma_start(
                        r2C[ta % 2][:], r2_d[128 * ta:128 * (ta + 1), :]), 16)
                    if ta >= 1:
                        tp = ta - 1
                        g.wait_ge(sem("s_outv"), tp + 1)
                        inc(f"s_outd{tp % 2}", g.dma_start(
                            y_d[128 * tp:128 * (tp + 1), :], outb[tp % 2][:]), 16)
                tp = NT2 - 1
                g.wait_ge(sem("s_outv"), tp + 1)
                inc(f"s_outd{tp % 2}", g.dma_start(
                    y_d[128 * tp:128 * (tp + 1), :], outb[tp % 2][:]), 16)
                g.wait_ge(sem("s_outd0"), 16 * ((NT2 + 1) // 2))
                g.wait_ge(sem("s_outd1"), 16 * (NT2 // 2))

            @blk.sync
            def _(s):
                for ta in range(NT2):
                    if ta >= 2:
                        s.wait_ge(sem("s_f4"), ta - 1)
                    inc(f"s_qhT{ta % 2}", s.dma_start_transpose(
                        qhTv[ta % 2], qh_d[128 * ta:128 * (ta + 1), :]), 16)

            @blk.vector
            def _(v):
                v = _Proxy(chains["v"], v)
                v.reciprocal(g4s[:], sch_all[:])
                i1 = v.tensor_scalar(g4s[:], g4s[:], sflv[1][:], None, op0=ALU.mult)
                inc("s_pre2", i1)
                for ta in range(NT2):
                    v.wait_ge(sem("s_ffn"), ta + 1)
                    v.bn_stats(bn64[:, 0:6], ffn[ta % 2][:, 0:512])
                    v.bn_stats(bn64[:, 6:12], ffn[ta % 2][:, 512:1024])
                    v.bn_aggr(bn24[:], bn64[:])
                    iv4 = v.tensor_scalar(vee4[:], bn24[:, 1:2], EPS, None, op0=ALU.add)
                    inc("s_st4", iv4)
                    v.wait_ge(sem("s_rs4"), ta + 1)
                    v.tensor_scalar(bias4[:], bn24[:, 0:1],
                                    rstd4c[:, ta % 2:ta % 2 + 1], -1.0,
                                    op0=ALU.mult, op1=ALU.mult)
                    v.tensor_scalar(t4[:], ffn[ta % 2][:],
                                    rstd4c[:, ta % 2:ta % 2 + 1], bias4[:],
                                    op0=ALU.mult, op1=ALU.add)
                    v.wait_ge(sem(f"s_r2C{ta % 2}"), 16 * (ta // 2 + 1))
                    if ta >= 2:
                        v.wait_ge(sem(f"s_outd{ta % 2}"), 16 * (ta // 2))
                    io = v.tensor_tensor(outb[ta % 2][:], t4[:], r2C[ta % 2][:],
                                         op=ALU.add)
                    inc("s_outv", io)

            @blk.scalar
            def _(a):
                a = _Proxy(chains["a"], a)
                a.wait_ge(sem("s_pre2"), 1)
                for ta in range(NT2):
                    a.wait_ge(sem("s_f4"), ta + 1)
                    iff = a.activation(ffn[ta % 2][:], ps_f[ta % 2][:], AF.Identity,
                                       bias=zerov[:], scale=g4s[:, ta:ta + 1])
                    inc("s_ffn", iff)
                    a.wait_ge(sem("s_st4"), ta + 1)
                    a.activation(lnv4[:], vee4[:], AF.Ln, bias=zerov[:], scale=1.0)
                    ie = a.activation(rstd4c[:, ta % 2:ta % 2 + 1], lnv4[:], AF.Exp,
                                      bias=zerov[:], scale=-0.5)
                    inc("s_rs4", ie)

            @blk.tensor
            def _(t):
                t.wait_ge(sem("s_w2T"), 16)
                for ta in range(NT2):
                    t.wait_ge(sem(f"s_qhT{ta % 2}"), 16 * (ta // 2 + 1))
                    if ta >= 2:
                        t.wait_ge(sem("s_ffn"), ta - 1)
                    last = None
                    for half in range(FH):
                        pslice = ps_f[ta % 2][:, 512 * half:512 * (half + 1)]
                        for j in range(HT):
                            last = t.matmul(pslice, qhTv[ta % 2][:, j, :],
                                            w2v[:, j, 512 * half:512 * (half + 1)],
                                            start=(j == 0), stop=(j == HT - 1))
                    inc("s_f4", last)

    return nc


_NC_CACHE = {}


def kernel(L0_attn_x, L0_attn_lambda_log, L0_attn_gamma, L0_attn_W_in,
           L0_attn_b_in, L0_attn_W_out, L0_attn_b_out, L0_ffn_w1, L0_ffn_w2):
    x = np.ascontiguousarray(np.asarray(L0_attn_x, dtype=np.float32))
    Bf, Lf, Df = x.shape
    nb = Bf // NCORES
    key = (nb, Lf, Df)
    if key not in _NC_CACHE:
        _NC_CACHE[key] = build_program(NB=nb, L=Lf, D=Df,
                                       H=np.asarray(L0_ffn_w1).shape[0])
    nc = _NC_CACHE[key]

    def row(a):
        return np.ascontiguousarray(np.asarray(a, dtype=np.float32)).reshape(1, -1)

    shared = {
        "w_in": np.ascontiguousarray(np.asarray(L0_attn_W_in, dtype=np.float32)),
        "w_out": np.ascontiguousarray(np.asarray(L0_attn_W_out, dtype=np.float32)),
        "b_in": row(L0_attn_b_in),
        "b_out": row(L0_attn_b_out),
        "gamma": row(L0_attn_gamma),
        "lamlog": row(L0_attn_lambda_log),
        "w1": np.ascontiguousarray(np.asarray(L0_ffn_w1, dtype=np.float32)),
        "w2": np.ascontiguousarray(np.asarray(L0_ffn_w2, dtype=np.float32)),
        "ident": np.eye(128, dtype=np.float32),
    }
    in_maps = []
    for c in range(NCORES):
        m = dict(shared)
        m["x"] = np.ascontiguousarray(
            x[c * nb:(c + 1) * nb].reshape(nb * Lf, Df))
        in_maps.append(m)
    res = run_bass_kernel_spmd(nc, in_maps, core_ids=list(range(NCORES)))
    out = np.empty((Bf, Lf, Df), dtype=np.float32)
    for c in range(NCORES):
        out[c * nb:(c + 1) * nb] = np.asarray(
            res.results[c]["y"], dtype=np.float32).reshape(nb, Lf, Df)
    return out
